# version 03
# baseline (speedup 1.0000x reference)
import math
from contextlib import ExitStack

import numpy as np

import concourse.bass as bass
import concourse.mybir as mybir
from concourse.bass_utils import run_bass_kernel_spmd

F32 = mybir.dt.float32
BF16 = mybir.dt.bfloat16
AF = mybir.ActivationFunctionType
ALU = mybir.AluOpType
AX = mybir.AxisListType

T = 2048
D = 2048
H = 16
HD = 128
QC = 512
M = 16
NB = T // M
NIH, CI = 4, 64
G, DG = 4, 512
WIN = 256
EPS_RMS = 1e-6
EPS_LN = 1e-5
EPS_FFN = 1e-5
NEG = -1e30
SENT = -2e30
SCALE = 1.0 / math.sqrt(HD)
IF = 8192
NT = T // 128
NCH = D // 128

INPUT_SHAPES = {
    "x": [T, D],
    "L0_attn_w_qc": [D, QC], "L0_attn_w_qup": [QC, D],
    "L0_attn_kvc_w": [D, HD], "L0_attn_kvc_wz": [D, HD],
    "L0_attn_kvc_bias": [M, HD], "L0_attn_k_proj_w": [D, HD],
    "L0_attn_v_proj_w": [D, HD], "L0_attn_idx_c_w": [D, CI],
    "L0_attn_idx_c_wz": [D, CI], "L0_attn_idx_c_bias": [M, CI],
    "L0_attn_w_dq": [D, CI], "L0_attn_w_iuq": [CI, NIH * CI],
    "L0_attn_w_w": [D, NIH], "L0_attn_q_norm_w": [HD],
    "L0_attn_k_norm_w": [HD], "L0_attn_group_w": [G, 512, DG],
    "L0_attn_final_w": [G * DG, D],
    "L0_ffn_w_gate": [D, IF], "L0_ffn_w_up": [D, IF],
    "L0_ffn_w_down": [IF, D],
}
WEIGHT_NAMES = [n for n in INPUT_SHAPES if n != "x"]

ARENA = 32768
TRIM = 99
PTRIM = 99
PTILES = NT


def ts(i, n=128):
    return slice(i * n, (i + 1) * n)


class Sem:
    def __init__(self, handle):
        self.h = handle
        self.val = 0

    def inc(self, n=1):
        self.val += n
        return self.h, n


class K:
    def __init__(self, nc, ctx):
        self.nc = nc
        self.ctx = ctx
        self.sb = {}
        self.dr = {}
        self.taps = {}

    def sbuf(self, name, shape, dtype=F32):
        h = self.ctx.enter_context(self.nc.sbuf_tensor(name, shape, dtype))
        self.sb[name] = h
        return h

    def dram(self, name, shape, dtype=F32, kind="Internal"):
        h = self.nc.dram_tensor(name, shape, dtype, kind=kind)
        self.dr[name] = h
        return h

    def sem(self, name):
        return Sem(self.ctx.enter_context(self.nc.semaphore(name)))

    def tap(self, block, name, src_ap, wait=None):
        if name not in self.tapset:
            return
        shape = list(src_ap.shape)
        dh = self.dram("tap_" + name, shape, src_ap.dtype, kind="ExternalOutput")
        self.taps[name] = dh
        dtap = self.dtap

        @block.sync
        def _(sy):
            if wait is not None:
                sy.wait_ge(wait.h, wait.val)
            sy.dma_start(out=dh.ap(), in_=src_ap).then_inc(*dtap.inc(16))


def rope7(v, out, src, cc, ss, tmp, nV):
    v.tensor_mul(tmp[:, 0:32], src[:, 0:32], cc)
    v.tensor_mul(tmp[:, 32:64], src[:, 32:64], ss)
    v.tensor_mul(tmp[:, 64:96], src[:, 0:32], ss)
    v.tensor_mul(tmp[:, 96:128], src[:, 32:64], cc).then_inc(*nV.inc())
    v.wait_ge(nV.h, nV.val)
    v.tensor_sub(out[:, 0:32], tmp[:, 0:32], tmp[:, 32:64])
    v.tensor_add(out[:, 32:64], tmp[:, 64:96], tmp[:, 96:128])
    v.tensor_copy(out[:, 64:128], src[:, 64:128])


def peel_round(v, z, w, b, width):
    v.tensor_scalar(z, z, 128.5, None, ALU.add)
    src = z
    for kk in range(7, -1, -1):
        v.tensor_scalar(b, src, float(2 ** kk), None, ALU.is_ge)
        v.scalar_tensor_tensor(w, b, float(-(2 ** kk)), src, ALU.mult, ALU.add)
        src = w



def emit_constants(k, block):
    ident = k.sb["ident"]; h128 = k.sb["h128"]
    cos_tm = k.sb["cos_tm"]; sin_tm = k.sb["sin_tm"]
    wmask = k.sb["wmask"]
    wqs_b = k.sb["wqs_b"]; wk_b = k.sb["wk_b"]
    e0row = k.sb["e0row"]; bcsrc = k.sb["bcsrc"]
    scr = k.sb["scr"]; scr2 = k.sb["scr2"]
    scr3 = k.sb["scr3"]; scr4 = k.sb["scr4"]

    sG = k.sem("c_g"); sV = k.sem("c_v"); sS = k.sem("c_s")
    sT = k.sem("c_t"); sD = k.sem("c_d")

    @block.sync
    def _(sy):
        sy.dma_start(out=h128[:], in_=k.dr["h128_in"].ap()).then_inc(*sD.inc(16))
        sy.wait_ge(sG.h, 1)
        sy.dma_start(out=bcsrc[0:1, 0:128],
                     in_=k.dr["L0_attn_q_norm_w"].ap()[None, :]).then_inc(*sD.inc(16))
        sy.dma_start(out=bcsrc[0:1, 128:256],
                     in_=k.dr["L0_attn_k_norm_w"].ap()[None, :]).then_inc(*sD.inc(16))

    if TRIM < 2:
        k.c_sV = sD
        return

    @block.gpsimd
    def _(g):
        def step(mk):
            g.wait_ge(sG.h, sG.val)
            mk().then_inc(*sG.inc())
        step(lambda: g.memset(bcsrc[:], 0.0))
        step(lambda: g.memset(e0row[:], 1.0))
        step(lambda: g.affine_select(out=e0row[:], in_=e0row[:],
                                     pattern=[[0, 128]],
                                     compare_op=ALU.is_equal, fill=0.0,
                                     base=0, channel_multiplier=1))
        step(lambda: g.memset(ident[:], 1.0))
        step(lambda: g.affine_select(out=ident[:], in_=ident[:], pattern=[[-1, 128]],
                             compare_op=ALU.is_equal, fill=0.0, base=0,
                             channel_multiplier=1))
        for par in range(2):
            for w0 in range(2):
                dst = wmask[:, (par * 2 + w0) * 512:(par * 2 + w0 + 1) * 512]
                step(lambda: g.memset(scr2[:, 0:512], 1.0))
                step(lambda: g.affine_select(out=scr2[:, 0:512], in_=scr2[:, 0:512],
                                     pattern=[[1, 512]], compare_op=ALU.is_ge,
                                     fill=0.0, base=-128 * par - 1,
                                     channel_multiplier=-1))
                step(lambda: g.affine_select(out=dst, in_=scr2[:, 0:512],
                                     pattern=[[-1, 512]], compare_op=ALU.is_ge,
                                     fill=0.0, base=128 * par + 256,
                                     channel_multiplier=1))
                if w0:
                    step(lambda: g.affine_select(out=dst, in_=dst,
                                         pattern=[[1, 512]],
                                         compare_op=ALU.is_ge,
                                         fill=0.0, base=-256,
                                         channel_multiplier=0))
        for i, cv in enumerate(k.constvals):
            step(lambda: g.memset(k.sb["cpool"][:, i:i + 1], float(cv)))


    @block.vector
    def _(v):
        v.wait_ge(sG.h, sG.val)
        v.engine_nop().then_inc(*sV.inc())

    if TRIM < 3:
        k.c_sV = sV
        return

    LNT32 = math.log(10000.0) / 32.0
    C_HI = float(np.float32(LNT32)); C_LO = float(LNT32 - C_HI)
    TWO_PI = 2.0 * math.pi
    I2P_HI = float(np.float32(1.0 / TWO_PI))
    I2P_LO = float(1.0 / TWO_PI - I2P_HI)
    CW1 = float(np.float32(TWO_PI)); CW2 = float(np.float32(TWO_PI - CW1))
    CW3 = float(TWO_PI - CW1 - CW2)

    @block.gpsimd
    def _(g):
        g.wait_ge(sG.h, sG.val)
        g.iota(scr3[:, 0:512], pattern=[[0, 16], [1, 32]], base=0,
               channel_multiplier=0,
               allow_small_or_imprecise_dtypes=True).then_inc(*sG.inc())
        g.wait_ge(sG.h, sG.val)
        g.iota(scr3[:, 512:1024], pattern=[[128, 16], [0, 32]], base=0,
               channel_multiplier=1,
               allow_small_or_imprecise_dtypes=True).then_inc(*sG.inc())

    @block.vector
    def _(v):
        v.wait_ge(sG.h, sG.val)
        v.tensor_scalar(scr3[:, 1024:1536], scr3[:, 0:512], -C_HI, None, ALU.mult)
        v.scalar_tensor_tensor(scr3[:, 1024:1536], scr3[:, 0:512], -C_LO,
                               scr3[:, 1024:1536], ALU.mult, ALU.add)
        v.engine_nop().then_inc(*sV.inc())

    @block.scalar
    def _(s):
        s.wait_ge(sV.h, sV.val)
        s.activation(scr3[:, 0:512], scr3[:, 1024:1536], AF.Exp).then_inc(*sS.inc())

    @block.vector
    def _(v):
        v.wait_ge(sS.h, sS.val)
        a = scr3[:, 1024:1536]
        u = scr4[:, 0:512]
        w = scr4[:, 512:1024]
        b = scr4[:, 1024:1536]
        kk = scr3[:, 1536:2048]
        v.tensor_mul(a, scr3[:, 0:512], scr3[:, 512:1024])
        v.tensor_scalar(u, a, I2P_HI, None, ALU.mult)
        v.scalar_tensor_tensor(u, a, I2P_LO, u, ALU.mult, ALU.add)
        v.tensor_scalar(w, u, 512.5, None, ALU.add)
        for p2 in range(9, -1, -1):
            v.tensor_scalar(b, w, float(2 ** p2), None, ALU.is_ge)
            v.scalar_tensor_tensor(w, b, float(-(2 ** p2)), w, ALU.mult, ALU.add)
        v.tensor_scalar(b, u, 512.5, None, ALU.add)
        v.tensor_sub(kk, b, w)
        v.tensor_scalar(kk, kk, 512.0, None, ALU.subtract)
        v.scalar_tensor_tensor(w, kk, -CW1, a, ALU.mult, ALU.add)
        v.scalar_tensor_tensor(w, kk, -CW2, w, ALU.mult, ALU.add)
        v.scalar_tensor_tensor(w, kk, -CW3, w, ALU.mult, ALU.add)
        v.tensor_scalar(b, w, float(math.pi), None, ALU.is_gt)
        v.scalar_tensor_tensor(w, b, -TWO_PI, w, ALU.mult, ALU.add)
        v.tensor_scalar(b, w, float(-math.pi), None, ALU.is_lt)
        v.scalar_tensor_tensor(w, b, TWO_PI, w, ALU.mult, ALU.add)
        v.engine_nop().then_inc(*sV.inc())

    @block.scalar
    def _(s):
        s.wait_ge(sV.h, sV.val)
        s.activation(sin_tm[:], scr4[:, 512:1024], AF.Sin).then_inc(*sS.inc())

    @block.vector
    def _(v):
        v.wait_ge(sS.h, sS.val)
        w = scr4[:, 512:1024]; b = scr4[:, 1024:1536]
        v.tensor_scalar(w, w, float(math.pi / 2), None, ALU.add)
        v.tensor_scalar(b, w, float(math.pi), None, ALU.is_gt)
        v.scalar_tensor_tensor(w, b, -TWO_PI, w, ALU.mult, ALU.add)
        v.engine_nop().then_inc(*sV.inc())

    @block.scalar
    def _(s):
        s.wait_ge(sV.h, sV.val)
        s.activation(cos_tm[:], scr4[:, 512:1024], AF.Sin).then_inc(*sS.inc())

    ps = k.psN[0]

    @block.tensor
    def _(tt):
        tt.wait_ge(sD.h, sD.val)
        tt.wait_ge(sV.h, sV.val)
        tt.matmul(ps[:, 0:128], e0row[:], bcsrc[:, 0:128],
                  start=True, stop=True)
        tt.matmul(ps[:, 128:256], e0row[:], bcsrc[:, 128:256],
                  start=True, stop=True).then_inc(*sT.inc())

    @block.vector
    def _(v):
        v.wait_ge(sT.h, sT.val)
        v.tensor_scalar(wqs_b[:], ps[:, 0:128], SCALE, None, ALU.mult)
        v.tensor_copy(wk_b[:], ps[:, 128:256])
        v.engine_nop().then_inc(*sV.inc())

    k.c_sV = sV



CAT = [("L0_attn_k_proj_w", 0, 128), ("L0_attn_v_proj_w", 128, 128),
       ("L0_attn_kvc_w", 256, 128), ("L0_attn_kvc_wz", 384, 128),
       ("L0_attn_idx_c_w", 512, 64), ("L0_attn_idx_c_wz", 576, 64),
       ("L0_attn_w_dq", 640, 64), ("L0_attn_w_w", 704, 4)]
CAT_W = 708

A_WCAT = 0
A_XB = 11328
A_SQ = 15424
A_XLN = 17472
A_XFM = 19520
A_KVCC = 21568
A_KVCZ = 23616
A_IDXC = 25664
A_IDXZ = 27712
A_QIC = 29760


def emit_ln_tile(k, block, xb, t, stats, sems, sqb):
    nmean, rstd, nmr = stats
    pV, pS = sems[0], sems[1]
    tcol = k.sb["colscr"]

    @block.vector
    def _(v):
        v.reduce_sum(tcol[:, 0:1], xb, axis=AX.X)
        v.engine_nop().then_inc(*pV.inc())

    @block.scalar
    def _(s):
        s.wait_ge(pV.h, pV.val)
        s.activation(sqb, xb, AF.Square).then_inc(*pS.inc())

    @block.vector
    def _(v):
        nv = sems[2]
        v.wait_ge(pS.h, pS.val)
        v.reduce_sum(tcol[:, 1:2], sqb, axis=AX.X).then_inc(*nv.inc())
        v.wait_ge(nv.h, nv.val)
        v.tensor_scalar(nmean[:, t:t + 1], tcol[:, 0:1], -1.0 / D, None,
                        ALU.mult).then_inc(*nv.inc())
        v.wait_ge(nv.h, nv.val)
        v.tensor_mul(tcol[:, 2:3], nmean[:, t:t + 1],
                     nmean[:, t:t + 1]).then_inc(*nv.inc())
        v.wait_ge(nv.h, nv.val)
        v.scalar_tensor_tensor(tcol[:, 3:4], tcol[:, 1:2], 1.0 / D,
                               tcol[:, 2:3], ALU.mult,
                               ALU.subtract).then_inc(*nv.inc())
        v.wait_ge(nv.h, nv.val)
        v.tensor_scalar(tcol[:, 3:4], tcol[:, 3:4], EPS_LN, None,
                        ALU.add).then_inc(*nv.inc())
        v.wait_ge(nv.h, nv.val)
        v.reciprocal(tcol[:, 3:4], tcol[:, 3:4])
        v.engine_nop().then_inc(*pV.inc())

    @block.scalar
    def _(s):
        s.wait_ge(pV.h, pV.val)
        s.activation(rstd[:, t:t + 1], tcol[:, 3:4], AF.Sqrt).then_inc(*pS.inc())

    @block.vector
    def _(v):
        v.wait_ge(pS.h, pS.val)
        v.tensor_mul(nmr[:, t:t + 1], nmean[:, t:t + 1], rstd[:, t:t + 1])
        v.engine_nop().then_inc(*pV.inc())


def emit_transpose16(k, block, src, dstcols, pT, pV, evac_cb=None):
    PS = k.psN
    for grp in range(4):
        pb = PS[6 + (grp % 2)]

        @block.tensor
        def _(tt, grp=grp, pb=pb):
            tt.wait_ge(pV.h, pV.val)
            for j in range(4):
                c = grp * 4 + j
                tt.transpose(pb[:, ts(j)], src[:, ts(c)], k.sb["ident"][:]).then_inc(*pT.inc())

        @block.vector
        def _(v, grp=grp, pb=pb):
            v.wait_ge(pT.h, pT.val)
            for j in range(4):
                v.tensor_copy(dstcols(grp * 4 + j), pb[:, ts(j)])
            v.engine_nop().then_inc(*pV.inc())


def emit_prepass(k, block):
    ar = k.arena
    ident = k.sb["ident"]; wk_b = k.sb["wk_b"]
    cos_tm = k.sb["cos_tm"]; sin_tm = k.sb["sin_tm"]
    k_fm = k.sb["k_fm"]; v_tm = k.sb["v_tm"]
    nmean = k.sb["nmean"]; rstd1 = k.sb["rstd1"]; nmr1 = k.sb["nmr1"]
    wih_tm = k.sb["wih_tm"]; qi_fm = k.sb["qi_fm"]
    tcol = k.sb["colscr"]; kscr = k.sb["scr"]; stg = k.sb["scr3"]
    PS = k.psN
    x = k.dr["x"]

    wcat = ar[:, A_WCAT:A_WCAT + NCH * CAT_W]
    sqb = ar[:, A_SQ:A_SQ + D]
    xlnb = ar[:, A_XLN:A_XLN + D]
    xfm = ar[:, A_XFM:A_XFM + D]
    kvcc = ar[:, A_KVCC:A_KVCC + T]
    kvcz = ar[:, A_KVCZ:A_KVCZ + T]
    idxc = ar[:, A_IDXC:A_IDXC + T]
    idxz = ar[:, A_IDXZ:A_IDXZ + T]
    qicb = ar[:, A_QIC:A_QIC + T]

    dX = k.sem("p_dx"); dW = k.sem("p_dw")
    pV = k.sem("p_v"); pS = k.sem("p_s"); pT = k.sem("p_t")
    nV = k.sem("p_nv")
    k.p_pV = pV

    @block.sync
    def _(sy):
        for name, off, w in CAT:
            for c in range(NCH):
                sy.dma_start(out=wcat[:, c * CAT_W + off:c * CAT_W + off + w],
                             in_=k.dr[name].ap()[ts(c), :]).then_inc(*dW.inc(16))
        sy.dma_start(out=ar[:, A_XB:A_XB + D],
                     in_=x.ap()[0:128, :]).then_inc(*dX.inc(16))

    @block.gpsimd
    def _(g):
        g.memset(k_fm[:, 0:256], 0.0).then_inc(*k.gsem.inc())
        g.memset(v_tm[:, 0:256], 0.0).then_inc(*k.gsem.inc())

    for t in range(PTILES):
        xb = ar[:, A_XB + (t % 2) * D:A_XB + (t % 2 + 1) * D]

        @block.sync
        def _(sy, t=t):
            if t + 1 < NT:
                sy.wait_ge(pS.h, pS.val)
                dst = ar[:, A_XB + ((t + 1) % 2) * D:A_XB + ((t + 1) % 2 + 1) * D]
                sy.dma_start(out=dst, in_=x.ap()[ts(t + 1), :]).then_inc(*dX.inc(16))

        @block.vector
        def _(v, t=t):
            v.wait_ge(dX.h, 16 * (t + 1))
            if t == 0:
                v.wait_ge(k.c_sV.h, k.c_sV.val)
            v.engine_nop().then_inc(*pV.inc())

        emit_ln_tile(k, block, xb, t, (nmean, rstd1, nmr1), (pV, pS, nV), sqb)

        @block.scalar
        def _(s, t=t, xb=xb):
            s.wait_ge(pV.h, pV.val)
            s.activation(xlnb, xb, AF.Identity, bias=nmr1[:, t:t + 1],
                         scale=rstd1[:, t:t + 1]).then_inc(*pS.inc())

        @block.vector
        def _(v):
            v.wait_ge(pS.h, pS.val)
            v.engine_nop().then_inc(*pV.inc())

        if PTRIM < 2:
            continue
        emit_transpose16(k, block, xlnb, lambda c: xfm[:, ts(c)], pT, pV)
        if PTRIM < 3:
            continue

        @block.tensor
        def _(tt, t=t):
            tt.wait_ge(pV.h, pV.val)
            tt.wait_ge(dW.h, dW.val)
            if t > 0:
                tt.wait_ge(pS.h, pS.val)
            for c in range(NCH):
                tt.matmul(PS[0][:, 0:512], xfm[:, ts(c)],
                          wcat[:, c * CAT_W:c * CAT_W + 512],
                          start=(c == 0), stop=(c == NCH - 1))
                tt.matmul(PS[1][:, 0:CAT_W - 512], xfm[:, ts(c)],
                          wcat[:, c * CAT_W + 512:(c + 1) * CAT_W],
                          start=(c == 0), stop=(c == NCH - 1)).then_inc(*pT.inc())

        if PTRIM < 4:
            continue
        @block.vector
        def _(v, t=t):
            v.wait_ge(pT.h, pT.val)
            v.tensor_copy(stg[:, 0:128], PS[0][:, 0:128])
            v.tensor_copy(v_tm[:, (t + 2) * 128:(t + 3) * 128], PS[0][:, 128:256])
            v.tensor_copy(stg[:, 128:384], PS[0][:, 256:512])
            v.engine_nop().then_inc(*pV.inc())

        @block.scalar
        def _(s, t=t):
            s.wait_ge(pT.h, pT.val)
            s.activation(stg[:, 384:580], PS[1][:, 0:196], AF.Identity)
            s.activation(kscr[:, 0:128], PS[0][:, 0:128], AF.Square).then_inc(*pS.inc())

        @block.vector
        def _(v, t=t):
            v.wait_ge(pS.h, pS.val)
            v.reduce_sum(tcol[:, 4:5], kscr[:, 0:128], axis=AX.X).then_inc(*nV.inc())
            v.wait_ge(nV.h, nV.val)
            v.tensor_scalar(tcol[:, 4:5], tcol[:, 4:5], 1.0 / HD, EPS_RMS,
                            ALU.mult, ALU.add).then_inc(*nV.inc())
            v.wait_ge(nV.h, nV.val)
            v.reciprocal(tcol[:, 4:5], tcol[:, 4:5])
            v.tensor_copy(wih_tm[:, t * 4:(t + 1) * 4], stg[:, 576:580])
            v.engine_nop().then_inc(*pV.inc())

        @block.scalar
        def _(s, t=t):
            s.wait_ge(pV.h, pV.val)
            s.activation(tcol[:, 5:6], tcol[:, 4:5], AF.Sqrt).then_inc(*pS.inc())

        @block.vector
        def _(v, t=t):
            v.wait_ge(pS.h, pS.val)
            kn = kscr[:, 128:256]; ko = kscr[:, 256:384]
            v.tensor_scalar(kn, stg[:, 0:128], tcol[:, 5:6], None, ALU.mult)
            v.tensor_mul(kn, kn, wk_b[:])
            rope7(v, ko, kn, cos_tm[:, t * 32:(t + 1) * 32],
                  sin_tm[:, t * 32:(t + 1) * 32], kscr[:, 384:512], nV)
            v.engine_nop().then_inc(*pV.inc())

        if PTRIM < 5:
            continue
        @block.tensor
        def _(tt, t=t):
            tt.wait_ge(pV.h, pV.val)
            tt.transpose(PS[6][:, 0:128], kscr[:, 256:384], ident[:])
            tt.transpose(PS[6][:, 128:256], stg[:, 128:256], ident[:])
            tt.transpose(PS[6][:, 256:384], stg[:, 256:384], ident[:])
            tt.transpose(PS[7][0:64, 0:128], stg[:, 384:448], ident[:])
            tt.transpose(PS[7][0:64, 128:256], stg[:, 448:512], ident[:])
            tt.transpose(PS[7][0:64, 256:384], stg[:, 512:576],
                         ident[:]).then_inc(*pT.inc())

        @block.vector
        def _(v, t=t):
            v.wait_ge(pT.h, pT.val)
            v.tensor_copy(k_fm[:, (t + 2) * 128:(t + 3) * 128], PS[6][:, 0:128])
            v.tensor_copy(kvcc[:, ts(t)], PS[6][:, 128:256])
            v.tensor_copy(kvcz[:, ts(t)], PS[6][:, 256:384])
            v.tensor_copy(idxc[0:64, ts(t)], PS[7][0:64, 0:128])
            v.tensor_copy(idxz[0:64, ts(t)], PS[7][0:64, 128:256])
            v.tensor_copy(qicb[0:64, ts(t)], PS[7][0:64, 256:384])
            v.engine_nop().then_inc(*pV.inc())


    if PTRIM < 6:
        return
    scr2 = k.sb["scr2"]

    @block.sync
    def _(sy):
        sy.wait_ge(k.c_sV.h, k.c_sV.val)
        sy.dma_start(out=scr2[0:64, 0:256],
                     in_=k.dr["L0_attn_w_iuq"].ap()).then_inc(*dW.inc(16))

    for hh in range(2):
        for g4 in range(4):
            pb = PS[2 + (g4 % 2)]

            @block.tensor
            def _(tt, hh=hh, g4=g4, pb=pb):
                tt.wait_ge(dW.h, dW.val)
                tt.wait_ge(pV.h, pV.val)
                tt.matmul(pb[:, 0:512], scr2[0:64, hh * 128:(hh + 1) * 128],
                          qicb[0:64, g4 * 512:(g4 + 1) * 512],
                          start=True, stop=True).then_inc(*pT.inc())

            @block.vector
            def _(v, hh=hh, g4=g4, pb=pb):
                v.wait_ge(pT.h, pT.val)
                v.tensor_copy(qi_fm[:, hh * T + g4 * 512:hh * T + (g4 + 1) * 512],
                              pb[:, 0:512])
                v.engine_nop().then_inc(*pV.inc())



def emit_compress(k, block):
    ar = k.arena
    ident = k.sb["ident"]
    scr3 = k.sb["scr3"]; scr4 = k.sb["scr4"]; scr = k.sb["scr"]
    scr2 = k.sb["scr2"]
    ckv_raw = k.sb["ckv_raw"]; kcompT = k.sb["kcompT"]
    PS = k.psN
    pV = k.sem("q_v"); pS = k.sem("q_s"); pT = k.sem("q_t")
    dB = k.sem("q_d")

    kvcc = ar[:, A_KVCC:A_KVCC + T]
    kvcz = ar[:, A_KVCZ:A_KVCZ + T]
    idxc = ar[0:64, A_IDXC:A_IDXC + T]
    idxz = ar[0:64, A_IDXZ:A_IDXZ + T]

    @block.sync
    def _(sy):
        sy.wait_ge(k.c_sV.h, k.c_sV.val)
        sy.dma_start(out=scr2[0:16, 288:416],
                     in_=k.dr["L0_attn_kvc_bias"].ap()).then_inc(*dB.inc(16))
        sy.dma_start(out=scr2[0:16, 416:480],
                     in_=k.dr["L0_attn_idx_c_bias"].ap()).then_inc(*dB.inc(16))

    @block.tensor
    def _(tt):
        tt.wait_ge(dB.h, dB.val)
        tt.wait_ge(k.p_pV.h, k.p_pV.val)
        tt.transpose(PS[6][:, 0:16], scr2[0:16, 288:416], ident[0:16, 0:16])
        tt.transpose(PS[6][0:64, 16:32], scr2[0:16, 416:480],
                     ident[0:16, 0:16]).then_inc(*pT.inc())

    @block.vector
    def _(v):
        v.wait_ge(pT.h, pT.val)
        v.tensor_copy(scr2[:, 480:496], PS[6][:, 0:16])
        v.tensor_copy(scr2[0:64, 496:512], PS[6][0:64, 16:32])
        v.engine_nop().then_inc(*pV.inc())

    def softmax_compress(zap, cap, bias_ap, out_ap, rows, tag):
        z3 = zap.rearrange("p (b m) -> p b m", m=M)
        c3 = cap.rearrange("p (b m) -> p b m", m=M)
        zc3 = scr3[0:rows, 0:T].rearrange("p (b m) -> p b m", m=M)
        e3 = scr4[0:rows, 0:T].rearrange("p (b m) -> p b m", m=M)
        ec3 = scr3[0:rows, 0:T].rearrange("p (b m) -> p b m", m=M)
        zmax = scr[0:rows, 288:416]
        esum = scr[0:rows, 416:544]
        secs = scr[0:rows, 544:672]

        @block.vector
        def _(v):
            v.wait_ge(pV.h, pV.val)
            v.tensor_add(z3, z3, bias_ap.unsqueeze(1).to_broadcast(
                [rows, NB, M]))
            v.reduce_max(zmax, z3, axis=AX.X)
            v.tensor_sub(zc3, z3, zmax.unsqueeze(2).to_broadcast([rows, NB, M]))
            v.engine_nop().then_inc(*pV.inc())

        @block.scalar
        def _(s):
            s.wait_ge(pV.h, pV.val)
            s.activation(scr4[0:rows, 0:T], scr3[0:rows, 0:T], AF.Exp).then_inc(*pS.inc())

        @block.vector
        def _(v):
            v.wait_ge(pS.h, pS.val)
            v.reduce_sum(esum, e3, axis=AX.X)
            v.tensor_mul(ec3, e3, c3)
            v.reduce_sum(secs, ec3, axis=AX.X)
            v.reciprocal(esum, esum)
            v.tensor_mul(out_ap, secs, esum)
            v.engine_nop().then_inc(*pV.inc())

    softmax_compress(kvcz, kvcc, scr2[:, 480:496], ckv_raw[:], 128, "kvc")
    softmax_compress(idxz, idxc, scr2[0:64, 496:512], kcompT[0:64, :], 64, "idx")

    kcompT2 = k.sb["kcompT2"]
    dDup = k.sem("q_dup")

    @block.gpsimd
    def _(g):
        g.wait_ge(pV.h, pV.val)
        g.memset(kcompT2[:], 0.0).then_inc(*k.gsem.inc())

    @block.sync
    def _(sy):
        sy.wait_ge(pV.h, pV.val)
        sy.dma_start(out=kcompT[64:128, :], in_=kcompT[0:64, :]).then_inc(*dDup.inc(16))

    @block.vector
    def _(v):
        v.wait_ge(dDup.h, dDup.val)
        v.wait_ge(k.gsem.h, k.gsem.val)
        v.tensor_copy(kcompT2[0:64, 0:128], kcompT[0:64, :])
        v.tensor_copy(kcompT2[64:128, 128:256], kcompT[64:128, :])
        v.engine_nop().then_inc(*pV.inc())

    k.q_pV = pV






A_QST = 0
A_WQC = 16384
A_WQUP = 24576
A_GW = 16384
A_BSC = 24576


def emit_attention(k, block):
    ident = k.sb["ident"]
    wk_b = k.sb["wk_b"]
    ckv_raw = k.sb["ckv_raw"]; ckv_V = k.sb["ckv_V"]; ckv_K = k.sb["ckv_K"]
    tcol = k.sb["colscr"]
    scr = k.sb["scr"]
    PS = k.psN

    aV = k.sem("a_v"); aS = k.sem("a_s"); aT = k.sem("a_t")
    nA = k.sem("a_nv"); dA = k.sem("a_d"); aG = k.sem("a_g")

    @block.tensor
    def _(tt):
        tt.wait_ge(k.q_pV.h, k.q_pV.val)
        tt.transpose(PS[6][:, 0:128], ckv_raw[:], ident[:]).then_inc(*aT.inc())

    @block.vector
    def _(v):
        v.wait_ge(aT.h, aT.val)
        v.tensor_copy(scr[:, 0:128], PS[6][:, 0:128])
        v.engine_nop().then_inc(*aV.inc())

    @block.scalar
    def _(s):
        s.wait_ge(aV.h, aV.val)
        s.activation(scr[:, 128:256], scr[:, 0:128], AF.Square).then_inc(*aS.inc())

    @block.vector
    def _(v):
        v.wait_ge(aS.h, aS.val)
        v.reduce_sum(tcol[:, 8:9], scr[:, 128:256], axis=AX.X).then_inc(*nA.inc())
        v.wait_ge(nA.h, nA.val)
        v.tensor_scalar(tcol[:, 8:9], tcol[:, 8:9], 1.0 / HD, EPS_RMS,
                        ALU.mult, ALU.add).then_inc(*nA.inc())
        v.wait_ge(nA.h, nA.val)
        v.reciprocal(tcol[:, 8:9], tcol[:, 8:9])
        v.engine_nop().then_inc(*aV.inc())

    @block.scalar
    def _(s):
        s.wait_ge(aV.h, aV.val)
        s.activation(tcol[:, 9:10], tcol[:, 8:9], AF.Sqrt).then_inc(*aS.inc())

    @block.vector
    def _(v):
        v.wait_ge(aS.h, aS.val)
        v.tensor_scalar(ckv_V[:], scr[:, 0:128], tcol[:, 9:10], None, ALU.mult)
        v.tensor_mul(ckv_V[:], ckv_V[:], wk_b[:])
        v.engine_nop().then_inc(*aV.inc())

    @block.tensor
    def _(tt):
        tt.wait_ge(aV.h, aV.val)
        tt.transpose(PS[6][:, 128:256], ckv_V[:], ident[:]).then_inc(*aT.inc())

    @block.scalar
    def _(s):
        s.wait_ge(aT.h, aT.val)
        s.activation(ckv_K[:], PS[6][:, 128:256],
                     AF.Identity).then_inc(*aS.inc())

    k.b_done = [k.sem("b_done0"), k.sem("b_done1")]
    mode = getattr(k, "attn_mode", "full")
    if mode == "A0":
        emit_attn_A(k, block, 0, (aV, aS, aT, nA, dA, aG))
        k.fin_aV = aV; k.fin_dA = dA
        return
    if mode == "B0":
        emit_attn_A(k, block, 0, (aV, aS, aT, nA, dA, aG))
        emit_attn_B(k, block, 0, (aV, aS, aT, nA, dA, aG))
        k.fin_aV = aV; k.fin_dA = dA
        return
    for half in range(2):
        emit_attn_A(k, block, half, (aV, aS, aT, nA, dA, aG))
        emit_attn_B(k, block, half, (aV, aS, aT, nA, dA, aG))
    emit_attn_final(k, block, (aV, aS, aT, nA, dA, aG))


def emit_attn_A(k, block, half, sems):
    aV, aS, aT, nA, dA, aG = sems
    ar = k.arena
    ident = k.sb["ident"]
    cos_tm = k.sb["cos_tm"]; sin_tm = k.sb["sin_tm"]
    wqs_b = k.sb["wqs_b"]
    rstd1 = k.sb["rstd1"]; nmr1 = k.sb["nmr1"]
    tcol = k.sb["colscr"]
    scr2 = k.sb["scr2"]; scr3 = k.sb["scr3"]; scr4 = k.sb["scr4"]
    PS = k.psN
    x = k.dr["x"]

    wqc = ar[:, A_WQC:A_WQC + 8192]
    wqup = ar[:, A_WQUP:A_WQUP + 8192]
    xlnb = scr3[:, 0:D]
    xfm = scr4[:, 0:D]

    @block.sync
    def _(sy, half=half):
        if half == 0:
            sy.wait_ge(k.q_pV.h, k.q_pV.val)
        else:
            sy.wait_ge(k.b_done[0].h, k.b_done[0].val)
        for c in range(NCH):
            sy.dma_start(out=wqc[:, c * 512:(c + 1) * 512],
                         in_=k.dr["L0_attn_w_qc"].ap()[ts(c), :]).then_inc(*dA.inc(16))
        for q in range(4):
            sy.dma_start(out=wqup[:, q * 2048:(q + 1) * 2048],
                         in_=k.dr["L0_attn_w_qup"].ap()[ts(q), :]).then_inc(*dA.inc(16))

    for tl in range(8):
        t = half * 8 + tl

        @block.sync
        def _(sy, t=t):
            sy.wait_ge(aV.h, aV.val)
            sy.dma_start(out=k.sb["scr3"][:, 0:D],
                         in_=x.ap()[ts(t), :]).then_inc(*dA.inc(16))

        @block.scalar
        def _(s, t=t):
            s.wait_ge(dA.h, dA.val)
            s.activation(xlnb, xlnb, AF.Identity, bias=nmr1[:, t:t + 1],
                         scale=rstd1[:, t:t + 1]).then_inc(*aS.inc())

        @block.vector
        def _(v):
            v.wait_ge(aS.h, aS.val)
            v.engine_nop().then_inc(*aV.inc())

        emit_transpose16(k, block, xlnb, lambda c: xfm[:, ts(c)], aT, aV)

        @block.tensor
        def _(tt, t=t):
            tt.wait_ge(aV.h, aV.val)
            tt.wait_ge(dA.h, dA.val)
            for c in range(NCH):
                ins = tt.matmul(PS[0][:, 0:512], xfm[:, ts(c)],
                                wqc[:, c * 512:(c + 1) * 512],
                                start=(c == 0), stop=(c == NCH - 1))
                if c == NCH - 1:
                    ins.then_inc(*aT.inc())

        @block.vector
        def _(v):
            v.wait_ge(aT.h, aT.val)
            v.tensor_copy(scr2[:, 0:512], PS[0][:, 0:512])
            v.engine_nop().then_inc(*aV.inc())

        @block.tensor
        def _(tt):
            tt.wait_ge(aV.h, aV.val)
            for j in range(4):
                ins = tt.transpose(PS[6][:, ts(j)],
                                   scr2[:, j * 128:(j + 1) * 128], ident[:])
                if j == 3:
                    ins.then_inc(*aT.inc())

        @block.vector
        def _(v):
            v.wait_ge(aT.h, aT.val)
            v.tensor_copy(scr2[:, 512:1024], PS[6][:, 0:512])
            v.engine_nop().then_inc(*aV.inc())

        @block.tensor
        def _(tt):
            tt.wait_ge(aV.h, aV.val)
            for qc in range(4):
                for g4 in range(4):
                    ins = tt.matmul(
                        PS[2 + g4][:, 0:512],
                        scr2[:, 512 + qc * 128:512 + (qc + 1) * 128],
                        wqup[:, qc * 2048 + g4 * 512:qc * 2048 + (g4 + 1) * 512],
                        start=(qc == 0), stop=(qc == 3))
                    if qc == 3 and g4 == 3:
                        ins.then_inc(*aT.inc())

        @block.vector
        def _(v):
            v.wait_ge(aT.h, aT.val)
            for g4 in range(4):
                v.tensor_copy(scr3[:, g4 * 512:(g4 + 1) * 512],
                              PS[2 + g4][:, 0:512])
            v.engine_nop().then_inc(*aV.inc())

        @block.scalar
        def _(s):
            s.wait_ge(aV.h, aV.val)
            s.activation(scr4[:, 0:D], scr3[:, 0:D], AF.Square).then_inc(*aS.inc())

        @block.vector
        def _(v):
            v.wait_ge(aS.h, aS.val)
            sq3 = scr4[:, 0:D].rearrange("p (h d) -> p h d", d=HD)
            v.reduce_sum(tcol[:, 10:26], sq3, axis=AX.X).then_inc(*nA.inc())
            v.wait_ge(nA.h, nA.val)
            v.tensor_scalar(tcol[:, 10:26], tcol[:, 10:26], 1.0 / HD, EPS_RMS,
                            ALU.mult, ALU.add).then_inc(*nA.inc())
            v.wait_ge(nA.h, nA.val)
            v.reciprocal(tcol[:, 10:26], tcol[:, 10:26])
            v.engine_nop().then_inc(*aV.inc())

        @block.scalar
        def _(s):
            s.wait_ge(aV.h, aV.val)
            s.activation(tcol[:, 10:26], tcol[:, 10:26], AF.Sqrt).then_inc(*aS.inc())

        @block.vector
        def _(v, t=t):
            v.wait_ge(aS.h, aS.val)
            q3 = scr3[:, 0:D].rearrange("p (h d) -> p h d", d=HD)
            o3 = scr4[:, 0:D].rearrange("p (h d) -> p h d", d=HD)
            cc = cos_tm[:, t * 32:(t + 1) * 32].unsqueeze(1).to_broadcast(
                [128, H, 32])
            ss = sin_tm[:, t * 32:(t + 1) * 32].unsqueeze(1).to_broadcast(
                [128, H, 32])
            t3a = scr2[:, 0:512].rearrange("p (h d) -> p h d", d=32)
            t3b = scr2[:, 512:1024].rearrange("p (h d) -> p h d", d=32)
            t3c = k.sb["scr"][:, 0:512].rearrange("p (h d) -> p h d", d=32)
            t3d = k.sb["scr"][:, 512:1024].rearrange("p (h d) -> p h d", d=32)
            v.tensor_mul(t3a, q3[:, :, 0:32], cc)
            v.tensor_mul(t3b, q3[:, :, 32:64], ss)
            v.tensor_mul(t3c, q3[:, :, 0:32], ss)
            v.tensor_mul(t3d, q3[:, :, 32:64], cc).then_inc(*nA.inc())
            v.wait_ge(nA.h, nA.val)
            v.tensor_sub(o3[:, :, 0:32], t3a, t3b)
            v.tensor_add(o3[:, :, 32:64], t3c, t3d)
            v.tensor_copy(o3[:, :, 64:128], q3[:, :, 64:128]).then_inc(*nA.inc())
            v.wait_ge(nA.h, nA.val)
            v.tensor_mul(o3, o3, tcol[:, 10:26].unsqueeze(2).to_broadcast(
                [128, H, HD])).then_inc(*nA.inc())
            v.wait_ge(nA.h, nA.val)
            v.tensor_mul(o3, o3, wqs_b[:].unsqueeze(1).to_broadcast(
                [128, H, HD]))
            v.engine_nop().then_inc(*aV.inc())

        for grp in range(4):
            pb = PS[6 + (grp % 2)]

            @block.tensor
            def _(tt, grp=grp, pb=pb):
                tt.wait_ge(aV.h, aV.val)
                for j in range(4):
                    c = grp * 4 + j
                    ins = tt.transpose(pb[:, ts(j)], scr4[:, ts(c)], ident[:])
                    if j == 3:
                        ins.then_inc(*aT.inc())

            @block.vector
            def _(v, grp=grp, pb=pb, tl=tl):
                v.wait_ge(aT.h, aT.val)
                base = A_QST + tl * 2048
                v.tensor_copy(ar[:, base + grp * 512:base + (grp + 1) * 512],
                              pb[:, 0:512])
                v.engine_nop().then_inc(*aV.inc())


def emit_attn_B(k, block, half, sems):
    b_tiles = getattr(k, "b_tiles", 8)
    b_stage = getattr(k, "b_stage", 4)
    b_skip = getattr(k, "b_skip", set())
    aV, aS, aT, nA, dA, aG = sems
    ar = k.arena
    ident = k.sb["ident"]
    k_fm = k.sb["k_fm"]; v_tm = k.sb["v_tm"]
    kcompT = k.sb["kcompT"]; ckv_V = k.sb["ckv_V"]; ckv_K = k.sb["ckv_K"]
    qi_fm = k.sb["qi_fm"]; wih_tm = k.sb["wih_tm"]
    wmask = k.sb["wmask"]
    tcol = k.sb["colscr"]
    scr = k.sb["scr"]; scr2 = k.sb["scr2"]; scr3 = k.sb["scr3"]
    PS = k.psN

    gw = ar[:, A_GW:A_GW + 8192]
    attnb = ar[:, A_BSC:A_BSC + 2048]
    groupedb = ar[:, A_BSC + 2048:A_BSC + 4096]
    gfm = ar[:, A_BSC + 4096:A_BSC + 6144]

    if "gw" not in b_skip:
        @block.sync
        def _(sy, half=half):
            sy.wait_ge(aV.h, aV.val)
            for g in range(G):
                for ic in range(4):
                    sy.dma_start(
                        out=gw[:, (g * 4 + ic) * 512:(g * 4 + ic + 1) * 512],
                        in_=k.dr["L0_attn_group_w"].ap()[g, ic * 128:(ic + 1) * 128, :]
                    ).then_inc(*dA.inc(16))

    for tl in range(b_tiles):
        t = half * 8 + tl
        win = t // 2
        qbase = A_QST + tl * 2048

        if "mask" not in b_skip:
            @block.gpsimd
            def _(g, t=t):
                g.wait_ge(aV.h, aV.val)
                g.memset(scr[:, 640:768], 1.0).then_inc(*aG.inc())
                g.wait_ge(aG.h, aG.val)
                g.affine_select(out=scr[:, 640:768], in_=scr[:, 640:768],
                                pattern=[[-16, 128]], compare_op=ALU.is_gt,
                                fill=0.0, base=t * 128 - 15,
                                channel_multiplier=1).then_inc(*aG.inc())

        if "hs" in b_skip:
            continue
        @block.tensor
        def _(tt, t=t):
            tt.wait_ge(aV.h, aV.val)
            tt.wait_ge(dA.h, dA.val)
            for ih in range(4):
                lhs = qi_fm[:, (ih // 2) * T + t * 128:(ih // 2) * T + (t + 1) * 128]
                rhsv = k.sb["kcompT2"][:, (ih % 2) * 128:(ih % 2 + 1) * 128]
                ins = tt.matmul(PS[0][:, ih * 128:(ih + 1) * 128], lhs,
                                rhsv, start=True, stop=True)
                if ih == 3:
                    ins.then_inc(*aT.inc())

        if "isc" in b_skip:
            continue
        @block.vector
        def _(v, t=t):
            v.wait_ge(aT.h, aT.val)
            v.wait_ge(aG.h, aG.val)
            isc = scr2[:, 0:128]
            for ih in range(4):
                wb = wih_tm[:, t * 4 + ih:t * 4 + ih + 1].to_broadcast([128, 128])
                if ih == 0:
                    v.scalar_tensor_tensor(isc, PS[0][:, 0:128], 0.0, wb,
                                           ALU.max, ALU.mult)
                else:
                    v.scalar_tensor_tensor(scr2[:, 128:256],
                                           PS[0][:, ih * 128:(ih + 1) * 128],
                                           0.0, wb, ALU.max, ALU.mult)
                    v.tensor_add(isc, isc, scr2[:, 128:256])
            v.tensor_scalar(scr2[:, 256:384], scr[:, 640:768], 1e30, -1e30,
                            ALU.mult, ALU.add)
            v.tensor_add(isc, isc, scr2[:, 256:384])
            v.engine_nop().then_inc(*aV.inc())

        @block.vector
        def _(v, t=t):
            sbuf_a = scr2[:, 0:128]
            sbuf_b = scr2[:, 128:256]
            m8 = tcol[:, 16:24]
            if t >= 8:
                cur, oth = sbuf_a, sbuf_b
                for it in range(8):
                    v.max(out=m8, in_=cur).then_inc(*nA.inc())
                    v.wait_ge(nA.h, nA.val)
                    v.match_replace(out=oth, in_to_replace=m8, in_values=cur,
                                    imm_value=SENT).then_inc(*nA.inc())
                    v.wait_ge(nA.h, nA.val)
                    cur, oth = oth, cur
                v.tensor_scalar(scr2[:, 384:512], cur, SENT, None,
                                ALU.is_equal).then_inc(*nA.inc())
                v.wait_ge(nA.h, nA.val)
                v.tensor_mul(scr2[:, 384:512], scr2[:, 384:512], scr[:, 640:768])
            else:
                v.tensor_copy(scr2[:, 384:512], scr[:, 640:768])
            v.tensor_scalar(scr2[:, 256:384], scr2[:, 384:512], 1e30, -1e30,
                            ALU.mult, ALU.add)
            v.reduce_max(tcol[:, 26:27], scr2[:, 384:512], axis=AX.X)
            v.engine_nop().then_inc(*aV.inc())

        if b_stage < 2:
            continue
        for h in range(H if b_stage >= 3 else 1):
            bank = PS[2 + h // 4]
            bcol = (h % 4) * 128

            @block.tensor
            def _(tt, h=h, qbase=qbase):
                tt.wait_ge(aV.h, aV.val)
                tt.matmul(PS[0][:, 0:128],
                          ar[:, qbase + h * 128:qbase + (h + 1) * 128],
                          ckv_K[:], start=True, stop=True).then_inc(*aT.inc())

            @block.vector
            def _(v, h=h):
                v.wait_ge(aT.h, aT.val)
                smb = scr2[:, 512:640]
                v.tensor_add(smb, PS[0][:, 0:128], scr2[:, 256:384])
                v.reduce_max(tcol[:, 27:28], smb, axis=AX.X, negate=True)
                v.engine_nop().then_inc(*aV.inc())

            @block.scalar
            def _(s, h=h):
                s.wait_ge(aV.h, aV.val)
                s.activation(scr2[:, 640:768], scr2[:, 512:640], AF.Exp,
                             bias=tcol[:, 27:28],
                             accum_out=tcol[:, 28:29]).then_inc(*aS.inc())

            @block.vector
            def _(v, h=h):
                v.wait_ge(aS.h, aS.val)
                v.reciprocal(tcol[:, 28:29], tcol[:, 28:29]).then_inc(*nA.inc())
                v.wait_ge(nA.h, nA.val)
                v.tensor_mul(tcol[:, 28:29], tcol[:, 28:29],
                             tcol[:, 26:27]).then_inc(*nA.inc())
                v.wait_ge(nA.h, nA.val)
                v.tensor_scalar(scr2[:, 768:896], scr2[:, 640:768],
                                tcol[:, 28:29], None, ALU.mult)
                v.engine_nop().then_inc(*aV.inc())

            @block.tensor
            def _(tt, h=h):
                tt.wait_ge(aV.h, aV.val)
                tt.transpose(PS[1][:, 0:128], scr2[:, 768:896],
                             ident[:]).then_inc(*aT.inc())

            @block.vector
            def _(v, h=h):
                v.wait_ge(aT.h, aT.val)
                v.tensor_copy(scr2[:, 896:1024], PS[1][:, 0:128])
                v.engine_nop().then_inc(*aV.inc())

            @block.tensor
            def _(tt, h=h, win=win, bank=bank, bcol=bcol, qbase=qbase):
                tt.wait_ge(aV.h, aV.val)
                tt.wait_ge(aS.h, aS.val)
                tt.matmul(bank[:, bcol:bcol + 128], ckv_V[:],
                          scr2[:, 896:1024], start=(h % 4 == 0), stop=False)
                tt.matmul(PS[6][:, 0:512],
                          ar[:, qbase + h * 128:qbase + (h + 1) * 128],
                          k_fm[:, win * 256:win * 256 + 512],
                          start=True, stop=True).then_inc(*aT.inc())

            @block.vector
            def _(v, h=h):
                v.wait_ge(aT.h, aT.val)
                v.reduce_max(tcol[:, 29:30], PS[6][:, 0:512], axis=AX.X,
                             negate=True)
                v.engine_nop().then_inc(*aV.inc())

            @block.scalar
            def _(s, h=h):
                s.wait_ge(aV.h, aV.val)
                s.activation(scr3[:, 0:512], PS[6][:, 0:512], AF.Exp,
                             bias=tcol[:, 29:30]).then_inc(*aS.inc())

            @block.vector
            def _(v, h=h, t=t, win=win):
                v.wait_ge(aS.h, aS.val)
                mvar = (t % 2) * 2 + (1 if win == 0 else 0)
                v.tensor_mul(scr3[:, 0:512], scr3[:, 0:512],
                             wmask[:, mvar * 512:(mvar + 1) * 512]).then_inc(*nA.inc())
                v.wait_ge(nA.h, nA.val)
                v.reduce_sum(tcol[:, 30:31], scr3[:, 0:512],
                             axis=AX.X).then_inc(*nA.inc())
                v.wait_ge(nA.h, nA.val)
                v.reciprocal(tcol[:, 30:31], tcol[:, 30:31]).then_inc(*nA.inc())
                v.wait_ge(nA.h, nA.val)
                v.tensor_scalar(scr3[:, 512:1024], scr3[:, 0:512],
                                tcol[:, 30:31], None, ALU.mult)
                v.engine_nop().then_inc(*aV.inc())

            @block.tensor
            def _(tt, h=h):
                tt.wait_ge(aV.h, aV.val)
                for j in range(4):
                    ins = tt.transpose(PS[1][:, ts(j)],
                                       scr3[:, 512 + j * 128:512 + (j + 1) * 128],
                                       ident[:])
                    if j == 3:
                        ins.then_inc(*aT.inc())

            @block.vector
            def _(v, h=h):
                v.wait_ge(aT.h, aT.val)
                v.tensor_copy(scr3[:, 1024:1536], PS[1][:, 0:512])
                v.engine_nop().then_inc(*aV.inc())

            @block.tensor
            def _(tt, h=h, win=win, bank=bank, bcol=bcol):
                tt.wait_ge(aV.h, aV.val)
                for kc in range(4):
                    ins = tt.matmul(
                        bank[:, bcol:bcol + 128],
                        v_tm[:, (win * 2 + kc) * 128:(win * 2 + kc + 1) * 128],
                        scr3[:, 1024 + kc * 128:1024 + (kc + 1) * 128],
                        start=False, stop=(h % 4 == 3 and kc == 3))
                    if kc == 3:
                        ins.then_inc(*aT.inc())

        if b_stage < 4:
            continue
        @block.vector
        def _(v, t=t):
            v.wait_ge(aT.h, aT.val)
            for g4 in range(4):
                v.tensor_copy(attnb[:, g4 * 512:(g4 + 1) * 512],
                              PS[2 + g4][:, 0:512])
            v.engine_nop().then_inc(*aV.inc())

        @block.tensor
        def _(tt, t=t):
            tt.wait_ge(aV.h, aV.val)
            for g in range(G):
                for ic in range(4):
                    ins = tt.matmul(
                        PS[2 + g][:, 0:512],
                        attnb[:, (g * 4 + ic) * 128:(g * 4 + ic + 1) * 128],
                        gw[:, (g * 4 + ic) * 512:(g * 4 + ic + 1) * 512],
                        start=(ic == 0), stop=(ic == 3))
                    if g == 3 and ic == 3:
                        ins.then_inc(*aT.inc())

        @block.vector
        def _(v, t=t):
            v.wait_ge(aT.h, aT.val)
            for g4 in range(4):
                v.tensor_copy(groupedb[:, g4 * 512:(g4 + 1) * 512],
                              PS[2 + g4][:, 0:512])
            v.engine_nop().then_inc(*aV.inc())

        for grp in range(4):
            pb = PS[6 + (grp % 2)]

            @block.tensor
            def _(tt, grp=grp, pb=pb):
                tt.wait_ge(aV.h, aV.val)
                for j in range(4):
                    c = grp * 4 + j
                    ins = tt.transpose(pb[:, ts(j)], groupedb[:, ts(c)], ident[:])
                    if j == 3:
                        ins.then_inc(*aT.inc())

            @block.vector
            def _(v, grp=grp, pb=pb):
                v.wait_ge(aT.h, aT.val)
                v.tensor_copy(gfm[:, grp * 512:(grp + 1) * 512], pb[:, 0:512])
                v.engine_nop().then_inc(*aV.inc())

        @block.sync
        def _(sy, t=t):
            sy.wait_ge(aV.h, aV.val)
            for c in range(NCH):
                sy.dma_start(out=k.dr["grp_hbm"].ap()[ts(c), ts(t)],
                             in_=gfm[:, ts(c)]).then_inc(*dA.inc(16))

    @block.vector
    def _(v, half=half):
        v.wait_ge(dA.h, dA.val)
        v.engine_nop().then_inc(*k.b_done[half].inc())


def emit_attn_final(k, block, sems):
    aV, aS, aT, nA, dA, aG = sems
    ar = k.arena
    nmean = k.sb["nmean"]; rstd2 = k.sb["rstd2"]; nmr2 = k.sb["nmr2"]
    tcol = k.sb["colscr"]
    scr2 = k.sb["scr2"]; scr3 = k.sb["scr3"]; scr4 = k.sb["scr4"]
    PS = k.psN
    x = k.dr["x"]

    fw = ar[:, 0:ARENA]

    @block.sync
    def _(sy):
        sy.wait_ge(k.b_done[1].h, k.b_done[1].val)
        for kc in range(NCH):
            sy.dma_start(out=fw[:, kc * 2048:(kc + 1) * 2048],
                         in_=k.dr["L0_attn_final_w"].ap()[ts(kc), :]).then_inc(*dA.inc(16))

    for t in range(NT):
        @block.sync
        def _(sy, t=t):
            if t > 0:
                sy.wait_ge(aV.h, aV.val)
            sy.wait_ge(dA.h, dA.val)
            for c in range(NCH):
                sy.dma_start(out=scr3[:, ts(c)],
                             in_=k.dr["grp_hbm"].ap()[ts(c), ts(t)]).then_inc(*dA.inc(16))
            sy.dma_start(out=scr4[:, 0:D], in_=x.ap()[ts(t), :]).then_inc(*dA.inc(16))

        @block.tensor
        def _(tt, t=t):
            tt.wait_ge(dA.h, dA.val)
            if t > 0:
                tt.wait_ge(aV.h, aV.val)
            for kc in range(NCH):
                for g4 in range(4):
                    ins = tt.matmul(
                        PS[0 + g4][:, 0:512], scr3[:, ts(kc)],
                        fw[:, kc * 2048 + g4 * 512:kc * 2048 + (g4 + 1) * 512],
                        start=(kc == 0), stop=(kc == NCH - 1))
                    if kc == NCH - 1 and g4 == 3:
                        ins.then_inc(*aT.inc())

        @block.vector
        def _(v, t=t):
            v.wait_ge(aT.h, aT.val)
            for g4 in range(4):
                v.tensor_add(scr3[:, g4 * 512:(g4 + 1) * 512],
                             PS[0 + g4][:, 0:512],
                             scr4[:, g4 * 512:(g4 + 1) * 512])
            v.reduce_sum(tcol[:, 0:1], scr3[:, 0:D], axis=AX.X)
            v.engine_nop().then_inc(*aV.inc())

        @block.scalar
        def _(s, t=t):
            s.wait_ge(aV.h, aV.val)
            s.activation(scr4[:, 0:D], scr3[:, 0:D], AF.Square).then_inc(*aS.inc())

        @block.sync
        def _(sy, t=t):
            sy.wait_ge(aS.h, aS.val)
            sy.dma_start(out=k.dr["x2_hbm"].ap()[ts(t), :],
                         in_=scr3[:, 0:D]).then_inc(*dA.inc(16))

        @block.vector
        def _(v, t=t):
            v.wait_ge(aS.h, aS.val)
            v.reduce_sum(tcol[:, 1:2], scr4[:, 0:D], axis=AX.X).then_inc(*nA.inc())
            v.wait_ge(nA.h, nA.val)
            v.tensor_scalar(nmean[:, t:t + 1], tcol[:, 0:1], -1.0 / D, None,
                            ALU.mult).then_inc(*nA.inc())
            v.wait_ge(nA.h, nA.val)
            v.tensor_mul(tcol[:, 2:3], nmean[:, t:t + 1],
                         nmean[:, t:t + 1]).then_inc(*nA.inc())
            v.wait_ge(nA.h, nA.val)
            v.scalar_tensor_tensor(tcol[:, 3:4], tcol[:, 1:2], 1.0 / D,
                                   tcol[:, 2:3], ALU.mult,
                                   ALU.subtract).then_inc(*nA.inc())
            v.wait_ge(nA.h, nA.val)
            v.tensor_scalar(tcol[:, 3:4], tcol[:, 3:4], EPS_LN, None,
                            ALU.add).then_inc(*nA.inc())
            v.wait_ge(nA.h, nA.val)
            v.reciprocal(tcol[:, 3:4], tcol[:, 3:4])
            v.engine_nop().then_inc(*aV.inc())

        @block.scalar
        def _(s, t=t):
            s.wait_ge(aV.h, aV.val)
            s.activation(rstd2[:, t:t + 1], tcol[:, 3:4], AF.Sqrt).then_inc(*aS.inc())

        @block.vector
        def _(v, t=t):
            v.wait_ge(aS.h, aS.val)
            v.tensor_mul(nmr2[:, t:t + 1], nmean[:, t:t + 1], rstd2[:, t:t + 1])
            v.engine_nop().then_inc(*aV.inc())

    k.fin_aV = aV
    k.fin_dA = dA





SQ2048 = math.sqrt(2048.0)
SQ8192 = math.sqrt(8192.0)


def emit_ffn_tern(k, block):
    fV, fS, fT, nF, fD, fG = k.f_sems
    fW = k.fW
    ar = k.arena
    fscal = k.sb["fscal"]
    B_IN, B_TMP, B_OUT = 16384, 20480, 24576

    jobs = []
    for i in range(16):
        for j in range(4):
            jobs.append((k.dr["L0_ffn_w_gate"].ap()[ts(i), j * 2048:(j + 1) * 2048],
                         0, [("tern_g", i, j * 4 + jb, jb) for jb in range(4)]))
    for i in range(16):
        for j in range(4):
            jobs.append((k.dr["L0_ffn_w_up"].ap()[ts(i), j * 2048:(j + 1) * 2048],
                         1, [("tern_u", i, j * 4 + jb, jb) for jb in range(4)]))
    for i in range(64):
        jobs.append((k.dr["L0_ffn_w_down"].ap()[ts(i), :],
                     2, [("tern_d", i, jb, jb) for jb in range(4)]))

    vmark = []
    wmark = []
    xq_done = fV.val
    for nb, (srcap, mi, outs) in enumerate(jobs):
        b = nb % 2
        inb = ar[:, B_IN + b * 2048:B_IN + (b + 1) * 2048]
        tmp = ar[:, B_TMP + b * 2048:B_TMP + (b + 1) * 2048]
        out_bf = ar[:, B_OUT + b * 1024:B_OUT + (b + 1) * 1024].bitcast(BF16)

        @block.sync
        def _(sy, srcap=srcap, inb=inb, nb=nb):
            if nb == 0:
                sy.wait_ge(fV.h, xq_done)
            if nb >= 2:
                sy.wait_ge(fV.h, vmark[nb - 2])
            sy.dma_start(out=inb, in_=srcap).then_inc(*fD.inc(16))

        @block.vector
        def _(v, inb=inb, tmp=tmp, mi=mi):
            v.wait_ge(fD.h, fD.val)
            v.tensor_scalar(tmp, inb, fscal[:, mi:mi + 1], None, ALU.is_gt)
            v.tensor_scalar(inb, inb, fscal[:, 6 + mi:7 + mi], None, ALU.is_ge)
            v.tensor_tensor(tmp, tmp, inb, ALU.add)
            v.engine_nop().then_inc(*fV.inc())
        vmark.append(fV.val)

        @block.scalar
        def _(s, tmp=tmp, out_bf=out_bf, nb=nb):
            s.wait_ge(fV.h, fV.val)
            if nb >= 2:
                s.wait_ge(fW.h, wmark[nb - 2])
            s.activation(out_bf, tmp, AF.Identity,
                         bias=-1.0).then_inc(*fS.inc())

        @block.sync
        def _(sy, out_bf=out_bf, outs=outs):
            sy.wait_ge(fS.h, fS.val)
            for tname, i0, i1, jb in outs:
                sy.dma_start(out=k.dr[tname].ap()[i0, i1],
                             in_=out_bf[:, jb * 512:(jb + 1) * 512]
                             ).then_inc(*fW.inc(16))
        wmark.append(fW.val)


def emit_ffn_gateup(k, block):
    fV, fS, fT, nF, fD, fG = k.f_sems
    fW = k.fW
    ar = k.arena
    e0row = k.sb["e0row"]; h128 = k.sb["h128"]; ident = k.sb["ident"]
    sigma1 = k.sb["sigma1"]; fscal = k.sb["fscal"]
    scr = k.sb["scr"]; scr2 = k.sb["scr2"]; scr3 = k.sb["scr3"]; scr4 = k.sb["scr4"]
    PS = k.psN
    xq_bf = ar[:, 0:16384].bitcast(BF16)

    rowmark = []
    for g in range(4):
        @block.tensor
        def _(tt, g=g):
            if g == 0:
                tt.wait_ge(fV.h, fV.val)
            else:
                tt.wait_ge(fV.h, rowmark[g - 1])
            for j in range(4):
                ins = tt.transpose(PS[7][0:1, j * 128:(j + 1) * 128],
                                   sigma1[:, g * 4 + j:g * 4 + j + 1], ident[:])
            ins.then_inc(*fT.inc())

        @block.vector
        def _(v, g=g):
            v.wait_ge(fT.h, fT.val)
            v.tensor_copy(scr3[0:1, g * 512:(g + 1) * 512], PS[7][0:1, 0:512])
            v.engine_nop().then_inc(*fV.inc())
        rowmark.append(fV.val)

    @block.tensor
    def _(tt):
        tt.wait_ge(fV.h, fV.val)
        for q in range(4):
            ins = tt.matmul(PS[4 + q][:, 0:512], e0row[:],
                            scr3[0:128, q * 512:(q + 1) * 512],
                            start=True, stop=True)
        ins.then_inc(*fT.inc())

    @block.vector
    def _(v):
        v.wait_ge(fT.h, fT.val)
        for q in range(4):
            v.tensor_scalar(scr4[:, q * 512:(q + 1) * 512], PS[4 + q][:, 0:512],
                            fscal[:, 4:5], None, ALU.mult)
        for q in range(4):
            v.tensor_scalar(scr3[:, q * 512:(q + 1) * 512], PS[4 + q][:, 0:512],
                            fscal[:, 3:4], None, ALU.mult)
        v.engine_nop().then_inc(*fV.inc())

    NG = 64 * 4
    gumark = {}; hmark = {}; h128mark = {}; gsmark = {}; silumark = {}
    ubmark = {}; uwmark = {}; tmark = {}

    def emit_h128(gg):
        oc, qq = gg // 4, gg % 4

        @block.tensor
        def _(tt, gg=gg, qq=qq):
            tt.wait_ge(fV.h, hmark[gg])
            tt.matmul(PS[4 + qq % 2][:, 0:512], h128[:],
                      scr[:, (gg % 2) * 512:(gg % 2 + 1) * 512],
                      start=True, stop=True).then_inc(*fT.inc())
        h128mark[gg] = fT.val

    for ob in range(16):
        b = ob % 2
        stg_bf = ar[:, 16384 + b * 8192:16384 + (b + 1) * 8192].bitcast(BF16)

        @block.sync
        def _(sy, ob=ob, stg_bf=stg_bf):
            if ob == 0:
                sy.wait_ge(fW.h, fW.val)
            if ob >= 2:
                sy.wait_ge(fT.h, tmark[ob - 2])
            for hc in range(16):
                sy.dma_start(out=stg_bf[:, hc * 512:(hc + 1) * 512],
                             in_=k.dr["tern_g"].ap()[hc, ob]).then_inc(*fD.inc(16))
            for hc in range(16):
                sy.dma_start(out=stg_bf[:, 8192 + hc * 512:8192 + (hc + 1) * 512],
                             in_=k.dr["tern_u"].ap()[hc, ob]).then_inc(*fD.inc(16))
        dmark = fD.val

        for j in range(4):
            oc = ob * 4 + j
            for q in range(4):
                g = oc * 4 + q
                sl = g % 2

                @block.tensor
                def _(tt, j=j, q=q, stg_bf=stg_bf, dmark=dmark, first=(j == 0 and q == 0)):
                    if first:
                        tt.wait_ge(fD.h, dmark)
                    for hc in range(16):
                        tt.matmul(PS[q % 2][:, 0:512],
                                  stg_bf[:, hc * 512 + j * 128:hc * 512 + (j + 1) * 128],
                                  xq_bf[:, hc * 2048 + q * 512:hc * 2048 + (q + 1) * 512],
                                  start=(hc == 0), stop=(hc == 15))
                    for hc in range(16):
                        ins = tt.matmul(PS[2 + q % 2][:, 0:512],
                                        stg_bf[:, 8192 + hc * 512 + j * 128:
                                               8192 + hc * 512 + (j + 1) * 128],
                                        xq_bf[:, hc * 2048 + q * 512:hc * 2048 + (q + 1) * 512],
                                        start=(hc == 0), stop=(hc == 15))
                    ins.then_inc(*fT.inc())
                gumark[g] = fT.val

                @block.vector
                def _(v, g=g, q=q, sl=sl):
                    v.wait_ge(fT.h, gumark[g])
                    if g >= 2:
                        v.wait_ge(fW.h, uwmark[g - 2])
                    v.tensor_tensor(scr[:, sl * 512:(sl + 1) * 512],
                                    PS[q % 2][:, 0:512],
                                    scr3[:, q * 512:(q + 1) * 512], ALU.mult)
                    v.tensor_tensor(scr2[:, sl * 512:(sl + 1) * 512],
                                    PS[2 + q % 2][:, 0:512],
                                    scr4[:, q * 512:(q + 1) * 512], ALU.mult)
                    v.engine_nop().then_inc(*fV.inc())
                gsmark[g] = fV.val

                @block.scalar
                def _(s, g=g, sl=sl):
                    s.wait_ge(fV.h, gsmark[g])
                    s.activation(scr[:, sl * 512:(sl + 1) * 512],
                                 scr[:, sl * 512:(sl + 1) * 512],
                                 AF.Silu).then_inc(*fS.inc())
                silumark[g] = fS.val

                @block.vector
                def _(v, g=g, sl=sl):
                    v.wait_ge(fS.h, silumark[g])
                    v.tensor_tensor(scr[:, sl * 512:(sl + 1) * 512],
                                    scr[:, sl * 512:(sl + 1) * 512],
                                    scr2[:, sl * 512:(sl + 1) * 512], ALU.mult)
                    v.engine_nop().then_inc(*fV.inc())
                hmark[g] = fV.val

                if g >= 1:
                    emit_h128(g - 1)

                    @block.vector
                    def _(v, g1=g - 1):
                        v.wait_ge(fT.h, h128mark[g1])
                        if g1 >= 2:
                            v.wait_ge(fW.h, uwmark[g1 - 2])
                        v.tensor_copy(scr2[:, (g1 % 2) * 512:(g1 % 2 + 1) * 512],
                                      PS[4 + g1 % 2][:, 0:512])
                        v.engine_nop().then_inc(*fV.inc())
                    ubmark[g - 1] = fV.val

                    @block.sync
                    def _(sy, g1=g - 1):
                        sy.wait_ge(fV.h, ubmark[g1])
                        sy.dma_start(
                            out=k.dr["u_hbm"].ap()[ts(g1 // 4),
                                                   (g1 % 4) * 512:(g1 % 4 + 1) * 512],
                            in_=scr2[:, (g1 % 2) * 512:(g1 % 2 + 1) * 512]
                        ).then_inc(*fW.inc(16))
                    uwmark[g - 1] = fW.val
        tmark[ob] = gumark[(ob * 4 + 3) * 4 + 3]

    g = NG - 1
    emit_h128(g)

    @block.vector
    def _(v, g=g):
        v.wait_ge(fT.h, h128mark[g])
        v.tensor_copy(scr2[:, (g % 2) * 512:(g % 2 + 1) * 512],
                      PS[4 + g % 2][:, 0:512])
        v.engine_nop().then_inc(*fV.inc())
    ubmark[g] = fV.val

    @block.sync
    def _(sy, g=g):
        sy.wait_ge(fV.h, ubmark[g])
        sy.dma_start(out=k.dr["u_hbm"].ap()[ts(g // 4),
                                            (g % 4) * 512:(g % 4 + 1) * 512],
                     in_=scr2[:, (g % 2) * 512:(g % 2 + 1) * 512]
                     ).then_inc(*fW.inc(16))


def emit_ffn_phase2(k, block):
    fV, fS, fT, nF, fD, fG = k.f_sems
    fW = k.fW
    ar = k.arena
    e0row = k.sb["e0row"]; ident = k.sb["ident"]
    sigma2 = k.sb["sigma2"]; tcol = k.sb["colscr"]
    scr3 = k.sb["scr3"]
    PS = k.psN
    B = 16384
    lastz = None
    castmark = {}; xqwmark = {}

    for s in range(8):
        @block.sync
        def _(sy, s=s, lastz=lastz):
            if s == 0:
                sy.wait_ge(fW.h, fW.val)
            else:
                sy.wait_ge(fV.h, lastz)
            for oc in range(64):
                sy.dma_start(out=ar[:, oc * 256:(oc + 1) * 256],
                             in_=k.dr["u_hbm"].ap()[ts(oc), s * 256:(s + 1) * 256]
                             ).then_inc(*fD.inc(16))
        dmark = fD.val

        for st in range(6):
            stp = 1 << st

            @block.vector
            def _(v, st=st, stp=stp, dmark=dmark, scast=fS.val):
                if st == 0:
                    v.wait_ge(fD.h, dmark)
                    v.wait_ge(fS.h, scast)
                cur0 = 0 if st % 2 == 0 else B
                oth0 = B if st % 2 == 0 else 0
                for c in range(64):
                    p = c ^ stp
                    if c < p:
                        v.tensor_add(ar[:, oth0 + c * 256:oth0 + (c + 1) * 256],
                                     ar[:, cur0 + c * 256:cur0 + (c + 1) * 256],
                                     ar[:, cur0 + p * 256:cur0 + (p + 1) * 256])
                    else:
                        v.tensor_sub(ar[:, oth0 + c * 256:oth0 + (c + 1) * 256],
                                     ar[:, cur0 + p * 256:cur0 + (p + 1) * 256],
                                     ar[:, cur0 + c * 256:cur0 + (c + 1) * 256])
                v.engine_nop().then_inc(*fV.inc())

        @block.scalar
        def _(s_):
            s_.wait_ge(fV.h, fV.val)
            for c in range(64):
                ins = s_.activation(ar[:, B + c * 256:B + (c + 1) * 256],
                                    ar[:, c * 256:(c + 1) * 256], AF.Square)
            ins.then_inc(*fS.inc())

        @block.vector
        def _(v):
            v.wait_ge(fS.h, fS.val)
            n = 64
            while n > 1:
                n //= 2
                for i in range(n):
                    v.tensor_tensor(ar[:, B + i * 256:B + (i + 1) * 256],
                                    ar[:, B + (2 * i) * 256:B + (2 * i + 1) * 256],
                                    ar[:, B + (2 * i + 1) * 256:B + (2 * i + 2) * 256],
                                    ALU.max)
                v.engine_nop().then_inc(*nF.inc())
                v.wait_ge(nF.h, nF.val)
            v.engine_nop().then_inc(*fV.inc())

        @block.tensor
        def _(tt):
            tt.wait_ge(fV.h, fV.val)
            tt.transpose(PS[6][:, 0:128], ar[:, B:B + 128], ident[:])
            tt.transpose(PS[6][:, 128:256], ar[:, B + 128:B + 256],
                         ident[:]).then_inc(*fT.inc())

        @block.vector
        def _(v):
            v.wait_ge(fT.h, fT.val)
            v.reduce_max(tcol[:, 0:1], PS[6][:, 0:128], axis=AX.X).then_inc(*nF.inc())
            v.wait_ge(nF.h, nF.val)
            v.reduce_max(tcol[:, 1:2], PS[6][:, 128:256], axis=AX.X).then_inc(*nF.inc())
            v.wait_ge(nF.h, nF.val)
            v.tensor_scalar(tcol[:, 0:1], tcol[:, 0:1], EPS_FFN * EPS_FFN,
                            None, ALU.max)
            v.tensor_scalar(tcol[:, 1:2], tcol[:, 1:2], EPS_FFN * EPS_FFN,
                            None, ALU.max)
            v.engine_nop().then_inc(*fV.inc())

        @block.scalar
        def _(s_):
            s_.wait_ge(fV.h, fV.val)
            s_.activation(tcol[:, 0:2], tcol[:, 0:2],
                          AF.Sqrt).then_inc(*fS.inc())

        @block.vector
        def _(v, s=s):
            v.wait_ge(fS.h, fS.val)
            v.tensor_scalar(sigma2[:, 2 * s:2 * s + 1], tcol[:, 0:1],
                            1.0 / (127.0 * SQ8192), None, ALU.mult)
            v.tensor_scalar(sigma2[:, 2 * s + 1:2 * s + 2], tcol[:, 1:2],
                            1.0 / (127.0 * SQ8192), None, ALU.mult)
            v.reciprocal(tcol[:, 4:5], tcol[:, 0:1])
            v.reciprocal(tcol[:, 5:6], tcol[:, 1:2])
            v.engine_nop().then_inc(*nF.inc())
            v.wait_ge(nF.h, nF.val)
            v.tensor_scalar(tcol[:, 4:5], tcol[:, 4:5], 127.0, None, ALU.mult)
            v.tensor_scalar(tcol[:, 5:6], tcol[:, 5:6], 127.0, None, ALU.mult)
            v.engine_nop().then_inc(*fV.inc())

        @block.tensor
        def _(tt):
            tt.wait_ge(fV.h, fV.val)
            tt.transpose(PS[7][0:1, 0:128], tcol[:, 4:5], ident[:])
            tt.transpose(PS[7][0:1, 128:256], tcol[:, 5:6],
                         ident[:]).then_inc(*fT.inc())

        @block.vector
        def _(v):
            v.wait_ge(fT.h, fT.val)
            v.tensor_copy(scr3[0:1, 0:256], PS[7][0:1, 0:256])
            v.engine_nop().then_inc(*fV.inc())

        @block.tensor
        def _(tt):
            tt.wait_ge(fV.h, fV.val)
            tt.matmul(PS[6][:, 0:256], e0row[:], scr3[0:128, 0:256],
                      start=True, stop=True)
            tt.matmul(PS[6][:, 256:512], e0row[:], scr3[0:128, 0:256],
                      start=True, stop=True).then_inc(*fT.inc())

        @block.vector
        def _(v):
            v.wait_ge(fT.h, fT.val)
            v.tensor_copy(ar[:, B + 256:B + 768], PS[6][:, 0:512])
            v.engine_nop().then_inc(*fV.inc())

        for pp in range(32):
            @block.vector
            def _(v, pp=pp, s=s):
                if pp >= 2:
                    v.wait_ge(fS.h, castmark[(s, pp - 2)])
                z = ar[:, B + 768:B + 1280]
                w = ar[:, B + 1280 + (pp % 2) * 512:B + 1792 + (pp % 2) * 512]
                bb = ar[:, B + 2304:B + 2816]
                v.tensor_tensor(z, ar[:, pp * 512:(pp + 1) * 512],
                                ar[:, B + 256:B + 768], ALU.mult)
                peel_round(v, z, w, bb, 512)
                v.tensor_sub(w, z, w)
                v.engine_nop().then_inc(*fV.inc())
            vmarkp = fV.val

            @block.scalar
            def _(s_, pp=pp, s=s):
                s_.wait_ge(fV.h, fV.val)
                if pp >= 2:
                    s_.wait_ge(fW.h, xqwmark[(s, pp - 2)])
                cb = ar[:, B + 2816 + (pp % 2) * 256:
                        B + 3072 + (pp % 2) * 256].bitcast(BF16)
                w = ar[:, B + 1280 + (pp % 2) * 512:B + 1792 + (pp % 2) * 512]
                s_.activation(cb, w, AF.Identity, bias=-128.0).then_inc(*fS.inc())
            castmark[(s, pp)] = fS.val

            @block.sync
            def _(sy, pp=pp, s=s):
                sy.wait_ge(fS.h, fS.val)
                cb = ar[:, B + 2816 + (pp % 2) * 256:
                        B + 3072 + (pp % 2) * 256].bitcast(BF16)
                sy.dma_start(out=k.dr["xq2_hbm"].ap()[s, ts(2 * pp), :],
                             in_=cb[:, 0:256]).then_inc(*fW.inc(16))
                sy.dma_start(out=k.dr["xq2_hbm"].ap()[s, ts(2 * pp + 1), :],
                             in_=cb[:, 256:512]).then_inc(*fW.inc(16))
            xqwmark[(s, pp)] = fW.val
        lastz = vmarkp


def emit_ffn_down(k, block):
    fV, fS, fT, nF, fD, fG = k.f_sems
    fW = k.fW
    ar = k.arena
    e0row = k.sb["e0row"]; ident = k.sb["ident"]
    sigma2 = k.sb["sigma2"]; fscal = k.sb["fscal"]
    scr3 = k.sb["scr3"]
    PS = k.psN
    STG, SDQ, FM, YB, X2B = 16384, 24576, 25088, 26112, 28160
    xq2q_bf = ar[:, 0:16384].bitcast(BF16)
    qtmark = None
    dmW = {}
    scmark = {}; trmark = {}; ycmark = {}; yam = {}; ydm = {}; xdm = {}

    for q in range(4):
        rmk = []

        @block.tensor
        def _(tt, q=q):
            tt.wait_ge(fV.h, fV.val)
            for tl in range(4):
                ins = tt.transpose(PS[7][0:1, tl * 128:(tl + 1) * 128],
                                   sigma2[:, 4 * q + tl:4 * q + tl + 1], ident[:])
            ins.then_inc(*fT.inc())

        @block.vector
        def _(v):
            v.wait_ge(fT.h, fT.val)
            v.tensor_copy(scr3[0:1, 0:512], PS[7][0:1, 0:512])
            v.engine_nop().then_inc(*fV.inc())

        @block.tensor
        def _(tt):
            tt.wait_ge(fV.h, fV.val)
            tt.matmul(PS[6][:, 0:512], e0row[:], scr3[0:128, 0:512],
                      start=True, stop=True).then_inc(*fT.inc())

        @block.vector
        def _(v):
            v.wait_ge(fT.h, fT.val)
            v.tensor_scalar(ar[:, SDQ:SDQ + 512], PS[6][:, 0:512],
                            fscal[:, 5:6], None, ALU.mult)
            v.engine_nop().then_inc(*fV.inc())

        @block.sync
        def _(sy, q=q, qtmark=qtmark):
            if q == 0:
                sy.wait_ge(fW.h, fW.val)
            else:
                sy.wait_ge(fT.h, qtmark)
            for oc in range(64):
                for hh in range(2):
                    sy.dma_start(
                        out=xq2q_bf[:, oc * 512 + hh * 256:oc * 512 + (hh + 1) * 256],
                        in_=k.dr["xq2_hbm"].ap()[2 * q + hh, ts(oc), :]
                    ).then_inc(*fD.inc(16))
        qdmark = fD.val

        for dq in range(4):
            for w8 in range(4):
                gw = dq * 4 + w8

                @block.sync
                def _(sy, q=q, dq=dq, w8=w8, gw=gw):
                    if gw >= 2:
                        sy.wait_ge(fT.h, dmW[(q, gw - 2, "c")])
                    for oi in range(16):
                        oc = w8 * 16 + oi
                        sy.dma_start(
                            out=ar[:, STG + (gw % 2) * 4096 + oi * 256:
                                   STG + (gw % 2) * 4096 + (oi + 1) * 256
                                   ].bitcast(BF16),
                            in_=k.dr["tern_d"].ap()[oc, dq]).then_inc(*fD.inc(16))
                dmW[(q, gw)] = fD.val

                @block.tensor
                def _(tt, q=q, dq=dq, w8=w8, gw=gw, qdmark=qdmark):
                    tt.wait_ge(fD.h, dmW[(q, gw)])
                    if w8 == 0 and dq == 0:
                        tt.wait_ge(fD.h, qdmark)
                        tt.wait_ge(fV.h, fV.val)
                    for oi in range(16):
                        oc = w8 * 16 + oi
                        stg_bf = ar[:, STG + (gw % 2) * 4096 + oi * 256:
                                    STG + (gw % 2) * 4096 + (oi + 1) * 256
                                    ].bitcast(BF16)
                        for jdc in range(4):
                            ins = tt.matmul(
                                PS[jdc][:, 0:512],
                                stg_bf[:, jdc * 128:(jdc + 1) * 128],
                                xq2q_bf[:, oc * 512:(oc + 1) * 512],
                                start=(oc == 0), stop=(oc == 63))
                    ins.then_inc(*fT.inc())
                dmW[(q, gw, "c")] = fT.val
            mmark = fT.val

            for jdc in range(4):
                @block.vector
                def _(v, q=q, dq=dq, jdc=jdc, mmark=mmark):
                    v.wait_ge(fT.h, mmark)
                    if jdc >= 2:
                        v.wait_ge(fT.h, trmark[(q, dq, jdc - 2)])
                    if jdc == 0 and (q, dq) != (0, 0):
                        pq, pdq = (q, dq - 1) if dq > 0 else (q - 1, 3)
                        v.wait_ge(fW.h, ydm[(pq, pdq)])
                    v.tensor_tensor(ar[:, FM + (jdc % 2) * 512:FM + (jdc % 2 + 1) * 512],
                                    PS[jdc][:, 0:512], ar[:, SDQ:SDQ + 512], ALU.mult)
                    v.engine_nop().then_inc(*fV.inc())
                scmark[(q, dq, jdc)] = fV.val

                @block.tensor
                def _(tt, q=q, dq=dq, jdc=jdc):
                    tt.wait_ge(fV.h, scmark[(q, dq, jdc)])
                    if jdc >= 2:
                        tt.wait_ge(fV.h, ycmark[(q, dq, jdc - 2)])
                    for tl in range(4):
                        ins = tt.transpose(
                            PS[4 + jdc % 2][:, tl * 128:(tl + 1) * 128],
                            ar[:, FM + (jdc % 2) * 512 + tl * 128:
                               FM + (jdc % 2) * 512 + (tl + 1) * 128], ident[:])
                    ins.then_inc(*fT.inc())
                trmark[(q, dq, jdc)] = fT.val

                @block.vector
                def _(v, q=q, dq=dq, jdc=jdc):
                    v.wait_ge(fT.h, trmark[(q, dq, jdc)])
                    for tl in range(4):
                        v.tensor_copy(ar[:, YB + tl * 512 + jdc * 128:
                                      YB + tl * 512 + (jdc + 1) * 128],
                                      PS[4 + jdc % 2][:, tl * 128:(tl + 1) * 128])
                    v.engine_nop().then_inc(*fV.inc())
                ycmark[(q, dq, jdc)] = fV.val

            for tl in range(4):
                @block.sync
                def _(sy, q=q, dq=dq, tl=tl):
                    if tl >= 2:
                        sy.wait_ge(fV.h, yam[(q, dq, tl - 2)])
                    elif (q, dq) != (0, 0):
                        pq, pdq = (q, dq - 1) if dq > 0 else (q - 1, 3)
                        sy.wait_ge(fV.h, yam[(pq, pdq, tl + 2)])
                    sy.dma_start(out=ar[:, X2B + (tl % 2) * 512:X2B + (tl % 2 + 1) * 512],
                                 in_=k.dr["x2_hbm"].ap()[ts(q * 4 + tl),
                                                         dq * 512:(dq + 1) * 512]
                                 ).then_inc(*fD.inc(16))
                xdm[(q, dq, tl)] = fD.val

                @block.vector
                def _(v, q=q, dq=dq, tl=tl):
                    v.wait_ge(fD.h, xdm[(q, dq, tl)])
                    v.tensor_tensor(ar[:, YB + tl * 512:YB + (tl + 1) * 512],
                                    ar[:, YB + tl * 512:YB + (tl + 1) * 512],
                                    ar[:, X2B + (tl % 2) * 512:X2B + (tl % 2 + 1) * 512],
                                    ALU.add)
                    v.engine_nop().then_inc(*fV.inc())
                yam[(q, dq, tl)] = fV.val

                @block.sync
                def _(sy, q=q, dq=dq, tl=tl):
                    sy.wait_ge(fV.h, yam[(q, dq, tl)])
                    sy.dma_start(out=k.dr["y"].ap()[ts(q * 4 + tl),
                                                    dq * 512:(dq + 1) * 512],
                                 in_=ar[:, YB + tl * 512:YB + (tl + 1) * 512]
                                 ).then_inc(*fW.inc(16))
            ydm[(q, dq)] = fW.val
        qtmark = fT.val
    k.ffn_y_done = fW.val


def emit_ffn(k, block):
    fV = k.sem("f_v"); fS = k.sem("f_s"); fT = k.sem("f_t")
    nF = k.sem("f_nv"); fD = k.sem("f_d"); fG = k.sem("f_g")
    k.f_sems = (fV, fS, fT, nF, fD, fG)
    k.fW = k.sem("f_w")
    emit_ffn_scale(k, block)
    if getattr(k, "ffn_mode", "full") == "scale":
        k.fin2_V = fV; k.fin2_D = fD
        return
    emit_ffn_xq(k, block)
    if getattr(k, "ffn_mode", "full") == "xq":
        k.fin2_V = fV; k.fin2_D = fD
        return
    emit_ffn_tern(k, block)
    emit_ffn_gateup(k, block)
    if getattr(k, "ffn_mode", "full") == "gu":
        k.fin2_V = fV; k.fin2_D = fD; k.fin2_W = k.fW
        return
    emit_ffn_phase2(k, block)
    if getattr(k, "ffn_mode", "full") == "p2":
        k.fin2_V = fV; k.fin2_D = fD; k.fin2_W = k.fW
        return
    emit_ffn_down(k, block)
    k.fin2_V = fV; k.fin2_D = fD; k.fin2_W = k.fW


def emit_ffn_scale(k, block):
    fV, fS, fT, nF, fD, fG = k.f_sems
    ar = k.arena
    e0row = k.sb["e0row"]; h128 = k.sb["h128"]
    scr2 = k.sb["scr2"]
    PS = k.psN
    fscal = k.sbuf("fscal", [128, 16], F32)
    accb = k.sbuf("accb", [128, 256], F32)

    @block.gpsimd
    def _(g):
        g.memset(accb[:], 0.0).then_inc(*fG.inc())
        g.memset(fscal[:], 0.0).then_inc(*fG.inc())
        g.memset(k.sb["sigma1"][:], 0.0).then_inc(*fG.inc())
        g.memset(k.sb["sigma2"][:], 0.0).then_inc(*fG.inc())

    @block.vector
    def _(v):
        v.wait_ge(fG.h, fG.val)
        v.engine_nop().then_inc(*fV.inc())

    loads = []
    for i in range(16):
        for j in range(4):
            loads.append(("L0_ffn_w_gate",
                          k.dr["L0_ffn_w_gate"].ap()[ts(i), j * 2048:(j + 1) * 2048]))
    for i in range(16):
        for j in range(4):
            loads.append(("L0_ffn_w_up",
                          k.dr["L0_ffn_w_up"].ap()[ts(i), j * 2048:(j + 1) * 2048]))
    for i in range(64):
        loads.append(("L0_ffn_w_down", k.dr["L0_ffn_w_down"].ap()[ts(i), :]))
    dump = ar[:, 16384:18432]
    for nb, (name, srcap) in enumerate(loads):
        buf = ar[:, (nb % 2) * 2048:(nb % 2) * 2048 + 2048]

        @block.sync
        def _(sy, srcap=srcap, buf=buf):
            sy.wait_ge(fS.h, fS.val)
            sy.dma_start(out=buf, in_=srcap).then_inc(*fD.inc(16))

        @block.scalar
        def _(s, buf=buf, nb=nb):
            s.wait_ge(fD.h, fD.val)
            s.activation(dump, buf, AF.Abs,
                         accum_out=accb[:, nb:nb + 1]).then_inc(*fS.inc())

    @block.vector
    def _(v):
        v.wait_ge(fS.h, fS.val)
        v.reduce_sum(accb[:, 200:201], accb[:, 0:64], axis=AX.X).then_inc(*nF.inc())
        v.wait_ge(nF.h, nF.val)
        v.reduce_sum(accb[:, 201:202], accb[:, 64:128], axis=AX.X).then_inc(*nF.inc())
        v.wait_ge(nF.h, nF.val)
        v.reduce_sum(accb[:, 202:203], accb[:, 128:192], axis=AX.X)
        v.engine_nop().then_inc(*fV.inc())

    @block.tensor
    def _(tt):
        tt.wait_ge(fV.h, fV.val)
        tt.matmul(PS[6][0:1, 0:3], h128[:, 0:1], accb[:, 200:203],
                  start=True, stop=True).then_inc(*fT.inc())

    @block.vector
    def _(v):
        v.wait_ge(fT.h, fT.val)
        v.tensor_copy(accb[0:1, 204:207], PS[6][0:1, 0:3])
        v.engine_nop().then_inc(*fV.inc())

    @block.tensor
    def _(tt):
        tt.wait_ge(fV.h, fV.val)
        tt.matmul(PS[6][:, 4:7], e0row[:], accb[:, 204:207],
                  start=True, stop=True).then_inc(*fT.inc())

    @block.vector
    def _(v):
        v.wait_ge(fT.h, fT.val)
        v.tensor_scalar(fscal[:, 3:4], PS[6][:, 4:5], 1.0 / (IF * D), EPS_FFN,
                        ALU.mult, ALU.max).then_inc(*nF.inc())
        v.wait_ge(nF.h, nF.val)
        v.tensor_scalar(fscal[:, 4:5], PS[6][:, 5:6], 1.0 / (IF * D), EPS_FFN,
                        ALU.mult, ALU.max).then_inc(*nF.inc())
        v.wait_ge(nF.h, nF.val)
        v.tensor_scalar(fscal[:, 5:6], PS[6][:, 6:7], 1.0 / (IF * D), EPS_FFN,
                        ALU.mult, ALU.max).then_inc(*nF.inc())
        v.wait_ge(nF.h, nF.val)
        v.tensor_scalar(fscal[:, 0:1], fscal[:, 3:4], 0.5, None, ALU.mult)
        v.tensor_scalar(fscal[:, 1:2], fscal[:, 4:5], 0.5, None, ALU.mult)
        v.tensor_scalar(fscal[:, 2:3], fscal[:, 5:6], 0.5, None, ALU.mult)
        v.engine_nop().then_inc(*nF.inc())
        v.wait_ge(nF.h, nF.val)
        v.tensor_scalar(fscal[:, 6:7], fscal[:, 0:1], -1.0, None, ALU.mult)
        v.tensor_scalar(fscal[:, 7:8], fscal[:, 1:2], -1.0, None, ALU.mult)
        v.tensor_scalar(fscal[:, 8:9], fscal[:, 2:3], -1.0, None, ALU.mult)
        v.engine_nop().then_inc(*fV.inc())


def emit_ffn_xq(k, block):
    fV, fS, fT, nF, fD, fG = k.f_sems
    ar = k.arena
    ident = k.sb["ident"]; h128 = k.sb["h128"]; e0row = k.sb["e0row"]
    rstd2 = k.sb["rstd2"]; nmr2 = k.sb["nmr2"]
    sigma1 = k.sb["sigma1"]
    tcol = k.sb["colscr"]
    scr = k.sb["scr"]; scr2 = k.sb["scr2"]; scr3 = k.sb["scr3"]; scr4 = k.sb["scr4"]
    PS = k.psN
    xq_bf = ar[:, 0:16384].bitcast(BF16)
    FT = getattr(k, "ftrim", 99)

    for qq in range(4):
        fmA = ar[:, 16384:24576]
        fmB = ar[:, 24576:32768]

        for tl in range(4):
            t = qq * 4 + tl

            @block.sync
            def _(sy, t=t):
                sy.wait_ge(fV.h, fV.val)
                sy.wait_ge(fS.h, fS.val)
                sy.dma_start(out=scr3[:, 0:D],
                             in_=k.dr["x2_hbm"].ap()[ts(t), :]).then_inc(*fD.inc(16))

            @block.scalar
            def _(s, t=t):
                s.wait_ge(fD.h, fD.val)
                s.activation(scr3[:, 0:D], scr3[:, 0:D], AF.Identity,
                             bias=nmr2[:, t:t + 1],
                             scale=rstd2[:, t:t + 1]).then_inc(*fS.inc())

            @block.vector
            def _(v):
                v.wait_ge(fS.h, fS.val)
                v.engine_nop().then_inc(*fV.inc())

            emit_transpose16(k, block, scr3[:, 0:D],
                             lambda c, tl=tl, fmA=fmA: fmA[:, c * 512 + tl * 128:
                                                          c * 512 + (tl + 1) * 128],
                             fT, fV)

        if FT < 2:
            continue
        for c in range(NCH):
            @block.tensor
            def _(tt, c=c):
                tt.wait_ge(fV.h, fV.val)
                tt.matmul(PS[2 + (c % 2)][:, 0:512], h128[:],
                          fmA[:, c * 512:(c + 1) * 512],
                          start=True, stop=True).then_inc(*fT.inc())

            @block.vector
            def _(v, c=c):
                v.wait_ge(fT.h, fT.val)
                v.tensor_copy(fmB[:, c * 512:(c + 1) * 512],
                              PS[2 + (c % 2)][:, 0:512])
                v.engine_nop().then_inc(*fV.inc())

        cur, oth = fmB, fmA
        if FT < 3:
            continue
        for st_ in range(4):
            stp = 1 << st_

            @block.vector
            def _(v, cur=cur, oth=oth, stp=stp):
                v.wait_ge(fV.h, fV.val)
                for c in range(NCH):
                    p = c ^ stp
                    if c < p:
                        v.tensor_add(oth[:, c * 512:(c + 1) * 512],
                                     cur[:, c * 512:(c + 1) * 512],
                                     cur[:, p * 512:(p + 1) * 512])
                    else:
                        v.tensor_sub(oth[:, c * 512:(c + 1) * 512],
                                     cur[:, p * 512:(p + 1) * 512],
                                     cur[:, c * 512:(c + 1) * 512])
                v.engine_nop().then_inc(*fV.inc())
            cur, oth = oth, cur

        if FT < 4:
            continue
        @block.scalar
        def _(s, cur=cur, oth=oth):
            s.wait_ge(fV.h, fV.val)
            for c in range(NCH):
                ins = s.activation(oth[:, c * 512:(c + 1) * 512],
                                   cur[:, c * 512:(c + 1) * 512], AF.Square)
            ins.then_inc(*fS.inc())

        @block.vector
        def _(v, oth=oth):
            v.wait_ge(fS.h, fS.val)
            n = NCH
            while n > 1:
                n //= 2
                for i in range(n):
                    v.tensor_tensor(oth[:, i * 512:(i + 1) * 512],
                                    oth[:, (2 * i) * 512:(2 * i + 1) * 512],
                                    oth[:, (2 * i + 1) * 512:(2 * i + 2) * 512],
                                    ALU.max)
                v.engine_nop().then_inc(*nF.inc())
                v.wait_ge(nF.h, nF.val)
            v.engine_nop().then_inc(*fV.inc())

        @block.tensor
        def _(tt, oth=oth):
            tt.wait_ge(fV.h, fV.val)
            for j in range(4):
                ins = tt.transpose(PS[6][:, ts(j)],
                                   oth[:, j * 128:(j + 1) * 128], ident[:])
                if j == 3:
                    ins.then_inc(*fT.inc())

        @block.vector
        def _(v, qq=qq):
            v.wait_ge(fT.h, fT.val)
            for j in range(4):
                v.reduce_max(tcol[:, j:j + 1], PS[6][:, ts(j)],
                             axis=AX.X).then_inc(*nF.inc())
            v.wait_ge(nF.h, nF.val)
            for j in range(4):
                v.tensor_scalar(tcol[:, j:j + 1], tcol[:, j:j + 1],
                                EPS_FFN * EPS_FFN, None, ALU.max)
            v.engine_nop().then_inc(*fV.inc())

        @block.scalar
        def _(s):
            s.wait_ge(fV.h, fV.val)
            s.activation(tcol[:, 0:4], tcol[:, 0:4],
                         AF.Sqrt).then_inc(*fS.inc())

        @block.vector
        def _(v, qq=qq):
            v.wait_ge(fS.h, fS.val)
            for j in range(4):
                v.tensor_scalar(sigma1[:, qq * 4 + j:qq * 4 + j + 1],
                                tcol[:, j:j + 1], 1.0 / (127.0 * SQ2048), None,
                                ALU.mult)
                v.reciprocal(tcol[:, 4 + j:5 + j], tcol[:, j:j + 1])
            v.engine_nop().then_inc(*nF.inc())
            v.wait_ge(nF.h, nF.val)
            for j in range(4):
                v.tensor_scalar(tcol[:, 4 + j:5 + j], tcol[:, 4 + j:5 + j],
                                127.0, None, ALU.mult)
            v.engine_nop().then_inc(*fV.inc())

        if FT < 5:
            continue
        @block.tensor
        def _(tt):
            tt.wait_ge(fV.h, fV.val)
            for j in range(4):
                ins = tt.transpose(PS[7][0:1, j * 128:(j + 1) * 128],
                                   tcol[:, 4 + j:5 + j], ident[:])
                if j == 3:
                    ins.then_inc(*fT.inc())

        @block.vector
        def _(v):
            v.wait_ge(fT.h, fT.val)
            v.tensor_copy(scr[0:1, 0:512], PS[7][0:1, 0:512])
            v.engine_nop().then_inc(*fV.inc())

        @block.tensor
        def _(tt):
            tt.wait_ge(fV.h, fV.val)
            tt.matmul(PS[6][:, 0:512], e0row[:], scr[0:1, 0:512].to_broadcast(
                [128, 512]) if False else scr[0:128, 0:512],
                start=True, stop=True).then_inc(*fT.inc())

        @block.vector
        def _(v):
            v.wait_ge(fT.h, fT.val)
            v.tensor_copy(scr2[:, 0:512], PS[6][:, 0:512])
            v.engine_nop().then_inc(*fV.inc())

        if FT < 6:
            continue
        for c in range(NCH):
            @block.vector
            def _(v, c=c, cur=cur, oth=oth):
                v.wait_ge(fS.h, fS.val)
                z = oth[:, 512:1024]
                w = scr3[:, 0:512]
                b = scr3[:, 512:1024]
                v.tensor_mul(z, cur[:, c * 512:(c + 1) * 512],
                             scr2[:, 0:512]).then_inc(*nF.inc())
                v.wait_ge(nF.h, nF.val)
                peel_round(v, z, w, b, 512)
                v.tensor_sub(w, z, w)
                v.engine_nop().then_inc(*fV.inc())

            @block.scalar
            def _(s, c=c, qq=qq):
                s.wait_ge(fV.h, fV.val)
                s.activation(xq_bf[:, c * 2048 + qq * 512:c * 2048 + (qq + 1) * 512],
                             scr3[:, 0:512], AF.Identity,
                             bias=-128.0).then_inc(*fS.inc())


def emit_finish_dev(k, block):
    fin = k.sem("fin")
    last = k.last_sem
    full = hasattr(k, "ffn_y_done")

    if not full:
        @block.sync
        def _(sy):
            sy.wait_ge(last.h, last.val)
            sy.dma_start(out=k.dr["y"].ap()[0:128, :],
                         in_=k.sb["wmask"][:]).then_inc(*fin.inc(16))

    @block.gpsimd
    def _(g):
        if full:
            g.wait_ge(k.fW.h, k.ffn_y_done)
        else:
            g.wait_ge(fin.h, fin.val)
        if hasattr(k, "fin_dA"):
            g.wait_ge(k.fin_dA.h, k.fin_dA.val)
        if hasattr(k, "fin2_D"):
            g.wait_ge(k.fin2_D.h, k.fin2_D.val)
        if hasattr(k, "fin2_W"):
            g.wait_ge(k.fin2_W.h, k.fin2_W.val)
        if k.dtap.val:
            g.wait_ge(k.dtap.h, k.dtap.val)


def build_program(upto="compress", taps=(), attn_mode="full", ffn_mode="full"):
    nc = bass.Bass("TRN2", target_bir_lowering=False, detect_race_conditions=False)
    ctx = ExitStack()
    with ctx:
        k = K(nc, ctx)
        k.upto = upto
        k.tapset = set(taps)
        k.attn_mode = attn_mode
        k.ffn_mode = ffn_mode
        k.dram("x", [T, D], F32, kind="ExternalInput")
        k.dram("h128_in", [128, 128], F32, kind="ExternalInput")
        for n in WEIGHT_NAMES:
            k.dram(n, INPUT_SHAPES[n], F32, kind="ExternalInput")
        k.dram("y", [T, D], F32, kind="ExternalOutput")
        fm = getattr(k, "ffn_mode", "full")
        dbg = "ExternalOutput" if (upto == "attn" or fm != "full") else "Internal"
        k.dram("x2_hbm", [T, D], F32, kind=dbg)
        k.dram("grp_hbm", [D, T], F32, kind=dbg)
        k.dram("u_hbm", [IF, T], F32,
               kind="ExternalOutput" if fm == "gu" else "Internal")
        k.dram("xq2_hbm", [8, IF, 256], BF16,
               kind="ExternalOutput" if fm == "p2" else "Internal")
        k.dram("tern_g", [NCH, 16, 128, 512], BF16)
        k.dram("tern_u", [NCH, 16, 128, 512], BF16)
        k.dram("tern_d", [64, 4, 128, 512], BF16)

        k.arena = k.sbuf("arena", [128, ARENA], F32)
        k.sbuf("ident", [128, 128], F32)
        k.sbuf("h128", [128, 128], F32)
        k.sbuf("cos_tm", [128, NT * 32], F32)
        k.sbuf("sin_tm", [128, NT * 32], F32)
        k.sbuf("wmask", [128, 4 * 512], F32)
        k.sbuf("e0row", [128, 128], F32)
        k.sbuf("bcsrc", [128, 256], F32)
        k.sbuf("wqs_b", [128, 128], F32)
        k.sbuf("wk_b", [128, 128], F32)
        k.sbuf("nmean", [128, NT], F32)
        k.sbuf("rstd1", [128, NT], F32)
        k.sbuf("nmr1", [128, NT], F32)
        k.sbuf("rstd2", [128, NT], F32)
        k.sbuf("nmr2", [128, NT], F32)
        k.sbuf("k_fm", [128, (NT + 2) * 128], F32)
        k.sbuf("v_tm", [128, (NT + 2) * 128], F32)
        k.sbuf("kcompT", [128, 128], F32)
        k.sbuf("kcompT2", [128, 256], F32)
        k.sbuf("ckv_raw", [128, 128], F32)
        k.sbuf("ckv_V", [128, 128], F32)
        k.sbuf("ckv_K", [128, 128], F32)
        k.sbuf("qi_fm", [128, 2 * T], F32)
        k.sbuf("wih_tm", [128, NT * 4], F32)
        k.sbuf("sigma1", [128, NT], F32)
        k.sbuf("sigma2", [128, NT], F32)
        k.sbuf("scr", [128, 1024], F32)
        k.sbuf("scr2", [128, 1024], F32)
        k.sbuf("scr3", [128, 2048], F32)
        k.sbuf("scr4", [128, 2048], F32)
        k.sbuf("colscr", [128, 32], F32)
        cpool = k.sbuf("cpool", [128, 8], F32)
        k.constvals = [EPS_LN, EPS_RMS, -128.0, 1e30, 0.5, -1.0]
        for i, cv in enumerate(k.constvals):
            nc.const_aps.register(F32, float(cv), cpool[:, i:i + 1])
        k.psN = [ctx.enter_context(nc.psum_tensor(f"PS{i}", [128, 512], F32))
                 for i in range(8)]

        with nc.Block() as block:
            k.gsem = k.sem("g_prog")
            k.dtap = k.sem("d_tap")
            emit_constants(k, block)
            k.last_sem = k.c_sV
            if upto != "const":
                emit_prepass(k, block)
                k.last_sem = k.p_pV
            if upto not in ("const", "prepass"):
                emit_compress(k, block)
                k.last_sem = k.q_pV
            if upto in ("attn", "all"):
                emit_attention(k, block)
                k.last_sem = k.fin_aV
            if upto == "all":
                emit_ffn(k, block)
                k.last_sem = k.fin2_V
            ARENA_TAPS = {"kvcc": (A_KVCC, T, 128), "kvcz": (A_KVCZ, T, 128),
                          "idxc": (A_IDXC, T, 64), "idxz": (A_IDXZ, T, 64),
                          "qic": (A_QIC, T, 64), "xfm": (A_XFM, D, 128),
                          "xln": (A_XLN, D, 128)}
            for name in list(k.tapset):
                if name in k.sb:
                    k.tap(block, name, k.sb[name][:], wait=k.last_sem)
                elif name in ARENA_TAPS:
                    off, w, rows = ARENA_TAPS[name]
                    k.tap(block, name, k.arena[0:rows, off:off + w],
                          wait=k.last_sem)
            emit_finish_dev(k, block)
    return nc, k


def hadamard128():
    i = np.arange(128, dtype=np.uint32)
    a = i[:, None] & i[None, :]
    pc = np.zeros_like(a)
    for b in range(7):
        pc ^= (a >> b) & 1
    return np.where(pc == 0, 1.0, -1.0).astype(np.float32)


def kernel(**inputs):
    xfull = np.ascontiguousarray(np.asarray(inputs["L0_attn_h"], np.float32))
    weights = {n: np.ascontiguousarray(np.asarray(inputs[n], np.float32))
               for n in WEIGHT_NAMES}
    for n in ("L0_ffn_w_gate", "L0_ffn_w_up", "L0_ffn_w_down"):
        weights[n] = np.ascontiguousarray(weights[n].T)
    weights["h128_in"] = hadamard128()
    nc, _ = build_program(upto="all")
    in_maps = [dict(weights, x=np.ascontiguousarray(xfull[i])) for i in range(8)]
    res = run_bass_kernel_spmd(nc, in_maps, core_ids=list(range(8)))
    return np.stack([r["y"] for r in res.results], axis=0)


# version 04
# speedup vs baseline: 1.0457x; 1.0457x over previous
import math
from contextlib import ExitStack

import numpy as np

import concourse.bass as bass
import concourse.mybir as mybir
from concourse.bass_utils import run_bass_kernel_spmd

F32 = mybir.dt.float32
BF16 = mybir.dt.bfloat16
AF = mybir.ActivationFunctionType
ALU = mybir.AluOpType
AX = mybir.AxisListType

T = 2048
D = 2048
H = 16
HD = 128
QC = 512
M = 16
NB = T // M
NIH, CI = 4, 64
G, DG = 4, 512
WIN = 256
EPS_RMS = 1e-6
EPS_LN = 1e-5
EPS_FFN = 1e-5
NEG = -1e30
SENT = -2e30
SCALE = 1.0 / math.sqrt(HD)
IF = 8192
NT = T // 128
NCH = D // 128

INPUT_SHAPES = {
    "x": [T, D],
    "L0_attn_w_qc": [D, QC], "L0_attn_w_qup": [QC, D],
    "L0_attn_kvc_w": [D, HD], "L0_attn_kvc_wz": [D, HD],
    "L0_attn_kvc_bias": [M, HD], "L0_attn_k_proj_w": [D, HD],
    "L0_attn_v_proj_w": [D, HD], "L0_attn_idx_c_w": [D, CI],
    "L0_attn_idx_c_wz": [D, CI], "L0_attn_idx_c_bias": [M, CI],
    "L0_attn_w_dq": [D, CI], "L0_attn_w_iuq": [CI, NIH * CI],
    "L0_attn_w_w": [D, NIH], "L0_attn_q_norm_w": [HD],
    "L0_attn_k_norm_w": [HD], "L0_attn_group_w": [G, 512, DG],
    "L0_attn_final_w": [G * DG, D],
    "L0_ffn_w_gate": [D, IF], "L0_ffn_w_up": [D, IF],
    "L0_ffn_w_down": [IF, D],
}
WEIGHT_NAMES = [n for n in INPUT_SHAPES if n != "x"]

ARENA = 32768
TRIM = 99
PTRIM = 99
PTILES = NT


def ts(i, n=128):
    return slice(i * n, (i + 1) * n)


class Sem:
    def __init__(self, handle):
        self.h = handle
        self.val = 0

    def inc(self, n=1):
        self.val += n
        return self.h, n


class K:
    def __init__(self, nc, ctx):
        self.nc = nc
        self.ctx = ctx
        self.sb = {}
        self.dr = {}
        self.taps = {}

    def sbuf(self, name, shape, dtype=F32):
        h = self.ctx.enter_context(self.nc.sbuf_tensor(name, shape, dtype))
        self.sb[name] = h
        return h

    def dram(self, name, shape, dtype=F32, kind="Internal"):
        h = self.nc.dram_tensor(name, shape, dtype, kind=kind)
        self.dr[name] = h
        return h

    def sem(self, name):
        return Sem(self.ctx.enter_context(self.nc.semaphore(name)))

    def tap(self, block, name, src_ap, wait=None):
        if name not in self.tapset:
            return
        shape = list(src_ap.shape)
        dh = self.dram("tap_" + name, shape, src_ap.dtype, kind="ExternalOutput")
        self.taps[name] = dh
        dtap = self.dtap

        @block.sync
        def _(sy):
            if wait is not None:
                sy.wait_ge(wait.h, wait.val)
            sy.dma_start(out=dh.ap(), in_=src_ap).then_inc(*dtap.inc(16))


def rope7(v, out, src, cc, ss, tmp, nV):
    v.tensor_mul(tmp[:, 0:32], src[:, 0:32], cc)
    v.tensor_mul(tmp[:, 32:64], src[:, 32:64], ss)
    v.tensor_mul(tmp[:, 64:96], src[:, 0:32], ss)
    v.tensor_mul(tmp[:, 96:128], src[:, 32:64], cc).then_inc(*nV.inc())
    v.wait_ge(nV.h, nV.val)
    v.tensor_sub(out[:, 0:32], tmp[:, 0:32], tmp[:, 32:64])
    v.tensor_add(out[:, 32:64], tmp[:, 64:96], tmp[:, 96:128])
    v.tensor_copy(out[:, 64:128], src[:, 64:128])


def peel_round(v, z, w, b, width):
    v.tensor_scalar(z, z, 128.5, None, ALU.add)
    src = z
    for kk in range(7, -1, -1):
        v.tensor_scalar(b, src, float(2 ** kk), None, ALU.is_ge)
        v.scalar_tensor_tensor(w, b, float(-(2 ** kk)), src, ALU.mult, ALU.add)
        src = w



def emit_constants(k, block):
    ident = k.sb["ident"]; h128 = k.sb["h128"]
    cos_tm = k.sb["cos_tm"]; sin_tm = k.sb["sin_tm"]
    wmask = k.sb["wmask"]
    wqs_b = k.sb["wqs_b"]; wk_b = k.sb["wk_b"]
    e0row = k.sb["e0row"]; bcsrc = k.sb["bcsrc"]
    scr = k.sb["scr"]; scr2 = k.sb["scr2"]
    scr3 = k.sb["scr3"]; scr4 = k.sb["scr4"]

    sG = k.sem("c_g"); sV = k.sem("c_v"); sS = k.sem("c_s")
    sT = k.sem("c_t"); sD = k.sem("c_d")

    @block.sync
    def _(sy):
        sy.dma_start(out=h128[:], in_=k.dr["h128_in"].ap()).then_inc(*sD.inc(16))
        sy.wait_ge(sG.h, 1)
        sy.dma_start(out=bcsrc[0:1, 0:128],
                     in_=k.dr["L0_attn_q_norm_w"].ap()[None, :]).then_inc(*sD.inc(16))
        sy.dma_start(out=bcsrc[0:1, 128:256],
                     in_=k.dr["L0_attn_k_norm_w"].ap()[None, :]).then_inc(*sD.inc(16))

    if TRIM < 2:
        k.c_sV = sD
        return

    @block.gpsimd
    def _(g):
        def step(mk):
            g.wait_ge(sG.h, sG.val)
            mk().then_inc(*sG.inc())
        step(lambda: g.memset(bcsrc[:], 0.0))
        step(lambda: g.memset(e0row[:], 1.0))
        step(lambda: g.affine_select(out=e0row[:], in_=e0row[:],
                                     pattern=[[0, 128]],
                                     compare_op=ALU.is_equal, fill=0.0,
                                     base=0, channel_multiplier=1))
        step(lambda: g.memset(ident[:], 1.0))
        step(lambda: g.affine_select(out=ident[:], in_=ident[:], pattern=[[-1, 128]],
                             compare_op=ALU.is_equal, fill=0.0, base=0,
                             channel_multiplier=1))
        for par in range(2):
            for w0 in range(2):
                dst = wmask[:, (par * 2 + w0) * 512:(par * 2 + w0 + 1) * 512]
                step(lambda: g.memset(scr2[:, 0:512], 1.0))
                step(lambda: g.affine_select(out=scr2[:, 0:512], in_=scr2[:, 0:512],
                                     pattern=[[1, 512]], compare_op=ALU.is_ge,
                                     fill=0.0, base=-128 * par - 1,
                                     channel_multiplier=-1))
                step(lambda: g.affine_select(out=dst, in_=scr2[:, 0:512],
                                     pattern=[[-1, 512]], compare_op=ALU.is_ge,
                                     fill=0.0, base=128 * par + 256,
                                     channel_multiplier=1))
                if w0:
                    step(lambda: g.affine_select(out=dst, in_=dst,
                                         pattern=[[1, 512]],
                                         compare_op=ALU.is_ge,
                                         fill=0.0, base=-256,
                                         channel_multiplier=0))
        for i, cv in enumerate(k.constvals):
            step(lambda: g.memset(k.sb["cpool"][:, i:i + 1], float(cv)))


    @block.vector
    def _(v):
        v.wait_ge(sG.h, sG.val)
        v.engine_nop().then_inc(*sV.inc())

    if TRIM < 3:
        k.c_sV = sV
        return

    LNT32 = math.log(10000.0) / 32.0
    C_HI = float(np.float32(LNT32)); C_LO = float(LNT32 - C_HI)
    TWO_PI = 2.0 * math.pi
    I2P_HI = float(np.float32(1.0 / TWO_PI))
    I2P_LO = float(1.0 / TWO_PI - I2P_HI)
    CW1 = float(np.float32(TWO_PI)); CW2 = float(np.float32(TWO_PI - CW1))
    CW3 = float(TWO_PI - CW1 - CW2)

    @block.gpsimd
    def _(g):
        g.wait_ge(sG.h, sG.val)
        g.iota(scr3[:, 0:512], pattern=[[0, 16], [1, 32]], base=0,
               channel_multiplier=0,
               allow_small_or_imprecise_dtypes=True).then_inc(*sG.inc())
        g.wait_ge(sG.h, sG.val)
        g.iota(scr3[:, 512:1024], pattern=[[128, 16], [0, 32]], base=0,
               channel_multiplier=1,
               allow_small_or_imprecise_dtypes=True).then_inc(*sG.inc())

    @block.vector
    def _(v):
        v.wait_ge(sG.h, sG.val)
        v.tensor_scalar(scr3[:, 1024:1536], scr3[:, 0:512], -C_HI, None, ALU.mult)
        v.scalar_tensor_tensor(scr3[:, 1024:1536], scr3[:, 0:512], -C_LO,
                               scr3[:, 1024:1536], ALU.mult, ALU.add)
        v.engine_nop().then_inc(*sV.inc())

    @block.scalar
    def _(s):
        s.wait_ge(sV.h, sV.val)
        s.activation(scr3[:, 0:512], scr3[:, 1024:1536], AF.Exp).then_inc(*sS.inc())

    @block.vector
    def _(v):
        v.wait_ge(sS.h, sS.val)
        a = scr3[:, 1024:1536]
        u = scr4[:, 0:512]
        w = scr4[:, 512:1024]
        b = scr4[:, 1024:1536]
        kk = scr3[:, 1536:2048]
        v.tensor_mul(a, scr3[:, 0:512], scr3[:, 512:1024])
        v.tensor_scalar(u, a, I2P_HI, None, ALU.mult)
        v.scalar_tensor_tensor(u, a, I2P_LO, u, ALU.mult, ALU.add)
        v.tensor_scalar(w, u, 512.5, None, ALU.add)
        for p2 in range(9, -1, -1):
            v.tensor_scalar(b, w, float(2 ** p2), None, ALU.is_ge)
            v.scalar_tensor_tensor(w, b, float(-(2 ** p2)), w, ALU.mult, ALU.add)
        v.tensor_scalar(b, u, 512.5, None, ALU.add)
        v.tensor_sub(kk, b, w)
        v.tensor_scalar(kk, kk, 512.0, None, ALU.subtract)
        v.scalar_tensor_tensor(w, kk, -CW1, a, ALU.mult, ALU.add)
        v.scalar_tensor_tensor(w, kk, -CW2, w, ALU.mult, ALU.add)
        v.scalar_tensor_tensor(w, kk, -CW3, w, ALU.mult, ALU.add)
        v.tensor_scalar(b, w, float(math.pi), None, ALU.is_gt)
        v.scalar_tensor_tensor(w, b, -TWO_PI, w, ALU.mult, ALU.add)
        v.tensor_scalar(b, w, float(-math.pi), None, ALU.is_lt)
        v.scalar_tensor_tensor(w, b, TWO_PI, w, ALU.mult, ALU.add)
        v.engine_nop().then_inc(*sV.inc())

    @block.scalar
    def _(s):
        s.wait_ge(sV.h, sV.val)
        s.activation(sin_tm[:], scr4[:, 512:1024], AF.Sin).then_inc(*sS.inc())

    @block.vector
    def _(v):
        v.wait_ge(sS.h, sS.val)
        w = scr4[:, 512:1024]; b = scr4[:, 1024:1536]
        v.tensor_scalar(w, w, float(math.pi / 2), None, ALU.add)
        v.tensor_scalar(b, w, float(math.pi), None, ALU.is_gt)
        v.scalar_tensor_tensor(w, b, -TWO_PI, w, ALU.mult, ALU.add)
        v.engine_nop().then_inc(*sV.inc())

    @block.scalar
    def _(s):
        s.wait_ge(sV.h, sV.val)
        s.activation(cos_tm[:], scr4[:, 512:1024], AF.Sin).then_inc(*sS.inc())

    ps = k.psN[0]

    @block.tensor
    def _(tt):
        tt.wait_ge(sD.h, sD.val)
        tt.wait_ge(sV.h, sV.val)
        tt.matmul(ps[:, 0:128], e0row[:], bcsrc[:, 0:128],
                  start=True, stop=True)
        tt.matmul(ps[:, 128:256], e0row[:], bcsrc[:, 128:256],
                  start=True, stop=True).then_inc(*sT.inc())

    @block.vector
    def _(v):
        v.wait_ge(sT.h, sT.val)
        v.tensor_scalar(wqs_b[:], ps[:, 0:128], SCALE, None, ALU.mult)
        v.tensor_copy(wk_b[:], ps[:, 128:256])
        v.engine_nop().then_inc(*sV.inc())

    k.c_sV = sV



CAT = [("L0_attn_k_proj_w", 0, 128), ("L0_attn_v_proj_w", 128, 128),
       ("L0_attn_kvc_w", 256, 128), ("L0_attn_kvc_wz", 384, 128),
       ("L0_attn_idx_c_w", 512, 64), ("L0_attn_idx_c_wz", 576, 64),
       ("L0_attn_w_dq", 640, 64), ("L0_attn_w_w", 704, 4)]
CAT_W = 708

A_WCAT = 0
A_XB = 11328
A_SQ = 15424
A_XLN = 17472
A_XFM = 19520
A_KVCC = 21568
A_KVCZ = 23616
A_IDXC = 25664
A_IDXZ = 27712
A_QIC = 29760


def emit_ln_tile(k, block, xb, t, stats, sems, sqb):
    nmean, rstd, nmr = stats
    pV, pS = sems[0], sems[1]
    tcol = k.sb["colscr"]

    @block.vector
    def _(v):
        v.reduce_sum(tcol[:, 0:1], xb, axis=AX.X)
        v.engine_nop().then_inc(*pV.inc())

    @block.scalar
    def _(s):
        s.wait_ge(pV.h, pV.val)
        s.activation(sqb, xb, AF.Square).then_inc(*pS.inc())

    @block.vector
    def _(v):
        nv = sems[2]
        v.wait_ge(pS.h, pS.val)
        v.reduce_sum(tcol[:, 1:2], sqb, axis=AX.X).then_inc(*nv.inc())
        v.wait_ge(nv.h, nv.val)
        v.tensor_scalar(nmean[:, t:t + 1], tcol[:, 0:1], -1.0 / D, None,
                        ALU.mult).then_inc(*nv.inc())
        v.wait_ge(nv.h, nv.val)
        v.tensor_mul(tcol[:, 2:3], nmean[:, t:t + 1],
                     nmean[:, t:t + 1]).then_inc(*nv.inc())
        v.wait_ge(nv.h, nv.val)
        v.scalar_tensor_tensor(tcol[:, 3:4], tcol[:, 1:2], 1.0 / D,
                               tcol[:, 2:3], ALU.mult,
                               ALU.subtract).then_inc(*nv.inc())
        v.wait_ge(nv.h, nv.val)
        v.tensor_scalar(tcol[:, 3:4], tcol[:, 3:4], EPS_LN, None,
                        ALU.add).then_inc(*nv.inc())
        v.wait_ge(nv.h, nv.val)
        v.reciprocal(tcol[:, 3:4], tcol[:, 3:4])
        v.engine_nop().then_inc(*pV.inc())

    @block.scalar
    def _(s):
        s.wait_ge(pV.h, pV.val)
        s.activation(rstd[:, t:t + 1], tcol[:, 3:4], AF.Sqrt).then_inc(*pS.inc())

    @block.vector
    def _(v):
        v.wait_ge(pS.h, pS.val)
        v.tensor_mul(nmr[:, t:t + 1], nmean[:, t:t + 1], rstd[:, t:t + 1])
        v.engine_nop().then_inc(*pV.inc())


def emit_transpose16(k, block, src, dstcols, pT, pV, evac_cb=None):
    PS = k.psN
    for grp in range(4):
        pb = PS[6 + (grp % 2)]

        @block.tensor
        def _(tt, grp=grp, pb=pb):
            tt.wait_ge(pV.h, pV.val)
            for j in range(4):
                c = grp * 4 + j
                tt.transpose(pb[:, ts(j)], src[:, ts(c)], k.sb["ident"][:]).then_inc(*pT.inc())

        @block.vector
        def _(v, grp=grp, pb=pb):
            v.wait_ge(pT.h, pT.val)
            for j in range(4):
                v.tensor_copy(dstcols(grp * 4 + j), pb[:, ts(j)])
            v.engine_nop().then_inc(*pV.inc())


def emit_prepass(k, block):
    ar = k.arena
    ident = k.sb["ident"]; wk_b = k.sb["wk_b"]
    cos_tm = k.sb["cos_tm"]; sin_tm = k.sb["sin_tm"]
    k_fm = k.sb["k_fm"]; v_tm = k.sb["v_tm"]
    nmean = k.sb["nmean"]; rstd1 = k.sb["rstd1"]; nmr1 = k.sb["nmr1"]
    wih_tm = k.sb["wih_tm"]; qi_fm = k.sb["qi_fm"]
    tcol = k.sb["colscr"]; kscr = k.sb["scr"]; stg = k.sb["scr3"]
    PS = k.psN
    x = k.dr["x"]

    wcat = ar[:, A_WCAT:A_WCAT + NCH * CAT_W]
    sqb = ar[:, A_SQ:A_SQ + D]
    xlnb = ar[:, A_XLN:A_XLN + D]
    xfm = ar[:, A_XFM:A_XFM + D]
    kvcc = ar[:, A_KVCC:A_KVCC + T]
    kvcz = ar[:, A_KVCZ:A_KVCZ + T]
    idxc = ar[:, A_IDXC:A_IDXC + T]
    idxz = ar[:, A_IDXZ:A_IDXZ + T]
    qicb = ar[:, A_QIC:A_QIC + T]

    dX = k.sem("p_dx"); dW = k.sem("p_dw")
    pV = k.sem("p_v"); pS = k.sem("p_s"); pT = k.sem("p_t")
    nV = k.sem("p_nv")
    k.p_pV = pV

    @block.sync
    def _(sy):
        for name, off, w in CAT:
            for c in range(NCH):
                sy.dma_start(out=wcat[:, c * CAT_W + off:c * CAT_W + off + w],
                             in_=k.dr[name].ap()[ts(c), :]).then_inc(*dW.inc(16))
        sy.dma_start(out=ar[:, A_XB:A_XB + D],
                     in_=x.ap()[0:128, :]).then_inc(*dX.inc(16))

    @block.gpsimd
    def _(g):
        g.memset(k_fm[:, 0:256], 0.0).then_inc(*k.gsem.inc())
        g.memset(v_tm[:, 0:256], 0.0).then_inc(*k.gsem.inc())

    for t in range(PTILES):
        xb = ar[:, A_XB + (t % 2) * D:A_XB + (t % 2 + 1) * D]

        @block.sync
        def _(sy, t=t):
            if t + 1 < NT:
                sy.wait_ge(pS.h, pS.val)
                dst = ar[:, A_XB + ((t + 1) % 2) * D:A_XB + ((t + 1) % 2 + 1) * D]
                sy.dma_start(out=dst, in_=x.ap()[ts(t + 1), :]).then_inc(*dX.inc(16))

        @block.vector
        def _(v, t=t):
            v.wait_ge(dX.h, 16 * (t + 1))
            if t == 0:
                v.wait_ge(k.c_sV.h, k.c_sV.val)
            v.engine_nop().then_inc(*pV.inc())

        emit_ln_tile(k, block, xb, t, (nmean, rstd1, nmr1), (pV, pS, nV), sqb)

        @block.scalar
        def _(s, t=t, xb=xb):
            s.wait_ge(pV.h, pV.val)
            s.activation(xlnb, xb, AF.Identity, bias=nmr1[:, t:t + 1],
                         scale=rstd1[:, t:t + 1]).then_inc(*pS.inc())

        @block.vector
        def _(v):
            v.wait_ge(pS.h, pS.val)
            v.engine_nop().then_inc(*pV.inc())

        if PTRIM < 2:
            continue
        emit_transpose16(k, block, xlnb, lambda c: xfm[:, ts(c)], pT, pV)
        if PTRIM < 3:
            continue

        @block.tensor
        def _(tt, t=t):
            tt.wait_ge(pV.h, pV.val)
            tt.wait_ge(dW.h, dW.val)
            if t > 0:
                tt.wait_ge(pS.h, pS.val)
            for c in range(NCH):
                tt.matmul(PS[0][:, 0:512], xfm[:, ts(c)],
                          wcat[:, c * CAT_W:c * CAT_W + 512],
                          start=(c == 0), stop=(c == NCH - 1))
                tt.matmul(PS[1][:, 0:CAT_W - 512], xfm[:, ts(c)],
                          wcat[:, c * CAT_W + 512:(c + 1) * CAT_W],
                          start=(c == 0), stop=(c == NCH - 1)).then_inc(*pT.inc())

        if PTRIM < 4:
            continue
        @block.vector
        def _(v, t=t):
            v.wait_ge(pT.h, pT.val)
            v.tensor_copy(stg[:, 0:128], PS[0][:, 0:128])
            v.tensor_copy(v_tm[:, (t + 2) * 128:(t + 3) * 128], PS[0][:, 128:256])
            v.tensor_copy(stg[:, 128:384], PS[0][:, 256:512])
            v.engine_nop().then_inc(*pV.inc())

        @block.scalar
        def _(s, t=t):
            s.wait_ge(pT.h, pT.val)
            s.activation(stg[:, 384:580], PS[1][:, 0:196], AF.Identity)
            s.activation(kscr[:, 0:128], PS[0][:, 0:128], AF.Square).then_inc(*pS.inc())

        @block.vector
        def _(v, t=t):
            v.wait_ge(pS.h, pS.val)
            v.reduce_sum(tcol[:, 4:5], kscr[:, 0:128], axis=AX.X).then_inc(*nV.inc())
            v.wait_ge(nV.h, nV.val)
            v.tensor_scalar(tcol[:, 4:5], tcol[:, 4:5], 1.0 / HD, EPS_RMS,
                            ALU.mult, ALU.add).then_inc(*nV.inc())
            v.wait_ge(nV.h, nV.val)
            v.reciprocal(tcol[:, 4:5], tcol[:, 4:5])
            v.tensor_copy(wih_tm[:, t * 4:(t + 1) * 4], stg[:, 576:580])
            v.engine_nop().then_inc(*pV.inc())

        @block.scalar
        def _(s, t=t):
            s.wait_ge(pV.h, pV.val)
            s.activation(tcol[:, 5:6], tcol[:, 4:5], AF.Sqrt).then_inc(*pS.inc())

        @block.vector
        def _(v, t=t):
            v.wait_ge(pS.h, pS.val)
            kn = kscr[:, 128:256]; ko = kscr[:, 256:384]
            v.tensor_scalar(kn, stg[:, 0:128], tcol[:, 5:6], None, ALU.mult)
            v.tensor_mul(kn, kn, wk_b[:])
            rope7(v, ko, kn, cos_tm[:, t * 32:(t + 1) * 32],
                  sin_tm[:, t * 32:(t + 1) * 32], kscr[:, 384:512], nV)
            v.engine_nop().then_inc(*pV.inc())

        if PTRIM < 5:
            continue
        @block.tensor
        def _(tt, t=t):
            tt.wait_ge(pV.h, pV.val)
            tt.transpose(PS[6][:, 0:128], kscr[:, 256:384], ident[:])
            tt.transpose(PS[6][:, 128:256], stg[:, 128:256], ident[:])
            tt.transpose(PS[6][:, 256:384], stg[:, 256:384], ident[:])
            tt.transpose(PS[7][0:64, 0:128], stg[:, 384:448], ident[:])
            tt.transpose(PS[7][0:64, 128:256], stg[:, 448:512], ident[:])
            tt.transpose(PS[7][0:64, 256:384], stg[:, 512:576],
                         ident[:]).then_inc(*pT.inc())

        @block.vector
        def _(v, t=t):
            v.wait_ge(pT.h, pT.val)
            v.tensor_copy(k_fm[:, (t + 2) * 128:(t + 3) * 128], PS[6][:, 0:128])
            v.tensor_copy(kvcc[:, ts(t)], PS[6][:, 128:256])
            v.tensor_copy(kvcz[:, ts(t)], PS[6][:, 256:384])
            v.tensor_copy(idxc[0:64, ts(t)], PS[7][0:64, 0:128])
            v.tensor_copy(idxz[0:64, ts(t)], PS[7][0:64, 128:256])
            v.tensor_copy(qicb[0:64, ts(t)], PS[7][0:64, 256:384])
            v.engine_nop().then_inc(*pV.inc())


    if PTRIM < 6:
        return
    scr2 = k.sb["scr2"]

    @block.sync
    def _(sy):
        sy.wait_ge(k.c_sV.h, k.c_sV.val)
        sy.dma_start(out=scr2[0:64, 0:256],
                     in_=k.dr["L0_attn_w_iuq"].ap()).then_inc(*dW.inc(16))

    for hh in range(2):
        for g4 in range(4):
            pb = PS[2 + (g4 % 2)]

            @block.tensor
            def _(tt, hh=hh, g4=g4, pb=pb):
                tt.wait_ge(dW.h, dW.val)
                tt.wait_ge(pV.h, pV.val)
                tt.matmul(pb[:, 0:512], scr2[0:64, hh * 128:(hh + 1) * 128],
                          qicb[0:64, g4 * 512:(g4 + 1) * 512],
                          start=True, stop=True).then_inc(*pT.inc())

            @block.vector
            def _(v, hh=hh, g4=g4, pb=pb):
                v.wait_ge(pT.h, pT.val)
                v.tensor_copy(qi_fm[:, hh * T + g4 * 512:hh * T + (g4 + 1) * 512],
                              pb[:, 0:512])
                v.engine_nop().then_inc(*pV.inc())



def emit_compress(k, block):
    ar = k.arena
    ident = k.sb["ident"]
    scr3 = k.sb["scr3"]; scr4 = k.sb["scr4"]; scr = k.sb["scr"]
    scr2 = k.sb["scr2"]
    ckv_raw = k.sb["ckv_raw"]; kcompT = k.sb["kcompT"]
    PS = k.psN
    pV = k.sem("q_v"); pS = k.sem("q_s"); pT = k.sem("q_t")
    dB = k.sem("q_d")

    kvcc = ar[:, A_KVCC:A_KVCC + T]
    kvcz = ar[:, A_KVCZ:A_KVCZ + T]
    idxc = ar[0:64, A_IDXC:A_IDXC + T]
    idxz = ar[0:64, A_IDXZ:A_IDXZ + T]

    @block.sync
    def _(sy):
        sy.wait_ge(k.c_sV.h, k.c_sV.val)
        sy.dma_start(out=scr2[0:16, 288:416],
                     in_=k.dr["L0_attn_kvc_bias"].ap()).then_inc(*dB.inc(16))
        sy.dma_start(out=scr2[0:16, 416:480],
                     in_=k.dr["L0_attn_idx_c_bias"].ap()).then_inc(*dB.inc(16))

    @block.tensor
    def _(tt):
        tt.wait_ge(dB.h, dB.val)
        tt.wait_ge(k.p_pV.h, k.p_pV.val)
        tt.transpose(PS[6][:, 0:16], scr2[0:16, 288:416], ident[0:16, 0:16])
        tt.transpose(PS[6][0:64, 16:32], scr2[0:16, 416:480],
                     ident[0:16, 0:16]).then_inc(*pT.inc())

    @block.vector
    def _(v):
        v.wait_ge(pT.h, pT.val)
        v.tensor_copy(scr2[:, 480:496], PS[6][:, 0:16])
        v.tensor_copy(scr2[0:64, 496:512], PS[6][0:64, 16:32])
        v.engine_nop().then_inc(*pV.inc())

    def softmax_compress(zap, cap, bias_ap, out_ap, rows, tag):
        z3 = zap.rearrange("p (b m) -> p b m", m=M)
        c3 = cap.rearrange("p (b m) -> p b m", m=M)
        zc3 = scr3[0:rows, 0:T].rearrange("p (b m) -> p b m", m=M)
        e3 = scr4[0:rows, 0:T].rearrange("p (b m) -> p b m", m=M)
        ec3 = scr3[0:rows, 0:T].rearrange("p (b m) -> p b m", m=M)
        zmax = scr[0:rows, 288:416]
        esum = scr[0:rows, 416:544]
        secs = scr[0:rows, 544:672]

        @block.vector
        def _(v):
            v.wait_ge(pV.h, pV.val)
            v.tensor_add(z3, z3, bias_ap.unsqueeze(1).to_broadcast(
                [rows, NB, M]))
            v.reduce_max(zmax, z3, axis=AX.X)
            v.tensor_sub(zc3, z3, zmax.unsqueeze(2).to_broadcast([rows, NB, M]))
            v.engine_nop().then_inc(*pV.inc())

        @block.scalar
        def _(s):
            s.wait_ge(pV.h, pV.val)
            s.activation(scr4[0:rows, 0:T], scr3[0:rows, 0:T], AF.Exp).then_inc(*pS.inc())

        @block.vector
        def _(v):
            v.wait_ge(pS.h, pS.val)
            v.reduce_sum(esum, e3, axis=AX.X)
            v.tensor_mul(ec3, e3, c3)
            v.reduce_sum(secs, ec3, axis=AX.X)
            v.reciprocal(esum, esum)
            v.tensor_mul(out_ap, secs, esum)
            v.engine_nop().then_inc(*pV.inc())

    softmax_compress(kvcz, kvcc, scr2[:, 480:496], ckv_raw[:], 128, "kvc")
    softmax_compress(idxz, idxc, scr2[0:64, 496:512], kcompT[0:64, :], 64, "idx")

    kcompT2 = k.sb["kcompT2"]
    dDup = k.sem("q_dup")

    @block.gpsimd
    def _(g):
        g.wait_ge(pV.h, pV.val)
        g.memset(kcompT2[:], 0.0).then_inc(*k.gsem.inc())

    @block.sync
    def _(sy):
        sy.wait_ge(pV.h, pV.val)
        sy.dma_start(out=kcompT[64:128, :], in_=kcompT[0:64, :]).then_inc(*dDup.inc(16))

    @block.vector
    def _(v):
        v.wait_ge(dDup.h, dDup.val)
        v.wait_ge(k.gsem.h, k.gsem.val)
        v.tensor_copy(kcompT2[0:64, 0:128], kcompT[0:64, :])
        v.tensor_copy(kcompT2[64:128, 128:256], kcompT[64:128, :])
        v.engine_nop().then_inc(*pV.inc())

    k.q_pV = pV






A_QST = 0
A_WQC = 16384
A_WQUP = 24576
A_GW = 16384
A_BSC = 24576


def emit_attention(k, block):
    ident = k.sb["ident"]
    wk_b = k.sb["wk_b"]
    ckv_raw = k.sb["ckv_raw"]; ckv_V = k.sb["ckv_V"]; ckv_K = k.sb["ckv_K"]
    tcol = k.sb["colscr"]
    scr = k.sb["scr"]
    PS = k.psN

    aV = k.sem("a_v"); aS = k.sem("a_s"); aT = k.sem("a_t")
    nA = k.sem("a_nv"); dA = k.sem("a_d"); aG = k.sem("a_g")

    @block.tensor
    def _(tt):
        tt.wait_ge(k.q_pV.h, k.q_pV.val)
        tt.transpose(PS[6][:, 0:128], ckv_raw[:], ident[:]).then_inc(*aT.inc())

    @block.vector
    def _(v):
        v.wait_ge(aT.h, aT.val)
        v.tensor_copy(scr[:, 0:128], PS[6][:, 0:128])
        v.engine_nop().then_inc(*aV.inc())

    @block.scalar
    def _(s):
        s.wait_ge(aV.h, aV.val)
        s.activation(scr[:, 128:256], scr[:, 0:128], AF.Square).then_inc(*aS.inc())

    @block.vector
    def _(v):
        v.wait_ge(aS.h, aS.val)
        v.reduce_sum(tcol[:, 8:9], scr[:, 128:256], axis=AX.X).then_inc(*nA.inc())
        v.wait_ge(nA.h, nA.val)
        v.tensor_scalar(tcol[:, 8:9], tcol[:, 8:9], 1.0 / HD, EPS_RMS,
                        ALU.mult, ALU.add).then_inc(*nA.inc())
        v.wait_ge(nA.h, nA.val)
        v.reciprocal(tcol[:, 8:9], tcol[:, 8:9])
        v.engine_nop().then_inc(*aV.inc())

    @block.scalar
    def _(s):
        s.wait_ge(aV.h, aV.val)
        s.activation(tcol[:, 9:10], tcol[:, 8:9], AF.Sqrt).then_inc(*aS.inc())

    @block.vector
    def _(v):
        v.wait_ge(aS.h, aS.val)
        v.tensor_scalar(ckv_V[:], scr[:, 0:128], tcol[:, 9:10], None, ALU.mult)
        v.tensor_mul(ckv_V[:], ckv_V[:], wk_b[:])
        v.engine_nop().then_inc(*aV.inc())

    @block.tensor
    def _(tt):
        tt.wait_ge(aV.h, aV.val)
        tt.transpose(PS[6][:, 128:256], ckv_V[:], ident[:]).then_inc(*aT.inc())

    @block.scalar
    def _(s):
        s.wait_ge(aT.h, aT.val)
        s.activation(ckv_K[:], PS[6][:, 128:256],
                     AF.Identity).then_inc(*aS.inc())

    k.b_done = [k.sem("b_done0"), k.sem("b_done1")]
    mode = getattr(k, "attn_mode", "full")
    if mode == "A0":
        emit_attn_A(k, block, 0, (aV, aS, aT, nA, dA, aG))
        k.fin_aV = aV; k.fin_dA = dA
        return
    if mode == "B0":
        emit_attn_A(k, block, 0, (aV, aS, aT, nA, dA, aG))
        emit_attn_B(k, block, 0, (aV, aS, aT, nA, dA, aG))
        k.fin_aV = aV; k.fin_dA = dA
        return
    for half in range(2):
        emit_attn_A(k, block, half, (aV, aS, aT, nA, dA, aG))
        emit_attn_B(k, block, half, (aV, aS, aT, nA, dA, aG))
    emit_attn_final(k, block, (aV, aS, aT, nA, dA, aG))


def emit_attn_A(k, block, half, sems):
    aV, aS, aT, nA, dA, aG = sems
    ar = k.arena
    ident = k.sb["ident"]
    cos_tm = k.sb["cos_tm"]; sin_tm = k.sb["sin_tm"]
    wqs_b = k.sb["wqs_b"]
    rstd1 = k.sb["rstd1"]; nmr1 = k.sb["nmr1"]
    tcol = k.sb["colscr"]
    scr2 = k.sb["scr2"]; scr3 = k.sb["scr3"]; scr4 = k.sb["scr4"]
    PS = k.psN
    x = k.dr["x"]

    wqc = ar[:, A_WQC:A_WQC + 8192]
    wqup = ar[:, A_WQUP:A_WQUP + 8192]
    xlnb = scr3[:, 0:D]
    xfm = scr4[:, 0:D]

    @block.sync
    def _(sy, half=half):
        if half == 0:
            sy.wait_ge(k.q_pV.h, k.q_pV.val)
        else:
            sy.wait_ge(k.b_done[0].h, k.b_done[0].val)
        for c in range(NCH):
            sy.dma_start(out=wqc[:, c * 512:(c + 1) * 512],
                         in_=k.dr["L0_attn_w_qc"].ap()[ts(c), :]).then_inc(*dA.inc(16))
        for q in range(4):
            sy.dma_start(out=wqup[:, q * 2048:(q + 1) * 2048],
                         in_=k.dr["L0_attn_w_qup"].ap()[ts(q), :]).then_inc(*dA.inc(16))

    for tl in range(8):
        t = half * 8 + tl

        @block.sync
        def _(sy, t=t):
            sy.wait_ge(aV.h, aV.val)
            sy.dma_start(out=k.sb["scr3"][:, 0:D],
                         in_=x.ap()[ts(t), :]).then_inc(*dA.inc(16))

        @block.scalar
        def _(s, t=t):
            s.wait_ge(dA.h, dA.val)
            s.activation(xlnb, xlnb, AF.Identity, bias=nmr1[:, t:t + 1],
                         scale=rstd1[:, t:t + 1]).then_inc(*aS.inc())

        @block.vector
        def _(v):
            v.wait_ge(aS.h, aS.val)
            v.engine_nop().then_inc(*aV.inc())

        emit_transpose16(k, block, xlnb, lambda c: xfm[:, ts(c)], aT, aV)

        @block.tensor
        def _(tt, t=t):
            tt.wait_ge(aV.h, aV.val)
            tt.wait_ge(dA.h, dA.val)
            for c in range(NCH):
                ins = tt.matmul(PS[0][:, 0:512], xfm[:, ts(c)],
                                wqc[:, c * 512:(c + 1) * 512],
                                start=(c == 0), stop=(c == NCH - 1))
                if c == NCH - 1:
                    ins.then_inc(*aT.inc())

        @block.vector
        def _(v):
            v.wait_ge(aT.h, aT.val)
            v.tensor_copy(scr2[:, 0:512], PS[0][:, 0:512])
            v.engine_nop().then_inc(*aV.inc())

        @block.tensor
        def _(tt):
            tt.wait_ge(aV.h, aV.val)
            for j in range(4):
                ins = tt.transpose(PS[6][:, ts(j)],
                                   scr2[:, j * 128:(j + 1) * 128], ident[:])
                if j == 3:
                    ins.then_inc(*aT.inc())

        @block.vector
        def _(v):
            v.wait_ge(aT.h, aT.val)
            v.tensor_copy(scr2[:, 512:1024], PS[6][:, 0:512])
            v.engine_nop().then_inc(*aV.inc())

        @block.tensor
        def _(tt):
            tt.wait_ge(aV.h, aV.val)
            for qc in range(4):
                for g4 in range(4):
                    ins = tt.matmul(
                        PS[2 + g4][:, 0:512],
                        scr2[:, 512 + qc * 128:512 + (qc + 1) * 128],
                        wqup[:, qc * 2048 + g4 * 512:qc * 2048 + (g4 + 1) * 512],
                        start=(qc == 0), stop=(qc == 3))
                    if qc == 3 and g4 == 3:
                        ins.then_inc(*aT.inc())

        @block.vector
        def _(v):
            v.wait_ge(aT.h, aT.val)
            for g4 in range(4):
                v.tensor_copy(scr3[:, g4 * 512:(g4 + 1) * 512],
                              PS[2 + g4][:, 0:512])
            v.engine_nop().then_inc(*aV.inc())

        @block.scalar
        def _(s):
            s.wait_ge(aV.h, aV.val)
            s.activation(scr4[:, 0:D], scr3[:, 0:D], AF.Square).then_inc(*aS.inc())

        @block.vector
        def _(v):
            v.wait_ge(aS.h, aS.val)
            sq3 = scr4[:, 0:D].rearrange("p (h d) -> p h d", d=HD)
            v.reduce_sum(tcol[:, 10:26], sq3, axis=AX.X).then_inc(*nA.inc())
            v.wait_ge(nA.h, nA.val)
            v.tensor_scalar(tcol[:, 10:26], tcol[:, 10:26], 1.0 / HD, EPS_RMS,
                            ALU.mult, ALU.add).then_inc(*nA.inc())
            v.wait_ge(nA.h, nA.val)
            v.reciprocal(tcol[:, 10:26], tcol[:, 10:26])
            v.engine_nop().then_inc(*aV.inc())

        @block.scalar
        def _(s):
            s.wait_ge(aV.h, aV.val)
            s.activation(tcol[:, 10:26], tcol[:, 10:26], AF.Sqrt).then_inc(*aS.inc())

        @block.vector
        def _(v, t=t):
            v.wait_ge(aS.h, aS.val)
            q3 = scr3[:, 0:D].rearrange("p (h d) -> p h d", d=HD)
            o3 = scr4[:, 0:D].rearrange("p (h d) -> p h d", d=HD)
            cc = cos_tm[:, t * 32:(t + 1) * 32].unsqueeze(1).to_broadcast(
                [128, H, 32])
            ss = sin_tm[:, t * 32:(t + 1) * 32].unsqueeze(1).to_broadcast(
                [128, H, 32])
            t3a = scr2[:, 0:512].rearrange("p (h d) -> p h d", d=32)
            t3b = scr2[:, 512:1024].rearrange("p (h d) -> p h d", d=32)
            t3c = k.sb["scr"][:, 0:512].rearrange("p (h d) -> p h d", d=32)
            t3d = k.sb["scr"][:, 512:1024].rearrange("p (h d) -> p h d", d=32)
            v.tensor_mul(t3a, q3[:, :, 0:32], cc)
            v.tensor_mul(t3b, q3[:, :, 32:64], ss)
            v.tensor_mul(t3c, q3[:, :, 0:32], ss)
            v.tensor_mul(t3d, q3[:, :, 32:64], cc).then_inc(*nA.inc())
            v.wait_ge(nA.h, nA.val)
            v.tensor_sub(o3[:, :, 0:32], t3a, t3b)
            v.tensor_add(o3[:, :, 32:64], t3c, t3d)
            v.tensor_copy(o3[:, :, 64:128], q3[:, :, 64:128]).then_inc(*nA.inc())
            v.wait_ge(nA.h, nA.val)
            v.tensor_mul(o3, o3, tcol[:, 10:26].unsqueeze(2).to_broadcast(
                [128, H, HD])).then_inc(*nA.inc())
            v.wait_ge(nA.h, nA.val)
            v.tensor_mul(o3, o3, wqs_b[:].unsqueeze(1).to_broadcast(
                [128, H, HD]))
            v.engine_nop().then_inc(*aV.inc())

        for grp in range(4):
            pb = PS[6 + (grp % 2)]

            @block.tensor
            def _(tt, grp=grp, pb=pb):
                tt.wait_ge(aV.h, aV.val)
                for j in range(4):
                    c = grp * 4 + j
                    ins = tt.transpose(pb[:, ts(j)], scr4[:, ts(c)], ident[:])
                    if j == 3:
                        ins.then_inc(*aT.inc())

            @block.vector
            def _(v, grp=grp, pb=pb, tl=tl):
                v.wait_ge(aT.h, aT.val)
                base = A_QST + tl * 2048
                v.tensor_copy(ar[:, base + grp * 512:base + (grp + 1) * 512],
                              pb[:, 0:512])
                v.engine_nop().then_inc(*aV.inc())


def emit_attn_B(k, block, half, sems):
    b_tiles = getattr(k, "b_tiles", 8)
    b_stage = getattr(k, "b_stage", 4)
    b_skip = getattr(k, "b_skip", set())
    aV, aS, aT, nA, dA, aG = sems
    ar = k.arena
    ident = k.sb["ident"]
    k_fm = k.sb["k_fm"]; v_tm = k.sb["v_tm"]
    kcompT = k.sb["kcompT"]; ckv_V = k.sb["ckv_V"]; ckv_K = k.sb["ckv_K"]
    qi_fm = k.sb["qi_fm"]; wih_tm = k.sb["wih_tm"]
    wmask = k.sb["wmask"]
    tcol = k.sb["colscr"]
    scr = k.sb["scr"]; scr2 = k.sb["scr2"]; scr3 = k.sb["scr3"]
    PS = k.psN

    gw = ar[:, A_GW:A_GW + 8192]
    attnb = ar[:, A_BSC:A_BSC + 2048]
    groupedb = ar[:, A_BSC + 2048:A_BSC + 4096]
    gfm = ar[:, A_BSC + 4096:A_BSC + 6144]

    if "gw" not in b_skip:
        @block.sync
        def _(sy, half=half):
            sy.wait_ge(aV.h, aV.val)
            for g in range(G):
                for ic in range(4):
                    sy.dma_start(
                        out=gw[:, (g * 4 + ic) * 512:(g * 4 + ic + 1) * 512],
                        in_=k.dr["L0_attn_group_w"].ap()[g, ic * 128:(ic + 1) * 128, :]
                    ).then_inc(*dA.inc(16))

    for tl in range(b_tiles):
        t = half * 8 + tl
        win = t // 2
        qbase = A_QST + tl * 2048

        if "mask" not in b_skip:
            @block.gpsimd
            def _(g, t=t):
                g.wait_ge(aV.h, aV.val)
                g.memset(scr[:, 640:768], 1.0).then_inc(*aG.inc())
                g.wait_ge(aG.h, aG.val)
                g.affine_select(out=scr[:, 640:768], in_=scr[:, 640:768],
                                pattern=[[-16, 128]], compare_op=ALU.is_gt,
                                fill=0.0, base=t * 128 - 15,
                                channel_multiplier=1).then_inc(*aG.inc())

        if "hs" in b_skip:
            continue
        @block.tensor
        def _(tt, t=t):
            tt.wait_ge(aV.h, aV.val)
            tt.wait_ge(dA.h, dA.val)
            for ih in range(4):
                lhs = qi_fm[:, (ih // 2) * T + t * 128:(ih // 2) * T + (t + 1) * 128]
                rhsv = k.sb["kcompT2"][:, (ih % 2) * 128:(ih % 2 + 1) * 128]
                ins = tt.matmul(PS[0][:, ih * 128:(ih + 1) * 128], lhs,
                                rhsv, start=True, stop=True)
                if ih == 3:
                    ins.then_inc(*aT.inc())

        if "isc" in b_skip:
            continue
        @block.vector
        def _(v, t=t):
            v.wait_ge(aT.h, aT.val)
            v.wait_ge(aG.h, aG.val)
            isc = scr2[:, 0:128]
            for ih in range(4):
                wb = wih_tm[:, t * 4 + ih:t * 4 + ih + 1].to_broadcast([128, 128])
                if ih == 0:
                    v.scalar_tensor_tensor(isc, PS[0][:, 0:128], 0.0, wb,
                                           ALU.max, ALU.mult)
                else:
                    v.scalar_tensor_tensor(scr2[:, 128:256],
                                           PS[0][:, ih * 128:(ih + 1) * 128],
                                           0.0, wb, ALU.max, ALU.mult)
                    v.tensor_add(isc, isc, scr2[:, 128:256])
            v.tensor_scalar(scr2[:, 256:384], scr[:, 640:768], 1e30, -1e30,
                            ALU.mult, ALU.add)
            v.tensor_add(isc, isc, scr2[:, 256:384])
            v.engine_nop().then_inc(*aV.inc())

        @block.vector
        def _(v, t=t):
            sbuf_a = scr2[:, 0:128]
            sbuf_b = scr2[:, 128:256]
            m8 = tcol[:, 16:24]
            if t >= 8:
                cur, oth = sbuf_a, sbuf_b
                for it in range(8):
                    v.max(out=m8, in_=cur).then_inc(*nA.inc())
                    v.wait_ge(nA.h, nA.val)
                    v.match_replace(out=oth, in_to_replace=m8, in_values=cur,
                                    imm_value=SENT).then_inc(*nA.inc())
                    v.wait_ge(nA.h, nA.val)
                    cur, oth = oth, cur
                v.tensor_scalar(scr2[:, 384:512], cur, SENT, None,
                                ALU.is_equal).then_inc(*nA.inc())
                v.wait_ge(nA.h, nA.val)
                v.tensor_mul(scr2[:, 384:512], scr2[:, 384:512], scr[:, 640:768])
            else:
                v.tensor_copy(scr2[:, 384:512], scr[:, 640:768])
            v.tensor_scalar(scr2[:, 256:384], scr2[:, 384:512], 1e30, -1e30,
                            ALU.mult, ALU.add)
            v.reduce_max(tcol[:, 26:27], scr2[:, 384:512], axis=AX.X)
            v.engine_nop().then_inc(*aV.inc())

        if b_stage < 2:
            continue
        for h in range(H if b_stage >= 3 else 1):
            bank = PS[2 + h // 4]
            bcol = (h % 4) * 128

            @block.tensor
            def _(tt, h=h, qbase=qbase):
                tt.wait_ge(aV.h, aV.val)
                tt.matmul(PS[0][:, 0:128],
                          ar[:, qbase + h * 128:qbase + (h + 1) * 128],
                          ckv_K[:], start=True, stop=True).then_inc(*aT.inc())

            @block.vector
            def _(v, h=h):
                v.wait_ge(aT.h, aT.val)
                smb = scr2[:, 512:640]
                v.tensor_add(smb, PS[0][:, 0:128], scr2[:, 256:384])
                v.reduce_max(tcol[:, 27:28], smb, axis=AX.X, negate=True)
                v.engine_nop().then_inc(*aV.inc())

            @block.scalar
            def _(s, h=h):
                s.wait_ge(aV.h, aV.val)
                s.activation(scr2[:, 640:768], scr2[:, 512:640], AF.Exp,
                             bias=tcol[:, 27:28],
                             accum_out=tcol[:, 28:29]).then_inc(*aS.inc())

            @block.vector
            def _(v, h=h):
                v.wait_ge(aS.h, aS.val)
                v.reciprocal(tcol[:, 28:29], tcol[:, 28:29]).then_inc(*nA.inc())
                v.wait_ge(nA.h, nA.val)
                v.tensor_mul(tcol[:, 28:29], tcol[:, 28:29],
                             tcol[:, 26:27]).then_inc(*nA.inc())
                v.wait_ge(nA.h, nA.val)
                v.tensor_scalar(scr2[:, 768:896], scr2[:, 640:768],
                                tcol[:, 28:29], None, ALU.mult)
                v.engine_nop().then_inc(*aV.inc())

            @block.tensor
            def _(tt, h=h):
                tt.wait_ge(aV.h, aV.val)
                tt.transpose(PS[1][:, 0:128], scr2[:, 768:896],
                             ident[:]).then_inc(*aT.inc())

            @block.vector
            def _(v, h=h):
                v.wait_ge(aT.h, aT.val)
                v.tensor_copy(scr2[:, 896:1024], PS[1][:, 0:128])
                v.engine_nop().then_inc(*aV.inc())

            @block.tensor
            def _(tt, h=h, win=win, bank=bank, bcol=bcol, qbase=qbase):
                tt.wait_ge(aV.h, aV.val)
                tt.wait_ge(aS.h, aS.val)
                tt.matmul(bank[:, bcol:bcol + 128], ckv_V[:],
                          scr2[:, 896:1024], start=(h % 4 == 0), stop=False)
                tt.matmul(PS[6][:, 0:512],
                          ar[:, qbase + h * 128:qbase + (h + 1) * 128],
                          k_fm[:, win * 256:win * 256 + 512],
                          start=True, stop=True).then_inc(*aT.inc())

            @block.vector
            def _(v, h=h):
                v.wait_ge(aT.h, aT.val)
                v.reduce_max(tcol[:, 29:30], PS[6][:, 0:512], axis=AX.X,
                             negate=True)
                v.engine_nop().then_inc(*aV.inc())

            @block.scalar
            def _(s, h=h):
                s.wait_ge(aV.h, aV.val)
                s.activation(scr3[:, 0:512], PS[6][:, 0:512], AF.Exp,
                             bias=tcol[:, 29:30]).then_inc(*aS.inc())

            @block.vector
            def _(v, h=h, t=t, win=win):
                v.wait_ge(aS.h, aS.val)
                mvar = (t % 2) * 2 + (1 if win == 0 else 0)
                v.tensor_mul(scr3[:, 0:512], scr3[:, 0:512],
                             wmask[:, mvar * 512:(mvar + 1) * 512]).then_inc(*nA.inc())
                v.wait_ge(nA.h, nA.val)
                v.reduce_sum(tcol[:, 30:31], scr3[:, 0:512],
                             axis=AX.X).then_inc(*nA.inc())
                v.wait_ge(nA.h, nA.val)
                v.reciprocal(tcol[:, 30:31], tcol[:, 30:31]).then_inc(*nA.inc())
                v.wait_ge(nA.h, nA.val)
                v.tensor_scalar(scr3[:, 512:1024], scr3[:, 0:512],
                                tcol[:, 30:31], None, ALU.mult)
                v.engine_nop().then_inc(*aV.inc())

            @block.tensor
            def _(tt, h=h):
                tt.wait_ge(aV.h, aV.val)
                for j in range(4):
                    ins = tt.transpose(PS[1][:, ts(j)],
                                       scr3[:, 512 + j * 128:512 + (j + 1) * 128],
                                       ident[:])
                    if j == 3:
                        ins.then_inc(*aT.inc())

            @block.vector
            def _(v, h=h):
                v.wait_ge(aT.h, aT.val)
                v.tensor_copy(scr3[:, 1024:1536], PS[1][:, 0:512])
                v.engine_nop().then_inc(*aV.inc())

            @block.tensor
            def _(tt, h=h, win=win, bank=bank, bcol=bcol):
                tt.wait_ge(aV.h, aV.val)
                for kc in range(4):
                    ins = tt.matmul(
                        bank[:, bcol:bcol + 128],
                        v_tm[:, (win * 2 + kc) * 128:(win * 2 + kc + 1) * 128],
                        scr3[:, 1024 + kc * 128:1024 + (kc + 1) * 128],
                        start=False, stop=(h % 4 == 3 and kc == 3))
                    if kc == 3:
                        ins.then_inc(*aT.inc())

        if b_stage < 4:
            continue
        @block.vector
        def _(v, t=t):
            v.wait_ge(aT.h, aT.val)
            for g4 in range(4):
                v.tensor_copy(attnb[:, g4 * 512:(g4 + 1) * 512],
                              PS[2 + g4][:, 0:512])
            v.engine_nop().then_inc(*aV.inc())

        @block.tensor
        def _(tt, t=t):
            tt.wait_ge(aV.h, aV.val)
            for g in range(G):
                for ic in range(4):
                    ins = tt.matmul(
                        PS[2 + g][:, 0:512],
                        attnb[:, (g * 4 + ic) * 128:(g * 4 + ic + 1) * 128],
                        gw[:, (g * 4 + ic) * 512:(g * 4 + ic + 1) * 512],
                        start=(ic == 0), stop=(ic == 3))
                    if g == 3 and ic == 3:
                        ins.then_inc(*aT.inc())

        @block.vector
        def _(v, t=t):
            v.wait_ge(aT.h, aT.val)
            for g4 in range(4):
                v.tensor_copy(groupedb[:, g4 * 512:(g4 + 1) * 512],
                              PS[2 + g4][:, 0:512])
            v.engine_nop().then_inc(*aV.inc())

        for grp in range(4):
            pb = PS[6 + (grp % 2)]

            @block.tensor
            def _(tt, grp=grp, pb=pb):
                tt.wait_ge(aV.h, aV.val)
                for j in range(4):
                    c = grp * 4 + j
                    ins = tt.transpose(pb[:, ts(j)], groupedb[:, ts(c)], ident[:])
                    if j == 3:
                        ins.then_inc(*aT.inc())

            @block.vector
            def _(v, grp=grp, pb=pb):
                v.wait_ge(aT.h, aT.val)
                v.tensor_copy(gfm[:, grp * 512:(grp + 1) * 512], pb[:, 0:512])
                v.engine_nop().then_inc(*aV.inc())

        @block.sync
        def _(sy, t=t):
            sy.wait_ge(aV.h, aV.val)
            for c in range(NCH):
                sy.dma_start(out=k.dr["grp_hbm"].ap()[ts(c), ts(t)],
                             in_=gfm[:, ts(c)]).then_inc(*dA.inc(16))

    @block.vector
    def _(v, half=half):
        v.wait_ge(dA.h, dA.val)
        v.engine_nop().then_inc(*k.b_done[half].inc())


def emit_attn_final(k, block, sems):
    aV, aS, aT, nA, dA, aG = sems
    ar = k.arena
    nmean = k.sb["nmean"]; rstd2 = k.sb["rstd2"]; nmr2 = k.sb["nmr2"]
    tcol = k.sb["colscr"]
    scr2 = k.sb["scr2"]; scr3 = k.sb["scr3"]; scr4 = k.sb["scr4"]
    PS = k.psN
    x = k.dr["x"]

    fw = ar[:, 0:ARENA]

    @block.sync
    def _(sy):
        sy.wait_ge(k.b_done[1].h, k.b_done[1].val)
        for kc in range(NCH):
            sy.dma_start(out=fw[:, kc * 2048:(kc + 1) * 2048],
                         in_=k.dr["L0_attn_final_w"].ap()[ts(kc), :]).then_inc(*dA.inc(16))

    for t in range(NT):
        @block.sync
        def _(sy, t=t):
            if t > 0:
                sy.wait_ge(aV.h, aV.val)
            sy.wait_ge(dA.h, dA.val)
            for c in range(NCH):
                sy.dma_start(out=scr3[:, ts(c)],
                             in_=k.dr["grp_hbm"].ap()[ts(c), ts(t)]).then_inc(*dA.inc(16))
            sy.dma_start(out=scr4[:, 0:D], in_=x.ap()[ts(t), :]).then_inc(*dA.inc(16))

        @block.tensor
        def _(tt, t=t):
            tt.wait_ge(dA.h, dA.val)
            if t > 0:
                tt.wait_ge(aV.h, aV.val)
            for kc in range(NCH):
                for g4 in range(4):
                    ins = tt.matmul(
                        PS[0 + g4][:, 0:512], scr3[:, ts(kc)],
                        fw[:, kc * 2048 + g4 * 512:kc * 2048 + (g4 + 1) * 512],
                        start=(kc == 0), stop=(kc == NCH - 1))
                    if kc == NCH - 1 and g4 == 3:
                        ins.then_inc(*aT.inc())

        @block.vector
        def _(v, t=t):
            v.wait_ge(aT.h, aT.val)
            for g4 in range(4):
                v.tensor_add(scr3[:, g4 * 512:(g4 + 1) * 512],
                             PS[0 + g4][:, 0:512],
                             scr4[:, g4 * 512:(g4 + 1) * 512])
            v.reduce_sum(tcol[:, 0:1], scr3[:, 0:D], axis=AX.X)
            v.engine_nop().then_inc(*aV.inc())

        @block.scalar
        def _(s, t=t):
            s.wait_ge(aV.h, aV.val)
            s.activation(scr4[:, 0:D], scr3[:, 0:D], AF.Square).then_inc(*aS.inc())

        @block.sync
        def _(sy, t=t):
            sy.wait_ge(aS.h, aS.val)
            sy.dma_start(out=k.dr["x2_hbm"].ap()[ts(t), :],
                         in_=scr3[:, 0:D]).then_inc(*dA.inc(16))

        @block.vector
        def _(v, t=t):
            v.wait_ge(aS.h, aS.val)
            v.reduce_sum(tcol[:, 1:2], scr4[:, 0:D], axis=AX.X).then_inc(*nA.inc())
            v.wait_ge(nA.h, nA.val)
            v.tensor_scalar(nmean[:, t:t + 1], tcol[:, 0:1], -1.0 / D, None,
                            ALU.mult).then_inc(*nA.inc())
            v.wait_ge(nA.h, nA.val)
            v.tensor_mul(tcol[:, 2:3], nmean[:, t:t + 1],
                         nmean[:, t:t + 1]).then_inc(*nA.inc())
            v.wait_ge(nA.h, nA.val)
            v.scalar_tensor_tensor(tcol[:, 3:4], tcol[:, 1:2], 1.0 / D,
                                   tcol[:, 2:3], ALU.mult,
                                   ALU.subtract).then_inc(*nA.inc())
            v.wait_ge(nA.h, nA.val)
            v.tensor_scalar(tcol[:, 3:4], tcol[:, 3:4], EPS_LN, None,
                            ALU.add).then_inc(*nA.inc())
            v.wait_ge(nA.h, nA.val)
            v.reciprocal(tcol[:, 3:4], tcol[:, 3:4])
            v.engine_nop().then_inc(*aV.inc())

        @block.scalar
        def _(s, t=t):
            s.wait_ge(aV.h, aV.val)
            s.activation(rstd2[:, t:t + 1], tcol[:, 3:4], AF.Sqrt).then_inc(*aS.inc())

        @block.vector
        def _(v, t=t):
            v.wait_ge(aS.h, aS.val)
            v.tensor_mul(nmr2[:, t:t + 1], nmean[:, t:t + 1], rstd2[:, t:t + 1])
            v.engine_nop().then_inc(*aV.inc())

    k.fin_aV = aV
    k.fin_dA = dA





SQ2048 = math.sqrt(2048.0)
SQ8192 = math.sqrt(8192.0)


def emit_ffn_tern(k, block):
    fV, fS, fT, nF, fD, fG = k.f_sems
    fW = k.fW
    ar = k.arena
    fscal = k.sb["fscal"]
    B_IN, B_TMP = 16384, 24576
    NTB = 4

    jobs = []
    for i in range(16):
        for j in range(4):
            jobs.append((k.dr["L0_ffn_w_gate"].ap()[ts(i), j * 2048:(j + 1) * 2048],
                         0, [("tern_g", i, j * 4 + jb, jb) for jb in range(4)]))
    for i in range(16):
        for j in range(4):
            jobs.append((k.dr["L0_ffn_w_up"].ap()[ts(i), j * 2048:(j + 1) * 2048],
                         1, [("tern_u", i, j * 4 + jb, jb) for jb in range(4)]))
    for i in range(64):
        jobs.append((k.dr["L0_ffn_w_down"].ap()[ts(i), :],
                     2, [("tern_d", i, jb, jb) for jb in range(4)]))

    vmark = []
    wmark = []
    smark = []
    xq_done = fV.val
    for nb, (srcap, mi, outs) in enumerate(jobs):
        b = nb % NTB
        inb = ar[:, B_IN + b * 2048:B_IN + (b + 1) * 2048]
        tmp = ar[:, B_TMP + b * 2048:B_TMP + (b + 1) * 2048]
        out_bf = ar[:, B_IN + b * 2048:B_IN + b * 2048 + 1024].bitcast(BF16)

        @block.sync
        def _(sy, srcap=srcap, inb=inb, nb=nb):
            if nb == 0:
                sy.wait_ge(fV.h, xq_done)
            if nb >= NTB:
                sy.wait_ge(fS.h, smark[nb - NTB])
                sy.wait_ge(fW.h, wmark[nb - NTB])
            sy.dma_start(out=inb, in_=srcap).then_inc(*fD.inc(16))
        dmk = fD.val

        @block.vector
        def _(v, inb=inb, tmp=tmp, mi=mi, dmk=dmk, nb=nb):
            v.wait_ge(fD.h, dmk)
            if nb >= NTB:
                v.wait_ge(fS.h, smark[nb - NTB])
            v.tensor_scalar(tmp, inb, fscal[:, mi:mi + 1], None, ALU.is_gt)
            v.tensor_scalar(inb, inb, fscal[:, 6 + mi:7 + mi], None, ALU.is_ge)
            v.tensor_tensor(tmp, tmp, inb, ALU.add)
            v.engine_nop().then_inc(*fV.inc())
        vmark.append(fV.val)

        @block.scalar
        def _(s, tmp=tmp, out_bf=out_bf, nb=nb):
            s.wait_ge(fV.h, vmark[nb])
            if nb >= NTB:
                s.wait_ge(fW.h, wmark[nb - NTB])
            s.activation(out_bf, tmp, AF.Identity,
                         bias=-1.0).then_inc(*fS.inc())
        smark.append(fS.val)

        @block.sync
        def _(sy, out_bf=out_bf, outs=outs, nb=nb):
            sy.wait_ge(fS.h, smark[nb])
            for tname, i0, i1, jb in outs:
                sy.dma_start(out=k.dr[tname].ap()[i0, i1],
                             in_=out_bf[:, jb * 512:(jb + 1) * 512]
                             ).then_inc(*fW.inc(16))
        wmark.append(fW.val)


def emit_ffn_gateup(k, block):
    fV, fS, fT, nF, fD, fG = k.f_sems
    fW = k.fW
    ar = k.arena
    e0row = k.sb["e0row"]; h128 = k.sb["h128"]; ident = k.sb["ident"]
    sigma1 = k.sb["sigma1"]; fscal = k.sb["fscal"]
    scr = k.sb["scr"]; scr2 = k.sb["scr2"]; scr3 = k.sb["scr3"]; scr4 = k.sb["scr4"]
    PS = k.psN
    xq_bf = ar[:, 0:16384].bitcast(BF16)

    rowmark = []
    for g in range(4):
        @block.tensor
        def _(tt, g=g):
            if g == 0:
                tt.wait_ge(fV.h, fV.val)
            else:
                tt.wait_ge(fV.h, rowmark[g - 1])
            for j in range(4):
                ins = tt.transpose(PS[7][0:1, j * 128:(j + 1) * 128],
                                   sigma1[:, g * 4 + j:g * 4 + j + 1], ident[:])
            ins.then_inc(*fT.inc())

        @block.vector
        def _(v, g=g):
            v.wait_ge(fT.h, fT.val)
            v.tensor_copy(scr3[0:1, g * 512:(g + 1) * 512], PS[7][0:1, 0:512])
            v.engine_nop().then_inc(*fV.inc())
        rowmark.append(fV.val)

    @block.tensor
    def _(tt):
        tt.wait_ge(fV.h, fV.val)
        for q in range(4):
            ins = tt.matmul(PS[4 + q][:, 0:512], e0row[:],
                            scr3[0:128, q * 512:(q + 1) * 512],
                            start=True, stop=True)
        ins.then_inc(*fT.inc())

    @block.vector
    def _(v):
        v.wait_ge(fT.h, fT.val)
        for q in range(4):
            v.tensor_scalar(scr4[:, q * 512:(q + 1) * 512], PS[4 + q][:, 0:512],
                            fscal[:, 4:5], None, ALU.mult)
        for q in range(4):
            v.tensor_scalar(scr3[:, q * 512:(q + 1) * 512], PS[4 + q][:, 0:512],
                            fscal[:, 3:4], None, ALU.mult)
        v.engine_nop().then_inc(*fV.inc())

    NG = 64 * 4
    gumark = {}; hmark = {}; h128mark = {}; gsmark = {}; silumark = {}
    ubmark = {}; uwmark = {}; tmark = {}

    def emit_h128(gg):
        oc, qq = gg // 4, gg % 4

        @block.tensor
        def _(tt, gg=gg, qq=qq):
            tt.wait_ge(fV.h, hmark[gg])
            tt.matmul(PS[4 + qq % 2][:, 0:512], h128[:],
                      scr[:, (gg % 2) * 512:(gg % 2 + 1) * 512],
                      start=True, stop=True).then_inc(*fT.inc())
        h128mark[gg] = fT.val

    for ob in range(16):
        b = ob % 2
        stg_bf = ar[:, 16384 + b * 8192:16384 + (b + 1) * 8192].bitcast(BF16)

        @block.sync
        def _(sy, ob=ob, stg_bf=stg_bf):
            if ob == 0:
                sy.wait_ge(fW.h, fW.val)
            if ob >= 2:
                sy.wait_ge(fT.h, tmark[ob - 2])
            for hc in range(16):
                sy.dma_start(out=stg_bf[:, hc * 512:(hc + 1) * 512],
                             in_=k.dr["tern_g"].ap()[hc, ob]).then_inc(*fD.inc(16))
            for hc in range(16):
                sy.dma_start(out=stg_bf[:, 8192 + hc * 512:8192 + (hc + 1) * 512],
                             in_=k.dr["tern_u"].ap()[hc, ob]).then_inc(*fD.inc(16))
        dmark = fD.val

        for j in range(4):
            oc = ob * 4 + j
            for q in range(4):
                g = oc * 4 + q
                sl = g % 2

                @block.tensor
                def _(tt, j=j, q=q, stg_bf=stg_bf, dmark=dmark, first=(j == 0 and q == 0)):
                    if first:
                        tt.wait_ge(fD.h, dmark)
                    for hc in range(16):
                        tt.matmul(PS[q % 2][:, 0:512],
                                  stg_bf[:, hc * 512 + j * 128:hc * 512 + (j + 1) * 128],
                                  xq_bf[:, hc * 2048 + q * 512:hc * 2048 + (q + 1) * 512],
                                  start=(hc == 0), stop=(hc == 15))
                    for hc in range(16):
                        ins = tt.matmul(PS[2 + q % 2][:, 0:512],
                                        stg_bf[:, 8192 + hc * 512 + j * 128:
                                               8192 + hc * 512 + (j + 1) * 128],
                                        xq_bf[:, hc * 2048 + q * 512:hc * 2048 + (q + 1) * 512],
                                        start=(hc == 0), stop=(hc == 15))
                    ins.then_inc(*fT.inc())
                gumark[g] = fT.val

                @block.vector
                def _(v, g=g, q=q, sl=sl):
                    v.wait_ge(fT.h, gumark[g])
                    if g >= 2:
                        v.wait_ge(fW.h, uwmark[g - 2])
                    v.tensor_tensor(scr[:, sl * 512:(sl + 1) * 512],
                                    PS[q % 2][:, 0:512],
                                    scr3[:, q * 512:(q + 1) * 512], ALU.mult)
                    v.tensor_tensor(scr2[:, sl * 512:(sl + 1) * 512],
                                    PS[2 + q % 2][:, 0:512],
                                    scr4[:, q * 512:(q + 1) * 512], ALU.mult)
                    v.engine_nop().then_inc(*fV.inc())
                gsmark[g] = fV.val

                @block.scalar
                def _(s, g=g, sl=sl):
                    s.wait_ge(fV.h, gsmark[g])
                    s.activation(scr[:, sl * 512:(sl + 1) * 512],
                                 scr[:, sl * 512:(sl + 1) * 512],
                                 AF.Silu).then_inc(*fS.inc())
                silumark[g] = fS.val

                @block.vector
                def _(v, g=g, sl=sl):
                    v.wait_ge(fS.h, silumark[g])
                    v.tensor_tensor(scr[:, sl * 512:(sl + 1) * 512],
                                    scr[:, sl * 512:(sl + 1) * 512],
                                    scr2[:, sl * 512:(sl + 1) * 512], ALU.mult)
                    v.engine_nop().then_inc(*fV.inc())
                hmark[g] = fV.val

                if g >= 1:
                    emit_h128(g - 1)

                    @block.vector
                    def _(v, g1=g - 1):
                        v.wait_ge(fT.h, h128mark[g1])
                        if g1 >= 2:
                            v.wait_ge(fW.h, uwmark[g1 - 2])
                        v.tensor_copy(scr2[:, (g1 % 2) * 512:(g1 % 2 + 1) * 512],
                                      PS[4 + g1 % 2][:, 0:512])
                        v.engine_nop().then_inc(*fV.inc())
                    ubmark[g - 1] = fV.val

                    @block.sync
                    def _(sy, g1=g - 1):
                        sy.wait_ge(fV.h, ubmark[g1])
                        sy.dma_start(
                            out=k.dr["u_hbm"].ap()[ts(g1 // 4),
                                                   (g1 % 4) * 512:(g1 % 4 + 1) * 512],
                            in_=scr2[:, (g1 % 2) * 512:(g1 % 2 + 1) * 512]
                        ).then_inc(*fW.inc(16))
                    uwmark[g - 1] = fW.val
        tmark[ob] = gumark[(ob * 4 + 3) * 4 + 3]

    g = NG - 1
    emit_h128(g)

    @block.vector
    def _(v, g=g):
        v.wait_ge(fT.h, h128mark[g])
        v.tensor_copy(scr2[:, (g % 2) * 512:(g % 2 + 1) * 512],
                      PS[4 + g % 2][:, 0:512])
        v.engine_nop().then_inc(*fV.inc())
    ubmark[g] = fV.val

    @block.sync
    def _(sy, g=g):
        sy.wait_ge(fV.h, ubmark[g])
        sy.dma_start(out=k.dr["u_hbm"].ap()[ts(g // 4),
                                            (g % 4) * 512:(g % 4 + 1) * 512],
                     in_=scr2[:, (g % 2) * 512:(g % 2 + 1) * 512]
                     ).then_inc(*fW.inc(16))


def emit_ffn_phase2(k, block):
    fV, fS, fT, nF, fD, fG = k.f_sems
    fW = k.fW
    ar = k.arena
    e0row = k.sb["e0row"]; ident = k.sb["ident"]
    sigma2 = k.sb["sigma2"]; tcol = k.sb["colscr"]
    scr3 = k.sb["scr3"]
    PS = k.psN
    B = 16384
    lastz = None
    castmark = {}; xqwmark = {}

    for s in range(8):
        @block.sync
        def _(sy, s=s, lastz=lastz):
            if s == 0:
                sy.wait_ge(fW.h, fW.val)
            else:
                sy.wait_ge(fV.h, lastz)
            for oc in range(64):
                sy.dma_start(out=ar[:, oc * 256:(oc + 1) * 256],
                             in_=k.dr["u_hbm"].ap()[ts(oc), s * 256:(s + 1) * 256]
                             ).then_inc(*fD.inc(16))
        dmark = fD.val

        for st in range(6):
            stp = 1 << st

            @block.vector
            def _(v, st=st, stp=stp, dmark=dmark, scast=fS.val):
                if st == 0:
                    v.wait_ge(fD.h, dmark)
                    v.wait_ge(fS.h, scast)
                cur0 = 0 if st % 2 == 0 else B
                oth0 = B if st % 2 == 0 else 0
                for c in range(64):
                    p = c ^ stp
                    if c < p:
                        v.tensor_add(ar[:, oth0 + c * 256:oth0 + (c + 1) * 256],
                                     ar[:, cur0 + c * 256:cur0 + (c + 1) * 256],
                                     ar[:, cur0 + p * 256:cur0 + (p + 1) * 256])
                    else:
                        v.tensor_sub(ar[:, oth0 + c * 256:oth0 + (c + 1) * 256],
                                     ar[:, cur0 + p * 256:cur0 + (p + 1) * 256],
                                     ar[:, cur0 + c * 256:cur0 + (c + 1) * 256])
                v.engine_nop().then_inc(*fV.inc())

        @block.scalar
        def _(s_):
            s_.wait_ge(fV.h, fV.val)
            for c in range(64):
                ins = s_.activation(ar[:, B + c * 256:B + (c + 1) * 256],
                                    ar[:, c * 256:(c + 1) * 256], AF.Square)
            ins.then_inc(*fS.inc())

        @block.vector
        def _(v):
            v.wait_ge(fS.h, fS.val)
            n = 64
            while n > 1:
                n //= 2
                for i in range(n):
                    v.tensor_tensor(ar[:, B + i * 256:B + (i + 1) * 256],
                                    ar[:, B + (2 * i) * 256:B + (2 * i + 1) * 256],
                                    ar[:, B + (2 * i + 1) * 256:B + (2 * i + 2) * 256],
                                    ALU.max)
                v.engine_nop().then_inc(*nF.inc())
                v.wait_ge(nF.h, nF.val)
            v.engine_nop().then_inc(*fV.inc())

        @block.tensor
        def _(tt):
            tt.wait_ge(fV.h, fV.val)
            tt.transpose(PS[6][:, 0:128], ar[:, B:B + 128], ident[:])
            tt.transpose(PS[6][:, 128:256], ar[:, B + 128:B + 256],
                         ident[:]).then_inc(*fT.inc())

        @block.vector
        def _(v):
            v.wait_ge(fT.h, fT.val)
            v.reduce_max(tcol[:, 0:1], PS[6][:, 0:128], axis=AX.X).then_inc(*nF.inc())
            v.wait_ge(nF.h, nF.val)
            v.reduce_max(tcol[:, 1:2], PS[6][:, 128:256], axis=AX.X).then_inc(*nF.inc())
            v.wait_ge(nF.h, nF.val)
            v.tensor_scalar(tcol[:, 0:1], tcol[:, 0:1], EPS_FFN * EPS_FFN,
                            None, ALU.max)
            v.tensor_scalar(tcol[:, 1:2], tcol[:, 1:2], EPS_FFN * EPS_FFN,
                            None, ALU.max)
            v.engine_nop().then_inc(*fV.inc())

        @block.scalar
        def _(s_):
            s_.wait_ge(fV.h, fV.val)
            s_.activation(tcol[:, 0:2], tcol[:, 0:2],
                          AF.Sqrt).then_inc(*fS.inc())

        @block.vector
        def _(v, s=s):
            v.wait_ge(fS.h, fS.val)
            v.tensor_scalar(sigma2[:, 2 * s:2 * s + 1], tcol[:, 0:1],
                            1.0 / (127.0 * SQ8192), None, ALU.mult)
            v.tensor_scalar(sigma2[:, 2 * s + 1:2 * s + 2], tcol[:, 1:2],
                            1.0 / (127.0 * SQ8192), None, ALU.mult)
            v.reciprocal(tcol[:, 4:5], tcol[:, 0:1])
            v.reciprocal(tcol[:, 5:6], tcol[:, 1:2])
            v.engine_nop().then_inc(*nF.inc())
            v.wait_ge(nF.h, nF.val)
            v.tensor_scalar(tcol[:, 4:5], tcol[:, 4:5], 127.0, None, ALU.mult)
            v.tensor_scalar(tcol[:, 5:6], tcol[:, 5:6], 127.0, None, ALU.mult)
            v.engine_nop().then_inc(*fV.inc())

        @block.tensor
        def _(tt):
            tt.wait_ge(fV.h, fV.val)
            tt.transpose(PS[7][0:1, 0:128], tcol[:, 4:5], ident[:])
            tt.transpose(PS[7][0:1, 128:256], tcol[:, 5:6],
                         ident[:]).then_inc(*fT.inc())

        @block.vector
        def _(v):
            v.wait_ge(fT.h, fT.val)
            v.tensor_copy(scr3[0:1, 0:256], PS[7][0:1, 0:256])
            v.engine_nop().then_inc(*fV.inc())

        @block.tensor
        def _(tt):
            tt.wait_ge(fV.h, fV.val)
            tt.matmul(PS[6][:, 0:256], e0row[:], scr3[0:128, 0:256],
                      start=True, stop=True)
            tt.matmul(PS[6][:, 256:512], e0row[:], scr3[0:128, 0:256],
                      start=True, stop=True).then_inc(*fT.inc())

        @block.vector
        def _(v):
            v.wait_ge(fT.h, fT.val)
            v.tensor_copy(ar[:, B + 256:B + 768], PS[6][:, 0:512])
            v.engine_nop().then_inc(*fV.inc())

        for pp in range(32):
            @block.vector
            def _(v, pp=pp, s=s):
                if pp >= 2:
                    v.wait_ge(fS.h, castmark[(s, pp - 2)])
                z = ar[:, B + 768:B + 1280]
                w = ar[:, B + 1280 + (pp % 2) * 512:B + 1792 + (pp % 2) * 512]
                bb = ar[:, B + 2304:B + 2816]
                v.tensor_tensor(z, ar[:, pp * 512:(pp + 1) * 512],
                                ar[:, B + 256:B + 768], ALU.mult)
                peel_round(v, z, w, bb, 512)
                v.tensor_sub(w, z, w)
                v.engine_nop().then_inc(*fV.inc())
            vmarkp = fV.val

            @block.scalar
            def _(s_, pp=pp, s=s):
                s_.wait_ge(fV.h, fV.val)
                if pp >= 2:
                    s_.wait_ge(fW.h, xqwmark[(s, pp - 2)])
                cb = ar[:, B + 2816 + (pp % 2) * 256:
                        B + 3072 + (pp % 2) * 256].bitcast(BF16)
                w = ar[:, B + 1280 + (pp % 2) * 512:B + 1792 + (pp % 2) * 512]
                s_.activation(cb, w, AF.Identity, bias=-128.0).then_inc(*fS.inc())
            castmark[(s, pp)] = fS.val

            @block.sync
            def _(sy, pp=pp, s=s):
                sy.wait_ge(fS.h, fS.val)
                cb = ar[:, B + 2816 + (pp % 2) * 256:
                        B + 3072 + (pp % 2) * 256].bitcast(BF16)
                sy.dma_start(out=k.dr["xq2_hbm"].ap()[s, ts(2 * pp), :],
                             in_=cb[:, 0:256]).then_inc(*fW.inc(16))
                sy.dma_start(out=k.dr["xq2_hbm"].ap()[s, ts(2 * pp + 1), :],
                             in_=cb[:, 256:512]).then_inc(*fW.inc(16))
            xqwmark[(s, pp)] = fW.val
        lastz = vmarkp


def emit_ffn_down(k, block):
    fV, fS, fT, nF, fD, fG = k.f_sems
    fW = k.fW
    ar = k.arena
    e0row = k.sb["e0row"]; ident = k.sb["ident"]
    sigma2 = k.sb["sigma2"]; fscal = k.sb["fscal"]
    scr3 = k.sb["scr3"]
    PS = k.psN
    STG, SDQ, FM, YB, X2B = 16384, 24576, 25088, 26112, 28160
    xq2q_bf = ar[:, 0:16384].bitcast(BF16)
    qtmark = None
    dmW = {}
    scmark = {}; trmark = {}; ycmark = {}; yam = {}; ydm = {}; xdm = {}

    for q in range(4):
        rmk = []

        @block.tensor
        def _(tt, q=q):
            tt.wait_ge(fV.h, fV.val)
            for tl in range(4):
                ins = tt.transpose(PS[7][0:1, tl * 128:(tl + 1) * 128],
                                   sigma2[:, 4 * q + tl:4 * q + tl + 1], ident[:])
            ins.then_inc(*fT.inc())

        @block.vector
        def _(v):
            v.wait_ge(fT.h, fT.val)
            v.tensor_copy(scr3[0:1, 0:512], PS[7][0:1, 0:512])
            v.engine_nop().then_inc(*fV.inc())

        @block.tensor
        def _(tt):
            tt.wait_ge(fV.h, fV.val)
            tt.matmul(PS[6][:, 0:512], e0row[:], scr3[0:128, 0:512],
                      start=True, stop=True).then_inc(*fT.inc())

        @block.vector
        def _(v):
            v.wait_ge(fT.h, fT.val)
            v.tensor_scalar(ar[:, SDQ:SDQ + 512], PS[6][:, 0:512],
                            fscal[:, 5:6], None, ALU.mult)
            v.engine_nop().then_inc(*fV.inc())

        @block.sync
        def _(sy, q=q, qtmark=qtmark):
            if q == 0:
                sy.wait_ge(fW.h, fW.val)
            else:
                sy.wait_ge(fT.h, qtmark)
            for oc in range(64):
                for hh in range(2):
                    sy.dma_start(
                        out=xq2q_bf[:, oc * 512 + hh * 256:oc * 512 + (hh + 1) * 256],
                        in_=k.dr["xq2_hbm"].ap()[2 * q + hh, ts(oc), :]
                    ).then_inc(*fD.inc(16))
        qdmark = fD.val

        for dq in range(4):
            for w8 in range(4):
                gw = dq * 4 + w8

                @block.sync
                def _(sy, q=q, dq=dq, w8=w8, gw=gw):
                    if gw >= 2:
                        sy.wait_ge(fT.h, dmW[(q, gw - 2, "c")])
                    for oi in range(16):
                        oc = w8 * 16 + oi
                        sy.dma_start(
                            out=ar[:, STG + (gw % 2) * 4096 + oi * 256:
                                   STG + (gw % 2) * 4096 + (oi + 1) * 256
                                   ].bitcast(BF16),
                            in_=k.dr["tern_d"].ap()[oc, dq]).then_inc(*fD.inc(16))
                dmW[(q, gw)] = fD.val

                @block.tensor
                def _(tt, q=q, dq=dq, w8=w8, gw=gw, qdmark=qdmark):
                    tt.wait_ge(fD.h, dmW[(q, gw)])
                    if w8 == 0 and dq == 0:
                        tt.wait_ge(fD.h, qdmark)
                        tt.wait_ge(fV.h, fV.val)
                    for oi in range(16):
                        oc = w8 * 16 + oi
                        stg_bf = ar[:, STG + (gw % 2) * 4096 + oi * 256:
                                    STG + (gw % 2) * 4096 + (oi + 1) * 256
                                    ].bitcast(BF16)
                        for jdc in range(4):
                            ins = tt.matmul(
                                PS[jdc][:, 0:512],
                                stg_bf[:, jdc * 128:(jdc + 1) * 128],
                                xq2q_bf[:, oc * 512:(oc + 1) * 512],
                                start=(oc == 0), stop=(oc == 63))
                    ins.then_inc(*fT.inc())
                dmW[(q, gw, "c")] = fT.val
            mmark = fT.val

            for jdc in range(4):
                @block.vector
                def _(v, q=q, dq=dq, jdc=jdc, mmark=mmark):
                    v.wait_ge(fT.h, mmark)
                    if jdc >= 2:
                        v.wait_ge(fT.h, trmark[(q, dq, jdc - 2)])
                    if jdc == 0 and (q, dq) != (0, 0):
                        pq, pdq = (q, dq - 1) if dq > 0 else (q - 1, 3)
                        v.wait_ge(fW.h, ydm[(pq, pdq)])
                    v.tensor_tensor(ar[:, FM + (jdc % 2) * 512:FM + (jdc % 2 + 1) * 512],
                                    PS[jdc][:, 0:512], ar[:, SDQ:SDQ + 512], ALU.mult)
                    v.engine_nop().then_inc(*fV.inc())
                scmark[(q, dq, jdc)] = fV.val

                @block.tensor
                def _(tt, q=q, dq=dq, jdc=jdc):
                    tt.wait_ge(fV.h, scmark[(q, dq, jdc)])
                    if jdc >= 2:
                        tt.wait_ge(fV.h, ycmark[(q, dq, jdc - 2)])
                    for tl in range(4):
                        ins = tt.transpose(
                            PS[4 + jdc % 2][:, tl * 128:(tl + 1) * 128],
                            ar[:, FM + (jdc % 2) * 512 + tl * 128:
                               FM + (jdc % 2) * 512 + (tl + 1) * 128], ident[:])
                    ins.then_inc(*fT.inc())
                trmark[(q, dq, jdc)] = fT.val

                @block.vector
                def _(v, q=q, dq=dq, jdc=jdc):
                    v.wait_ge(fT.h, trmark[(q, dq, jdc)])
                    for tl in range(4):
                        v.tensor_copy(ar[:, YB + tl * 512 + jdc * 128:
                                      YB + tl * 512 + (jdc + 1) * 128],
                                      PS[4 + jdc % 2][:, tl * 128:(tl + 1) * 128])
                    v.engine_nop().then_inc(*fV.inc())
                ycmark[(q, dq, jdc)] = fV.val

            for tl in range(4):
                @block.sync
                def _(sy, q=q, dq=dq, tl=tl):
                    if tl >= 2:
                        sy.wait_ge(fV.h, yam[(q, dq, tl - 2)])
                    elif (q, dq) != (0, 0):
                        pq, pdq = (q, dq - 1) if dq > 0 else (q - 1, 3)
                        sy.wait_ge(fV.h, yam[(pq, pdq, tl + 2)])
                    sy.dma_start(out=ar[:, X2B + (tl % 2) * 512:X2B + (tl % 2 + 1) * 512],
                                 in_=k.dr["x2_hbm"].ap()[ts(q * 4 + tl),
                                                         dq * 512:(dq + 1) * 512]
                                 ).then_inc(*fD.inc(16))
                xdm[(q, dq, tl)] = fD.val

                @block.vector
                def _(v, q=q, dq=dq, tl=tl):
                    v.wait_ge(fD.h, xdm[(q, dq, tl)])
                    v.tensor_tensor(ar[:, YB + tl * 512:YB + (tl + 1) * 512],
                                    ar[:, YB + tl * 512:YB + (tl + 1) * 512],
                                    ar[:, X2B + (tl % 2) * 512:X2B + (tl % 2 + 1) * 512],
                                    ALU.add)
                    v.engine_nop().then_inc(*fV.inc())
                yam[(q, dq, tl)] = fV.val

                @block.sync
                def _(sy, q=q, dq=dq, tl=tl):
                    sy.wait_ge(fV.h, yam[(q, dq, tl)])
                    sy.dma_start(out=k.dr["y"].ap()[ts(q * 4 + tl),
                                                    dq * 512:(dq + 1) * 512],
                                 in_=ar[:, YB + tl * 512:YB + (tl + 1) * 512]
                                 ).then_inc(*fW.inc(16))
            ydm[(q, dq)] = fW.val
        qtmark = fT.val
    k.ffn_y_done = fW.val


def emit_ffn(k, block):
    fV = k.sem("f_v"); fS = k.sem("f_s"); fT = k.sem("f_t")
    nF = k.sem("f_nv"); fD = k.sem("f_d"); fG = k.sem("f_g")
    k.f_sems = (fV, fS, fT, nF, fD, fG)
    k.fW = k.sem("f_w")
    emit_ffn_scale(k, block)
    if getattr(k, "ffn_mode", "full") == "scale":
        k.fin2_V = fV; k.fin2_D = fD
        return
    emit_ffn_xq(k, block)
    if getattr(k, "ffn_mode", "full") == "xq":
        k.fin2_V = fV; k.fin2_D = fD
        return
    emit_ffn_tern(k, block)
    emit_ffn_gateup(k, block)
    if getattr(k, "ffn_mode", "full") == "gu":
        k.fin2_V = fV; k.fin2_D = fD; k.fin2_W = k.fW
        return
    emit_ffn_phase2(k, block)
    if getattr(k, "ffn_mode", "full") == "p2":
        k.fin2_V = fV; k.fin2_D = fD; k.fin2_W = k.fW
        return
    emit_ffn_down(k, block)
    k.fin2_V = fV; k.fin2_D = fD; k.fin2_W = k.fW


def emit_ffn_scale(k, block):
    fV, fS, fT, nF, fD, fG = k.f_sems
    ar = k.arena
    e0row = k.sb["e0row"]; h128 = k.sb["h128"]
    scr2 = k.sb["scr2"]
    PS = k.psN
    fscal = k.sbuf("fscal", [128, 16], F32)
    accb = k.sbuf("accb", [128, 256], F32)

    @block.gpsimd
    def _(g):
        g.memset(accb[:], 0.0).then_inc(*fG.inc())
        g.memset(fscal[:], 0.0).then_inc(*fG.inc())
        g.memset(k.sb["sigma1"][:], 0.0).then_inc(*fG.inc())
        g.memset(k.sb["sigma2"][:], 0.0).then_inc(*fG.inc())

    @block.vector
    def _(v):
        v.wait_ge(fG.h, fG.val)
        v.engine_nop().then_inc(*fV.inc())

    loads = []
    for i in range(16):
        for j in range(4):
            loads.append(("L0_ffn_w_gate",
                          k.dr["L0_ffn_w_gate"].ap()[ts(i), j * 2048:(j + 1) * 2048]))
    for i in range(16):
        for j in range(4):
            loads.append(("L0_ffn_w_up",
                          k.dr["L0_ffn_w_up"].ap()[ts(i), j * 2048:(j + 1) * 2048]))
    for i in range(64):
        loads.append(("L0_ffn_w_down", k.dr["L0_ffn_w_down"].ap()[ts(i), :]))
    dump = ar[:, 16384:18432]
    NSB = 6
    smark = []
    for nb, (name, srcap) in enumerate(loads):
        buf = ar[:, (nb % NSB) * 2048:(nb % NSB) * 2048 + 2048]

        @block.sync
        def _(sy, srcap=srcap, buf=buf, nb=nb):
            if nb >= NSB:
                sy.wait_ge(fS.h, smark[nb - NSB])
            elif nb == 0:
                sy.wait_ge(fS.h, fS.val)
            sy.dma_start(out=buf, in_=srcap).then_inc(*fD.inc(16))
        dmk = fD.val

        @block.scalar
        def _(s, buf=buf, nb=nb, dmk=dmk):
            s.wait_ge(fD.h, dmk)
            s.activation(dump, buf, AF.Abs,
                         accum_out=accb[:, nb:nb + 1]).then_inc(*fS.inc())
        smark.append(fS.val)

    @block.vector
    def _(v):
        v.wait_ge(fS.h, fS.val)
        v.reduce_sum(accb[:, 200:201], accb[:, 0:64], axis=AX.X).then_inc(*nF.inc())
        v.wait_ge(nF.h, nF.val)
        v.reduce_sum(accb[:, 201:202], accb[:, 64:128], axis=AX.X).then_inc(*nF.inc())
        v.wait_ge(nF.h, nF.val)
        v.reduce_sum(accb[:, 202:203], accb[:, 128:192], axis=AX.X)
        v.engine_nop().then_inc(*fV.inc())

    @block.tensor
    def _(tt):
        tt.wait_ge(fV.h, fV.val)
        tt.matmul(PS[6][0:1, 0:3], h128[:, 0:1], accb[:, 200:203],
                  start=True, stop=True).then_inc(*fT.inc())

    @block.vector
    def _(v):
        v.wait_ge(fT.h, fT.val)
        v.tensor_copy(accb[0:1, 204:207], PS[6][0:1, 0:3])
        v.engine_nop().then_inc(*fV.inc())

    @block.tensor
    def _(tt):
        tt.wait_ge(fV.h, fV.val)
        tt.matmul(PS[6][:, 4:7], e0row[:], accb[:, 204:207],
                  start=True, stop=True).then_inc(*fT.inc())

    @block.vector
    def _(v):
        v.wait_ge(fT.h, fT.val)
        v.tensor_scalar(fscal[:, 3:4], PS[6][:, 4:5], 1.0 / (IF * D), EPS_FFN,
                        ALU.mult, ALU.max).then_inc(*nF.inc())
        v.wait_ge(nF.h, nF.val)
        v.tensor_scalar(fscal[:, 4:5], PS[6][:, 5:6], 1.0 / (IF * D), EPS_FFN,
                        ALU.mult, ALU.max).then_inc(*nF.inc())
        v.wait_ge(nF.h, nF.val)
        v.tensor_scalar(fscal[:, 5:6], PS[6][:, 6:7], 1.0 / (IF * D), EPS_FFN,
                        ALU.mult, ALU.max).then_inc(*nF.inc())
        v.wait_ge(nF.h, nF.val)
        v.tensor_scalar(fscal[:, 0:1], fscal[:, 3:4], 0.5, None, ALU.mult)
        v.tensor_scalar(fscal[:, 1:2], fscal[:, 4:5], 0.5, None, ALU.mult)
        v.tensor_scalar(fscal[:, 2:3], fscal[:, 5:6], 0.5, None, ALU.mult)
        v.engine_nop().then_inc(*nF.inc())
        v.wait_ge(nF.h, nF.val)
        v.tensor_scalar(fscal[:, 6:7], fscal[:, 0:1], -1.0, None, ALU.mult)
        v.tensor_scalar(fscal[:, 7:8], fscal[:, 1:2], -1.0, None, ALU.mult)
        v.tensor_scalar(fscal[:, 8:9], fscal[:, 2:3], -1.0, None, ALU.mult)
        v.engine_nop().then_inc(*fV.inc())


def emit_ffn_xq(k, block):
    fV, fS, fT, nF, fD, fG = k.f_sems
    ar = k.arena
    ident = k.sb["ident"]; h128 = k.sb["h128"]; e0row = k.sb["e0row"]
    rstd2 = k.sb["rstd2"]; nmr2 = k.sb["nmr2"]
    sigma1 = k.sb["sigma1"]
    tcol = k.sb["colscr"]
    scr = k.sb["scr"]; scr2 = k.sb["scr2"]; scr3 = k.sb["scr3"]; scr4 = k.sb["scr4"]
    PS = k.psN
    xq_bf = ar[:, 0:16384].bitcast(BF16)
    FT = getattr(k, "ftrim", 99)

    for qq in range(4):
        fmA = ar[:, 16384:24576]
        fmB = ar[:, 24576:32768]

        for tl in range(4):
            t = qq * 4 + tl

            @block.sync
            def _(sy, t=t):
                sy.wait_ge(fV.h, fV.val)
                sy.wait_ge(fS.h, fS.val)
                sy.dma_start(out=scr3[:, 0:D],
                             in_=k.dr["x2_hbm"].ap()[ts(t), :]).then_inc(*fD.inc(16))

            @block.scalar
            def _(s, t=t):
                s.wait_ge(fD.h, fD.val)
                s.activation(scr3[:, 0:D], scr3[:, 0:D], AF.Identity,
                             bias=nmr2[:, t:t + 1],
                             scale=rstd2[:, t:t + 1]).then_inc(*fS.inc())

            @block.vector
            def _(v):
                v.wait_ge(fS.h, fS.val)
                v.engine_nop().then_inc(*fV.inc())

            emit_transpose16(k, block, scr3[:, 0:D],
                             lambda c, tl=tl, fmA=fmA: fmA[:, c * 512 + tl * 128:
                                                          c * 512 + (tl + 1) * 128],
                             fT, fV)

        if FT < 2:
            continue
        for c in range(NCH):
            @block.tensor
            def _(tt, c=c):
                tt.wait_ge(fV.h, fV.val)
                tt.matmul(PS[2 + (c % 2)][:, 0:512], h128[:],
                          fmA[:, c * 512:(c + 1) * 512],
                          start=True, stop=True).then_inc(*fT.inc())

            @block.vector
            def _(v, c=c):
                v.wait_ge(fT.h, fT.val)
                v.tensor_copy(fmB[:, c * 512:(c + 1) * 512],
                              PS[2 + (c % 2)][:, 0:512])
                v.engine_nop().then_inc(*fV.inc())

        cur, oth = fmB, fmA
        if FT < 3:
            continue
        for st_ in range(4):
            stp = 1 << st_

            @block.vector
            def _(v, cur=cur, oth=oth, stp=stp):
                v.wait_ge(fV.h, fV.val)
                for c in range(NCH):
                    p = c ^ stp
                    if c < p:
                        v.tensor_add(oth[:, c * 512:(c + 1) * 512],
                                     cur[:, c * 512:(c + 1) * 512],
                                     cur[:, p * 512:(p + 1) * 512])
                    else:
                        v.tensor_sub(oth[:, c * 512:(c + 1) * 512],
                                     cur[:, p * 512:(p + 1) * 512],
                                     cur[:, c * 512:(c + 1) * 512])
                v.engine_nop().then_inc(*fV.inc())
            cur, oth = oth, cur

        if FT < 4:
            continue
        @block.scalar
        def _(s, cur=cur, oth=oth):
            s.wait_ge(fV.h, fV.val)
            for c in range(NCH):
                ins = s.activation(oth[:, c * 512:(c + 1) * 512],
                                   cur[:, c * 512:(c + 1) * 512], AF.Square)
            ins.then_inc(*fS.inc())

        @block.vector
        def _(v, oth=oth):
            v.wait_ge(fS.h, fS.val)
            n = NCH
            while n > 1:
                n //= 2
                for i in range(n):
                    v.tensor_tensor(oth[:, i * 512:(i + 1) * 512],
                                    oth[:, (2 * i) * 512:(2 * i + 1) * 512],
                                    oth[:, (2 * i + 1) * 512:(2 * i + 2) * 512],
                                    ALU.max)
                v.engine_nop().then_inc(*nF.inc())
                v.wait_ge(nF.h, nF.val)
            v.engine_nop().then_inc(*fV.inc())

        @block.tensor
        def _(tt, oth=oth):
            tt.wait_ge(fV.h, fV.val)
            for j in range(4):
                ins = tt.transpose(PS[6][:, ts(j)],
                                   oth[:, j * 128:(j + 1) * 128], ident[:])
                if j == 3:
                    ins.then_inc(*fT.inc())

        @block.vector
        def _(v, qq=qq):
            v.wait_ge(fT.h, fT.val)
            for j in range(4):
                v.reduce_max(tcol[:, j:j + 1], PS[6][:, ts(j)],
                             axis=AX.X).then_inc(*nF.inc())
            v.wait_ge(nF.h, nF.val)
            for j in range(4):
                v.tensor_scalar(tcol[:, j:j + 1], tcol[:, j:j + 1],
                                EPS_FFN * EPS_FFN, None, ALU.max)
            v.engine_nop().then_inc(*fV.inc())

        @block.scalar
        def _(s):
            s.wait_ge(fV.h, fV.val)
            s.activation(tcol[:, 0:4], tcol[:, 0:4],
                         AF.Sqrt).then_inc(*fS.inc())

        @block.vector
        def _(v, qq=qq):
            v.wait_ge(fS.h, fS.val)
            for j in range(4):
                v.tensor_scalar(sigma1[:, qq * 4 + j:qq * 4 + j + 1],
                                tcol[:, j:j + 1], 1.0 / (127.0 * SQ2048), None,
                                ALU.mult)
                v.reciprocal(tcol[:, 4 + j:5 + j], tcol[:, j:j + 1])
            v.engine_nop().then_inc(*nF.inc())
            v.wait_ge(nF.h, nF.val)
            for j in range(4):
                v.tensor_scalar(tcol[:, 4 + j:5 + j], tcol[:, 4 + j:5 + j],
                                127.0, None, ALU.mult)
            v.engine_nop().then_inc(*fV.inc())

        if FT < 5:
            continue
        @block.tensor
        def _(tt):
            tt.wait_ge(fV.h, fV.val)
            for j in range(4):
                ins = tt.transpose(PS[7][0:1, j * 128:(j + 1) * 128],
                                   tcol[:, 4 + j:5 + j], ident[:])
                if j == 3:
                    ins.then_inc(*fT.inc())

        @block.vector
        def _(v):
            v.wait_ge(fT.h, fT.val)
            v.tensor_copy(scr[0:1, 0:512], PS[7][0:1, 0:512])
            v.engine_nop().then_inc(*fV.inc())

        @block.tensor
        def _(tt):
            tt.wait_ge(fV.h, fV.val)
            tt.matmul(PS[6][:, 0:512], e0row[:], scr[0:1, 0:512].to_broadcast(
                [128, 512]) if False else scr[0:128, 0:512],
                start=True, stop=True).then_inc(*fT.inc())

        @block.vector
        def _(v):
            v.wait_ge(fT.h, fT.val)
            v.tensor_copy(scr2[:, 0:512], PS[6][:, 0:512])
            v.engine_nop().then_inc(*fV.inc())

        if FT < 6:
            continue
        for c in range(NCH):
            @block.vector
            def _(v, c=c, cur=cur, oth=oth):
                v.wait_ge(fS.h, fS.val)
                z = oth[:, 512:1024]
                w = scr3[:, 0:512]
                b = scr3[:, 512:1024]
                v.tensor_mul(z, cur[:, c * 512:(c + 1) * 512],
                             scr2[:, 0:512]).then_inc(*nF.inc())
                v.wait_ge(nF.h, nF.val)
                peel_round(v, z, w, b, 512)
                v.tensor_sub(w, z, w)
                v.engine_nop().then_inc(*fV.inc())

            @block.scalar
            def _(s, c=c, qq=qq):
                s.wait_ge(fV.h, fV.val)
                s.activation(xq_bf[:, c * 2048 + qq * 512:c * 2048 + (qq + 1) * 512],
                             scr3[:, 0:512], AF.Identity,
                             bias=-128.0).then_inc(*fS.inc())


def emit_finish_dev(k, block):
    fin = k.sem("fin")
    last = k.last_sem
    full = hasattr(k, "ffn_y_done")

    if not full:
        @block.sync
        def _(sy):
            sy.wait_ge(last.h, last.val)
            sy.dma_start(out=k.dr["y"].ap()[0:128, :],
                         in_=k.sb["wmask"][:]).then_inc(*fin.inc(16))

    @block.gpsimd
    def _(g):
        if full:
            g.wait_ge(k.fW.h, k.ffn_y_done)
        else:
            g.wait_ge(fin.h, fin.val)
        if hasattr(k, "fin_dA"):
            g.wait_ge(k.fin_dA.h, k.fin_dA.val)
        if hasattr(k, "fin2_D"):
            g.wait_ge(k.fin2_D.h, k.fin2_D.val)
        if hasattr(k, "fin2_W"):
            g.wait_ge(k.fin2_W.h, k.fin2_W.val)
        if k.dtap.val:
            g.wait_ge(k.dtap.h, k.dtap.val)


def build_program(upto="compress", taps=(), attn_mode="full", ffn_mode="full"):
    nc = bass.Bass("TRN2", target_bir_lowering=False, detect_race_conditions=False)
    ctx = ExitStack()
    with ctx:
        k = K(nc, ctx)
        k.upto = upto
        k.tapset = set(taps)
        k.attn_mode = attn_mode
        k.ffn_mode = ffn_mode
        k.dram("x", [T, D], F32, kind="ExternalInput")
        k.dram("h128_in", [128, 128], F32, kind="ExternalInput")
        for n in WEIGHT_NAMES:
            k.dram(n, INPUT_SHAPES[n], F32, kind="ExternalInput")
        k.dram("y", [T, D], F32, kind="ExternalOutput")
        fm = getattr(k, "ffn_mode", "full")
        dbg = "ExternalOutput" if (upto == "attn" or fm != "full") else "Internal"
        k.dram("x2_hbm", [T, D], F32, kind=dbg)
        k.dram("grp_hbm", [D, T], F32, kind=dbg)
        k.dram("u_hbm", [IF, T], F32,
               kind="ExternalOutput" if fm == "gu" else "Internal")
        k.dram("xq2_hbm", [8, IF, 256], BF16,
               kind="ExternalOutput" if fm == "p2" else "Internal")
        k.dram("tern_g", [NCH, 16, 128, 512], BF16)
        k.dram("tern_u", [NCH, 16, 128, 512], BF16)
        k.dram("tern_d", [64, 4, 128, 512], BF16)

        k.arena = k.sbuf("arena", [128, ARENA], F32)
        k.sbuf("ident", [128, 128], F32)
        k.sbuf("h128", [128, 128], F32)
        k.sbuf("cos_tm", [128, NT * 32], F32)
        k.sbuf("sin_tm", [128, NT * 32], F32)
        k.sbuf("wmask", [128, 4 * 512], F32)
        k.sbuf("e0row", [128, 128], F32)
        k.sbuf("bcsrc", [128, 256], F32)
        k.sbuf("wqs_b", [128, 128], F32)
        k.sbuf("wk_b", [128, 128], F32)
        k.sbuf("nmean", [128, NT], F32)
        k.sbuf("rstd1", [128, NT], F32)
        k.sbuf("nmr1", [128, NT], F32)
        k.sbuf("rstd2", [128, NT], F32)
        k.sbuf("nmr2", [128, NT], F32)
        k.sbuf("k_fm", [128, (NT + 2) * 128], F32)
        k.sbuf("v_tm", [128, (NT + 2) * 128], F32)
        k.sbuf("kcompT", [128, 128], F32)
        k.sbuf("kcompT2", [128, 256], F32)
        k.sbuf("ckv_raw", [128, 128], F32)
        k.sbuf("ckv_V", [128, 128], F32)
        k.sbuf("ckv_K", [128, 128], F32)
        k.sbuf("qi_fm", [128, 2 * T], F32)
        k.sbuf("wih_tm", [128, NT * 4], F32)
        k.sbuf("sigma1", [128, NT], F32)
        k.sbuf("sigma2", [128, NT], F32)
        k.sbuf("scr", [128, 1024], F32)
        k.sbuf("scr2", [128, 1024], F32)
        k.sbuf("scr3", [128, 2048], F32)
        k.sbuf("scr4", [128, 2048], F32)
        k.sbuf("colscr", [128, 32], F32)
        cpool = k.sbuf("cpool", [128, 8], F32)
        k.constvals = [EPS_LN, EPS_RMS, -128.0, 1e30, 0.5, -1.0]
        for i, cv in enumerate(k.constvals):
            nc.const_aps.register(F32, float(cv), cpool[:, i:i + 1])
        k.psN = [ctx.enter_context(nc.psum_tensor(f"PS{i}", [128, 512], F32))
                 for i in range(8)]

        with nc.Block() as block:
            k.gsem = k.sem("g_prog")
            k.dtap = k.sem("d_tap")
            emit_constants(k, block)
            k.last_sem = k.c_sV
            if upto != "const":
                emit_prepass(k, block)
                k.last_sem = k.p_pV
            if upto not in ("const", "prepass"):
                emit_compress(k, block)
                k.last_sem = k.q_pV
            if upto in ("attn", "all"):
                emit_attention(k, block)
                k.last_sem = k.fin_aV
            if upto == "all":
                emit_ffn(k, block)
                k.last_sem = k.fin2_V
            ARENA_TAPS = {"kvcc": (A_KVCC, T, 128), "kvcz": (A_KVCZ, T, 128),
                          "idxc": (A_IDXC, T, 64), "idxz": (A_IDXZ, T, 64),
                          "qic": (A_QIC, T, 64), "xfm": (A_XFM, D, 128),
                          "xln": (A_XLN, D, 128)}
            for name in list(k.tapset):
                if name in k.sb:
                    k.tap(block, name, k.sb[name][:], wait=k.last_sem)
                elif name in ARENA_TAPS:
                    off, w, rows = ARENA_TAPS[name]
                    k.tap(block, name, k.arena[0:rows, off:off + w],
                          wait=k.last_sem)
            emit_finish_dev(k, block)
    return nc, k


def hadamard128():
    i = np.arange(128, dtype=np.uint32)
    a = i[:, None] & i[None, :]
    pc = np.zeros_like(a)
    for b in range(7):
        pc ^= (a >> b) & 1
    return np.where(pc == 0, 1.0, -1.0).astype(np.float32)


def kernel(**inputs):
    xfull = np.ascontiguousarray(np.asarray(inputs["L0_attn_h"], np.float32))
    weights = {n: np.ascontiguousarray(np.asarray(inputs[n], np.float32))
               for n in WEIGHT_NAMES}
    for n in ("L0_ffn_w_gate", "L0_ffn_w_up", "L0_ffn_w_down"):
        weights[n] = np.ascontiguousarray(weights[n].T)
    weights["h128_in"] = hadamard128()
    nc, _ = build_program(upto="all")
    in_maps = [dict(weights, x=np.ascontiguousarray(xfull[i])) for i in range(8)]
    res = run_bass_kernel_spmd(nc, in_maps, core_ids=list(range(8)))
    return np.stack([r["y"] for r in res.results], axis=0)


# version 05
# speedup vs baseline: 1.0786x; 1.0315x over previous
import math
from contextlib import ExitStack

import numpy as np

import concourse.bass as bass
import concourse.mybir as mybir
from concourse.bass_utils import run_bass_kernel_spmd

F32 = mybir.dt.float32
BF16 = mybir.dt.bfloat16
AF = mybir.ActivationFunctionType
ALU = mybir.AluOpType
AX = mybir.AxisListType

T = 2048
D = 2048
H = 16
HD = 128
QC = 512
M = 16
NB = T // M
NIH, CI = 4, 64
G, DG = 4, 512
WIN = 256
EPS_RMS = 1e-6
EPS_LN = 1e-5
EPS_FFN = 1e-5
NEG = -1e30
SENT = -2e30
SCALE = 1.0 / math.sqrt(HD)
IF = 8192
NT = T // 128
NCH = D // 128

INPUT_SHAPES = {
    "x": [T, D],
    "L0_attn_w_qc": [D, QC], "L0_attn_w_qup": [QC, D],
    "L0_attn_kvc_w": [D, HD], "L0_attn_kvc_wz": [D, HD],
    "L0_attn_kvc_bias": [M, HD], "L0_attn_k_proj_w": [D, HD],
    "L0_attn_v_proj_w": [D, HD], "L0_attn_idx_c_w": [D, CI],
    "L0_attn_idx_c_wz": [D, CI], "L0_attn_idx_c_bias": [M, CI],
    "L0_attn_w_dq": [D, CI], "L0_attn_w_iuq": [CI, NIH * CI],
    "L0_attn_w_w": [D, NIH], "L0_attn_q_norm_w": [HD],
    "L0_attn_k_norm_w": [HD], "L0_attn_group_w": [G, 512, DG],
    "L0_attn_final_w": [G * DG, D],
    "L0_ffn_w_gate": [D, IF], "L0_ffn_w_up": [D, IF],
    "L0_ffn_w_down": [IF, D],
}
WEIGHT_NAMES = [n for n in INPUT_SHAPES if n != "x"]

ARENA = 32768
TRIM = 99
PTRIM = 99
PTILES = NT


def ts(i, n=128):
    return slice(i * n, (i + 1) * n)


class Sem:
    def __init__(self, handle):
        self.h = handle
        self.val = 0

    def inc(self, n=1):
        self.val += n
        return self.h, n


class K:
    def __init__(self, nc, ctx):
        self.nc = nc
        self.ctx = ctx
        self.sb = {}
        self.dr = {}
        self.taps = {}

    def sbuf(self, name, shape, dtype=F32):
        h = self.ctx.enter_context(self.nc.sbuf_tensor(name, shape, dtype))
        self.sb[name] = h
        return h

    def dram(self, name, shape, dtype=F32, kind="Internal"):
        h = self.nc.dram_tensor(name, shape, dtype, kind=kind)
        self.dr[name] = h
        return h

    def sem(self, name):
        return Sem(self.ctx.enter_context(self.nc.semaphore(name)))

    def tap(self, block, name, src_ap, wait=None):
        if name not in self.tapset:
            return
        shape = list(src_ap.shape)
        dh = self.dram("tap_" + name, shape, src_ap.dtype, kind="ExternalOutput")
        self.taps[name] = dh
        dtap = self.dtap

        @block.sync
        def _(sy):
            if wait is not None:
                sy.wait_ge(wait.h, wait.val)
            sy.dma_start(out=dh.ap(), in_=src_ap).then_inc(*dtap.inc(16))


def rope7(v, out, src, cc, ss, tmp, nV):
    v.tensor_mul(tmp[:, 0:32], src[:, 0:32], cc)
    v.tensor_mul(tmp[:, 32:64], src[:, 32:64], ss)
    v.tensor_mul(tmp[:, 64:96], src[:, 0:32], ss)
    v.tensor_mul(tmp[:, 96:128], src[:, 32:64], cc).then_inc(*nV.inc())
    v.wait_ge(nV.h, nV.val)
    v.tensor_sub(out[:, 0:32], tmp[:, 0:32], tmp[:, 32:64])
    v.tensor_add(out[:, 32:64], tmp[:, 64:96], tmp[:, 96:128])
    v.tensor_copy(out[:, 64:128], src[:, 64:128])


def peel_round(v, z, w, b, width):
    v.tensor_scalar(z, z, 128.5, None, ALU.add)
    src = z
    for kk in range(7, -1, -1):
        v.tensor_scalar(b, src, float(2 ** kk), None, ALU.is_ge)
        v.scalar_tensor_tensor(w, b, float(-(2 ** kk)), src, ALU.mult, ALU.add)
        src = w



def emit_constants(k, block):
    ident = k.sb["ident"]; h128 = k.sb["h128"]
    cos_tm = k.sb["cos_tm"]; sin_tm = k.sb["sin_tm"]
    wmask = k.sb["wmask"]
    wqs_b = k.sb["wqs_b"]; wk_b = k.sb["wk_b"]
    e0row = k.sb["e0row"]; bcsrc = k.sb["bcsrc"]
    scr = k.sb["scr"]; scr2 = k.sb["scr2"]
    scr3 = k.sb["scr3"]; scr4 = k.sb["scr4"]

    sG = k.sem("c_g"); sV = k.sem("c_v"); sS = k.sem("c_s")
    sT = k.sem("c_t"); sD = k.sem("c_d")

    @block.sync
    def _(sy):
        sy.dma_start(out=h128[:], in_=k.dr["h128_in"].ap()).then_inc(*sD.inc(16))
        sy.wait_ge(sG.h, 1)
        sy.dma_start(out=bcsrc[0:1, 0:128],
                     in_=k.dr["L0_attn_q_norm_w"].ap()[None, :]).then_inc(*sD.inc(16))
        sy.dma_start(out=bcsrc[0:1, 128:256],
                     in_=k.dr["L0_attn_k_norm_w"].ap()[None, :]).then_inc(*sD.inc(16))

    if TRIM < 2:
        k.c_sV = sD
        return

    @block.gpsimd
    def _(g):
        def step(mk):
            g.wait_ge(sG.h, sG.val)
            mk().then_inc(*sG.inc())
        step(lambda: g.memset(bcsrc[:], 0.0))
        step(lambda: g.memset(e0row[:], 1.0))
        step(lambda: g.affine_select(out=e0row[:], in_=e0row[:],
                                     pattern=[[0, 128]],
                                     compare_op=ALU.is_equal, fill=0.0,
                                     base=0, channel_multiplier=1))
        step(lambda: g.memset(ident[:], 1.0))
        step(lambda: g.affine_select(out=ident[:], in_=ident[:], pattern=[[-1, 128]],
                             compare_op=ALU.is_equal, fill=0.0, base=0,
                             channel_multiplier=1))
        for par in range(2):
            for w0 in range(2):
                dst = wmask[:, (par * 2 + w0) * 512:(par * 2 + w0 + 1) * 512]
                step(lambda: g.memset(scr2[:, 0:512], 1.0))
                step(lambda: g.affine_select(out=scr2[:, 0:512], in_=scr2[:, 0:512],
                                     pattern=[[1, 512]], compare_op=ALU.is_ge,
                                     fill=0.0, base=-128 * par - 1,
                                     channel_multiplier=-1))
                step(lambda: g.affine_select(out=dst, in_=scr2[:, 0:512],
                                     pattern=[[-1, 512]], compare_op=ALU.is_ge,
                                     fill=0.0, base=128 * par + 256,
                                     channel_multiplier=1))
                if w0:
                    step(lambda: g.affine_select(out=dst, in_=dst,
                                         pattern=[[1, 512]],
                                         compare_op=ALU.is_ge,
                                         fill=0.0, base=-256,
                                         channel_multiplier=0))
        for i, cv in enumerate(k.constvals):
            step(lambda: g.memset(k.sb["cpool"][:, i:i + 1], float(cv)))


    @block.vector
    def _(v):
        v.wait_ge(sG.h, sG.val)
        v.engine_nop().then_inc(*sV.inc())

    if TRIM < 3:
        k.c_sV = sV
        return

    LNT32 = math.log(10000.0) / 32.0
    C_HI = float(np.float32(LNT32)); C_LO = float(LNT32 - C_HI)
    TWO_PI = 2.0 * math.pi
    I2P_HI = float(np.float32(1.0 / TWO_PI))
    I2P_LO = float(1.0 / TWO_PI - I2P_HI)
    CW1 = float(np.float32(TWO_PI)); CW2 = float(np.float32(TWO_PI - CW1))
    CW3 = float(TWO_PI - CW1 - CW2)

    @block.gpsimd
    def _(g):
        g.wait_ge(sG.h, sG.val)
        g.iota(scr3[:, 0:512], pattern=[[0, 16], [1, 32]], base=0,
               channel_multiplier=0,
               allow_small_or_imprecise_dtypes=True).then_inc(*sG.inc())
        g.wait_ge(sG.h, sG.val)
        g.iota(scr3[:, 512:1024], pattern=[[128, 16], [0, 32]], base=0,
               channel_multiplier=1,
               allow_small_or_imprecise_dtypes=True).then_inc(*sG.inc())

    @block.vector
    def _(v):
        v.wait_ge(sG.h, sG.val)
        v.tensor_scalar(scr3[:, 1024:1536], scr3[:, 0:512], -C_HI, None, ALU.mult)
        v.scalar_tensor_tensor(scr3[:, 1024:1536], scr3[:, 0:512], -C_LO,
                               scr3[:, 1024:1536], ALU.mult, ALU.add)
        v.engine_nop().then_inc(*sV.inc())

    @block.scalar
    def _(s):
        s.wait_ge(sV.h, sV.val)
        s.activation(scr3[:, 0:512], scr3[:, 1024:1536], AF.Exp).then_inc(*sS.inc())

    @block.vector
    def _(v):
        v.wait_ge(sS.h, sS.val)
        a = scr3[:, 1024:1536]
        u = scr4[:, 0:512]
        w = scr4[:, 512:1024]
        b = scr4[:, 1024:1536]
        kk = scr3[:, 1536:2048]
        v.tensor_mul(a, scr3[:, 0:512], scr3[:, 512:1024])
        v.tensor_scalar(u, a, I2P_HI, None, ALU.mult)
        v.scalar_tensor_tensor(u, a, I2P_LO, u, ALU.mult, ALU.add)
        v.tensor_scalar(w, u, 512.5, None, ALU.add)
        for p2 in range(9, -1, -1):
            v.tensor_scalar(b, w, float(2 ** p2), None, ALU.is_ge)
            v.scalar_tensor_tensor(w, b, float(-(2 ** p2)), w, ALU.mult, ALU.add)
        v.tensor_scalar(b, u, 512.5, None, ALU.add)
        v.tensor_sub(kk, b, w)
        v.tensor_scalar(kk, kk, 512.0, None, ALU.subtract)
        v.scalar_tensor_tensor(w, kk, -CW1, a, ALU.mult, ALU.add)
        v.scalar_tensor_tensor(w, kk, -CW2, w, ALU.mult, ALU.add)
        v.scalar_tensor_tensor(w, kk, -CW3, w, ALU.mult, ALU.add)
        v.tensor_scalar(b, w, float(math.pi), None, ALU.is_gt)
        v.scalar_tensor_tensor(w, b, -TWO_PI, w, ALU.mult, ALU.add)
        v.tensor_scalar(b, w, float(-math.pi), None, ALU.is_lt)
        v.scalar_tensor_tensor(w, b, TWO_PI, w, ALU.mult, ALU.add)
        v.engine_nop().then_inc(*sV.inc())

    @block.scalar
    def _(s):
        s.wait_ge(sV.h, sV.val)
        s.activation(sin_tm[:], scr4[:, 512:1024], AF.Sin).then_inc(*sS.inc())

    @block.vector
    def _(v):
        v.wait_ge(sS.h, sS.val)
        w = scr4[:, 512:1024]; b = scr4[:, 1024:1536]
        v.tensor_scalar(w, w, float(math.pi / 2), None, ALU.add)
        v.tensor_scalar(b, w, float(math.pi), None, ALU.is_gt)
        v.scalar_tensor_tensor(w, b, -TWO_PI, w, ALU.mult, ALU.add)
        v.engine_nop().then_inc(*sV.inc())

    @block.scalar
    def _(s):
        s.wait_ge(sV.h, sV.val)
        s.activation(cos_tm[:], scr4[:, 512:1024], AF.Sin).then_inc(*sS.inc())

    ps = k.psN[0]

    @block.tensor
    def _(tt):
        tt.wait_ge(sD.h, sD.val)
        tt.wait_ge(sV.h, sV.val)
        tt.matmul(ps[:, 0:128], e0row[:], bcsrc[:, 0:128],
                  start=True, stop=True)
        tt.matmul(ps[:, 128:256], e0row[:], bcsrc[:, 128:256],
                  start=True, stop=True).then_inc(*sT.inc())

    @block.vector
    def _(v):
        v.wait_ge(sT.h, sT.val)
        v.tensor_scalar(wqs_b[:], ps[:, 0:128], SCALE, None, ALU.mult)
        v.tensor_copy(wk_b[:], ps[:, 128:256])
        v.engine_nop().then_inc(*sV.inc())

    k.c_sV = sV



CAT = [("L0_attn_k_proj_w", 0, 128), ("L0_attn_v_proj_w", 128, 128),
       ("L0_attn_kvc_w", 256, 128), ("L0_attn_kvc_wz", 384, 128),
       ("L0_attn_idx_c_w", 512, 64), ("L0_attn_idx_c_wz", 576, 64),
       ("L0_attn_w_dq", 640, 64), ("L0_attn_w_w", 704, 4)]
CAT_W = 708

A_WCAT = 0
A_XB = 11328
A_SQ = 15424
A_XLN = 17472
A_XFM = 19520
A_KVCC = 21568
A_KVCZ = 23616
A_IDXC = 25664
A_IDXZ = 27712
A_QIC = 29760


def emit_ln_tile(k, block, xb, t, stats, sems, sqb):
    nmean, rstd, nmr = stats
    pV, pS = sems[0], sems[1]
    tcol = k.sb["colscr"]

    @block.vector
    def _(v):
        v.reduce_sum(tcol[:, 0:1], xb, axis=AX.X)
        v.engine_nop().then_inc(*pV.inc())

    @block.scalar
    def _(s):
        s.wait_ge(pV.h, pV.val)
        s.activation(sqb, xb, AF.Square).then_inc(*pS.inc())

    @block.vector
    def _(v):
        nv = sems[2]
        v.wait_ge(pS.h, pS.val)
        v.reduce_sum(tcol[:, 1:2], sqb, axis=AX.X).then_inc(*nv.inc())
        v.wait_ge(nv.h, nv.val)
        v.tensor_scalar(nmean[:, t:t + 1], tcol[:, 0:1], -1.0 / D, None,
                        ALU.mult).then_inc(*nv.inc())
        v.wait_ge(nv.h, nv.val)
        v.tensor_mul(tcol[:, 2:3], nmean[:, t:t + 1],
                     nmean[:, t:t + 1]).then_inc(*nv.inc())
        v.wait_ge(nv.h, nv.val)
        v.scalar_tensor_tensor(tcol[:, 3:4], tcol[:, 1:2], 1.0 / D,
                               tcol[:, 2:3], ALU.mult,
                               ALU.subtract).then_inc(*nv.inc())
        v.wait_ge(nv.h, nv.val)
        v.tensor_scalar(tcol[:, 3:4], tcol[:, 3:4], EPS_LN, None,
                        ALU.add).then_inc(*nv.inc())
        v.wait_ge(nv.h, nv.val)
        v.reciprocal(tcol[:, 3:4], tcol[:, 3:4])
        v.engine_nop().then_inc(*pV.inc())

    @block.scalar
    def _(s):
        s.wait_ge(pV.h, pV.val)
        s.activation(rstd[:, t:t + 1], tcol[:, 3:4], AF.Sqrt).then_inc(*pS.inc())

    @block.vector
    def _(v):
        v.wait_ge(pS.h, pS.val)
        v.tensor_mul(nmr[:, t:t + 1], nmean[:, t:t + 1], rstd[:, t:t + 1])
        v.engine_nop().then_inc(*pV.inc())


def emit_transpose16(k, block, src, dstcols, pT, pV, evac_cb=None):
    PS = k.psN
    for grp in range(4):
        pb = PS[6 + (grp % 2)]

        @block.tensor
        def _(tt, grp=grp, pb=pb):
            tt.wait_ge(pV.h, pV.val)
            for j in range(4):
                c = grp * 4 + j
                tt.transpose(pb[:, ts(j)], src[:, ts(c)], k.sb["ident"][:]).then_inc(*pT.inc())

        @block.vector
        def _(v, grp=grp, pb=pb):
            v.wait_ge(pT.h, pT.val)
            for j in range(4):
                v.tensor_copy(dstcols(grp * 4 + j), pb[:, ts(j)])
            v.engine_nop().then_inc(*pV.inc())


def emit_prepass(k, block):
    ar = k.arena
    ident = k.sb["ident"]; wk_b = k.sb["wk_b"]
    cos_tm = k.sb["cos_tm"]; sin_tm = k.sb["sin_tm"]
    k_fm = k.sb["k_fm"]; v_tm = k.sb["v_tm"]
    nmean = k.sb["nmean"]; rstd1 = k.sb["rstd1"]; nmr1 = k.sb["nmr1"]
    wih_tm = k.sb["wih_tm"]; qi_fm = k.sb["qi_fm"]
    tcol = k.sb["colscr"]; kscr = k.sb["scr"]; stg = k.sb["scr3"]
    PS = k.psN
    x = k.dr["x"]

    wcat = ar[:, A_WCAT:A_WCAT + NCH * CAT_W]
    sqb = ar[:, A_SQ:A_SQ + D]
    xlnb = ar[:, A_XLN:A_XLN + D]
    xfm = ar[:, A_XFM:A_XFM + D]
    kvcc = ar[:, A_KVCC:A_KVCC + T]
    kvcz = ar[:, A_KVCZ:A_KVCZ + T]
    idxc = ar[:, A_IDXC:A_IDXC + T]
    idxz = ar[:, A_IDXZ:A_IDXZ + T]
    qicb = ar[:, A_QIC:A_QIC + T]

    dX = k.sem("p_dx"); dW = k.sem("p_dw")
    pV = k.sem("p_v"); pS = k.sem("p_s"); pT = k.sem("p_t")
    nV = k.sem("p_nv")
    k.p_pV = pV

    @block.sync
    def _(sy):
        for name, off, w in CAT:
            for c in range(NCH):
                sy.dma_start(out=wcat[:, c * CAT_W + off:c * CAT_W + off + w],
                             in_=k.dr[name].ap()[ts(c), :]).then_inc(*dW.inc(16))
        sy.dma_start(out=ar[:, A_XB:A_XB + D],
                     in_=x.ap()[0:128, :]).then_inc(*dX.inc(16))

    @block.gpsimd
    def _(g):
        g.memset(k_fm[:, 0:256], 0.0).then_inc(*k.gsem.inc())
        g.memset(v_tm[:, 0:256], 0.0).then_inc(*k.gsem.inc())

    for t in range(PTILES):
        xb = ar[:, A_XB + (t % 2) * D:A_XB + (t % 2 + 1) * D]

        @block.sync
        def _(sy, t=t):
            if t + 1 < NT:
                sy.wait_ge(pS.h, pS.val)
                dst = ar[:, A_XB + ((t + 1) % 2) * D:A_XB + ((t + 1) % 2 + 1) * D]
                sy.dma_start(out=dst, in_=x.ap()[ts(t + 1), :]).then_inc(*dX.inc(16))

        @block.vector
        def _(v, t=t):
            v.wait_ge(dX.h, 16 * (t + 1))
            if t == 0:
                v.wait_ge(k.c_sV.h, k.c_sV.val)
            v.engine_nop().then_inc(*pV.inc())

        emit_ln_tile(k, block, xb, t, (nmean, rstd1, nmr1), (pV, pS, nV), sqb)

        @block.scalar
        def _(s, t=t, xb=xb):
            s.wait_ge(pV.h, pV.val)
            s.activation(xlnb, xb, AF.Identity, bias=nmr1[:, t:t + 1],
                         scale=rstd1[:, t:t + 1]).then_inc(*pS.inc())

        @block.vector
        def _(v):
            v.wait_ge(pS.h, pS.val)
            v.engine_nop().then_inc(*pV.inc())

        if PTRIM < 2:
            continue
        emit_transpose16(k, block, xlnb, lambda c: xfm[:, ts(c)], pT, pV)
        if PTRIM < 3:
            continue

        @block.tensor
        def _(tt, t=t):
            tt.wait_ge(pV.h, pV.val)
            tt.wait_ge(dW.h, dW.val)
            if t > 0:
                tt.wait_ge(pS.h, pS.val)
            for c in range(NCH):
                tt.matmul(PS[0][:, 0:512], xfm[:, ts(c)],
                          wcat[:, c * CAT_W:c * CAT_W + 512],
                          start=(c == 0), stop=(c == NCH - 1))
                tt.matmul(PS[1][:, 0:CAT_W - 512], xfm[:, ts(c)],
                          wcat[:, c * CAT_W + 512:(c + 1) * CAT_W],
                          start=(c == 0), stop=(c == NCH - 1)).then_inc(*pT.inc())

        if PTRIM < 4:
            continue
        @block.vector
        def _(v, t=t):
            v.wait_ge(pT.h, pT.val)
            v.tensor_copy(stg[:, 0:128], PS[0][:, 0:128])
            v.tensor_copy(v_tm[:, (t + 2) * 128:(t + 3) * 128], PS[0][:, 128:256])
            v.tensor_copy(stg[:, 128:384], PS[0][:, 256:512])
            v.engine_nop().then_inc(*pV.inc())

        @block.scalar
        def _(s, t=t):
            s.wait_ge(pT.h, pT.val)
            s.activation(stg[:, 384:580], PS[1][:, 0:196], AF.Identity)
            s.activation(kscr[:, 0:128], PS[0][:, 0:128], AF.Square).then_inc(*pS.inc())

        @block.vector
        def _(v, t=t):
            v.wait_ge(pS.h, pS.val)
            v.reduce_sum(tcol[:, 4:5], kscr[:, 0:128], axis=AX.X).then_inc(*nV.inc())
            v.wait_ge(nV.h, nV.val)
            v.tensor_scalar(tcol[:, 4:5], tcol[:, 4:5], 1.0 / HD, EPS_RMS,
                            ALU.mult, ALU.add).then_inc(*nV.inc())
            v.wait_ge(nV.h, nV.val)
            v.reciprocal(tcol[:, 4:5], tcol[:, 4:5])
            v.tensor_copy(wih_tm[:, t * 4:(t + 1) * 4], stg[:, 576:580])
            v.engine_nop().then_inc(*pV.inc())

        @block.scalar
        def _(s, t=t):
            s.wait_ge(pV.h, pV.val)
            s.activation(tcol[:, 5:6], tcol[:, 4:5], AF.Sqrt).then_inc(*pS.inc())

        @block.vector
        def _(v, t=t):
            v.wait_ge(pS.h, pS.val)
            kn = kscr[:, 128:256]; ko = kscr[:, 256:384]
            v.tensor_scalar(kn, stg[:, 0:128], tcol[:, 5:6], None, ALU.mult)
            v.tensor_mul(kn, kn, wk_b[:])
            rope7(v, ko, kn, cos_tm[:, t * 32:(t + 1) * 32],
                  sin_tm[:, t * 32:(t + 1) * 32], kscr[:, 384:512], nV)
            v.engine_nop().then_inc(*pV.inc())

        if PTRIM < 5:
            continue
        @block.tensor
        def _(tt, t=t):
            tt.wait_ge(pV.h, pV.val)
            tt.transpose(PS[6][:, 0:128], kscr[:, 256:384], ident[:])
            tt.transpose(PS[6][:, 128:256], stg[:, 128:256], ident[:])
            tt.transpose(PS[6][:, 256:384], stg[:, 256:384], ident[:])
            tt.transpose(PS[7][0:64, 0:128], stg[:, 384:448], ident[:])
            tt.transpose(PS[7][0:64, 128:256], stg[:, 448:512], ident[:])
            tt.transpose(PS[7][0:64, 256:384], stg[:, 512:576],
                         ident[:]).then_inc(*pT.inc())

        @block.vector
        def _(v, t=t):
            v.wait_ge(pT.h, pT.val)
            v.tensor_copy(k_fm[:, (t + 2) * 128:(t + 3) * 128], PS[6][:, 0:128])
            v.tensor_copy(kvcc[:, ts(t)], PS[6][:, 128:256])
            v.tensor_copy(kvcz[:, ts(t)], PS[6][:, 256:384])
            v.tensor_copy(idxc[0:64, ts(t)], PS[7][0:64, 0:128])
            v.tensor_copy(idxz[0:64, ts(t)], PS[7][0:64, 128:256])
            v.tensor_copy(qicb[0:64, ts(t)], PS[7][0:64, 256:384])
            v.engine_nop().then_inc(*pV.inc())


    if PTRIM < 6:
        return
    scr2 = k.sb["scr2"]

    @block.sync
    def _(sy):
        sy.wait_ge(k.c_sV.h, k.c_sV.val)
        sy.dma_start(out=scr2[0:64, 0:256],
                     in_=k.dr["L0_attn_w_iuq"].ap()).then_inc(*dW.inc(16))

    for hh in range(2):
        for g4 in range(4):
            pb = PS[2 + (g4 % 2)]

            @block.tensor
            def _(tt, hh=hh, g4=g4, pb=pb):
                tt.wait_ge(dW.h, dW.val)
                tt.wait_ge(pV.h, pV.val)
                tt.matmul(pb[:, 0:512], scr2[0:64, hh * 128:(hh + 1) * 128],
                          qicb[0:64, g4 * 512:(g4 + 1) * 512],
                          start=True, stop=True).then_inc(*pT.inc())

            @block.vector
            def _(v, hh=hh, g4=g4, pb=pb):
                v.wait_ge(pT.h, pT.val)
                v.tensor_copy(qi_fm[:, hh * T + g4 * 512:hh * T + (g4 + 1) * 512],
                              pb[:, 0:512])
                v.engine_nop().then_inc(*pV.inc())



def emit_compress(k, block):
    ar = k.arena
    ident = k.sb["ident"]
    scr3 = k.sb["scr3"]; scr4 = k.sb["scr4"]; scr = k.sb["scr"]
    scr2 = k.sb["scr2"]
    ckv_raw = k.sb["ckv_raw"]; kcompT = k.sb["kcompT"]
    PS = k.psN
    pV = k.sem("q_v"); pS = k.sem("q_s"); pT = k.sem("q_t")
    dB = k.sem("q_d")

    kvcc = ar[:, A_KVCC:A_KVCC + T]
    kvcz = ar[:, A_KVCZ:A_KVCZ + T]
    idxc = ar[0:64, A_IDXC:A_IDXC + T]
    idxz = ar[0:64, A_IDXZ:A_IDXZ + T]

    @block.sync
    def _(sy):
        sy.wait_ge(k.c_sV.h, k.c_sV.val)
        sy.dma_start(out=scr2[0:16, 288:416],
                     in_=k.dr["L0_attn_kvc_bias"].ap()).then_inc(*dB.inc(16))
        sy.dma_start(out=scr2[0:16, 416:480],
                     in_=k.dr["L0_attn_idx_c_bias"].ap()).then_inc(*dB.inc(16))

    @block.tensor
    def _(tt):
        tt.wait_ge(dB.h, dB.val)
        tt.wait_ge(k.p_pV.h, k.p_pV.val)
        tt.transpose(PS[6][:, 0:16], scr2[0:16, 288:416], ident[0:16, 0:16])
        tt.transpose(PS[6][0:64, 16:32], scr2[0:16, 416:480],
                     ident[0:16, 0:16]).then_inc(*pT.inc())

    @block.vector
    def _(v):
        v.wait_ge(pT.h, pT.val)
        v.tensor_copy(scr2[:, 480:496], PS[6][:, 0:16])
        v.tensor_copy(scr2[0:64, 496:512], PS[6][0:64, 16:32])
        v.engine_nop().then_inc(*pV.inc())

    def softmax_compress(zap, cap, bias_ap, out_ap, rows, tag):
        z3 = zap.rearrange("p (b m) -> p b m", m=M)
        c3 = cap.rearrange("p (b m) -> p b m", m=M)
        zc3 = scr3[0:rows, 0:T].rearrange("p (b m) -> p b m", m=M)
        e3 = scr4[0:rows, 0:T].rearrange("p (b m) -> p b m", m=M)
        ec3 = scr3[0:rows, 0:T].rearrange("p (b m) -> p b m", m=M)
        zmax = scr[0:rows, 288:416]
        esum = scr[0:rows, 416:544]
        secs = scr[0:rows, 544:672]

        @block.vector
        def _(v):
            v.wait_ge(pV.h, pV.val)
            v.tensor_add(z3, z3, bias_ap.unsqueeze(1).to_broadcast(
                [rows, NB, M]))
            v.reduce_max(zmax, z3, axis=AX.X)
            v.tensor_sub(zc3, z3, zmax.unsqueeze(2).to_broadcast([rows, NB, M]))
            v.engine_nop().then_inc(*pV.inc())

        @block.scalar
        def _(s):
            s.wait_ge(pV.h, pV.val)
            s.activation(scr4[0:rows, 0:T], scr3[0:rows, 0:T], AF.Exp).then_inc(*pS.inc())

        @block.vector
        def _(v):
            v.wait_ge(pS.h, pS.val)
            v.reduce_sum(esum, e3, axis=AX.X)
            v.tensor_mul(ec3, e3, c3)
            v.reduce_sum(secs, ec3, axis=AX.X)
            v.reciprocal(esum, esum)
            v.tensor_mul(out_ap, secs, esum)
            v.engine_nop().then_inc(*pV.inc())

    softmax_compress(kvcz, kvcc, scr2[:, 480:496], ckv_raw[:], 128, "kvc")
    softmax_compress(idxz, idxc, scr2[0:64, 496:512], kcompT[0:64, :], 64, "idx")

    kcompT2 = k.sb["kcompT2"]
    dDup = k.sem("q_dup")

    @block.gpsimd
    def _(g):
        g.wait_ge(pV.h, pV.val)
        g.memset(kcompT2[:], 0.0).then_inc(*k.gsem.inc())

    @block.sync
    def _(sy):
        sy.wait_ge(pV.h, pV.val)
        sy.dma_start(out=kcompT[64:128, :], in_=kcompT[0:64, :]).then_inc(*dDup.inc(16))

    @block.vector
    def _(v):
        v.wait_ge(dDup.h, dDup.val)
        v.wait_ge(k.gsem.h, k.gsem.val)
        v.tensor_copy(kcompT2[0:64, 0:128], kcompT[0:64, :])
        v.tensor_copy(kcompT2[64:128, 128:256], kcompT[64:128, :])
        v.engine_nop().then_inc(*pV.inc())

    k.q_pV = pV






A_QST = 0
A_WQC = 16384
A_WQUP = 24576
A_GW = 16384
A_BSC = 24576


def emit_attention(k, block):
    ident = k.sb["ident"]
    wk_b = k.sb["wk_b"]
    ckv_raw = k.sb["ckv_raw"]; ckv_V = k.sb["ckv_V"]; ckv_K = k.sb["ckv_K"]
    tcol = k.sb["colscr"]
    scr = k.sb["scr"]
    PS = k.psN

    aV = k.sem("a_v"); aS = k.sem("a_s"); aT = k.sem("a_t")
    nA = k.sem("a_nv"); dA = k.sem("a_d"); aG = k.sem("a_g")

    @block.tensor
    def _(tt):
        tt.wait_ge(k.q_pV.h, k.q_pV.val)
        tt.transpose(PS[6][:, 0:128], ckv_raw[:], ident[:]).then_inc(*aT.inc())

    @block.vector
    def _(v):
        v.wait_ge(aT.h, aT.val)
        v.tensor_copy(scr[:, 0:128], PS[6][:, 0:128])
        v.engine_nop().then_inc(*aV.inc())

    @block.scalar
    def _(s):
        s.wait_ge(aV.h, aV.val)
        s.activation(scr[:, 128:256], scr[:, 0:128], AF.Square).then_inc(*aS.inc())

    @block.vector
    def _(v):
        v.wait_ge(aS.h, aS.val)
        v.reduce_sum(tcol[:, 8:9], scr[:, 128:256], axis=AX.X).then_inc(*nA.inc())
        v.wait_ge(nA.h, nA.val)
        v.tensor_scalar(tcol[:, 8:9], tcol[:, 8:9], 1.0 / HD, EPS_RMS,
                        ALU.mult, ALU.add).then_inc(*nA.inc())
        v.wait_ge(nA.h, nA.val)
        v.reciprocal(tcol[:, 8:9], tcol[:, 8:9])
        v.engine_nop().then_inc(*aV.inc())

    @block.scalar
    def _(s):
        s.wait_ge(aV.h, aV.val)
        s.activation(tcol[:, 9:10], tcol[:, 8:9], AF.Sqrt).then_inc(*aS.inc())

    @block.vector
    def _(v):
        v.wait_ge(aS.h, aS.val)
        v.tensor_scalar(ckv_V[:], scr[:, 0:128], tcol[:, 9:10], None, ALU.mult)
        v.tensor_mul(ckv_V[:], ckv_V[:], wk_b[:])
        v.engine_nop().then_inc(*aV.inc())

    @block.tensor
    def _(tt):
        tt.wait_ge(aV.h, aV.val)
        tt.transpose(PS[6][:, 128:256], ckv_V[:], ident[:]).then_inc(*aT.inc())

    @block.scalar
    def _(s):
        s.wait_ge(aT.h, aT.val)
        s.activation(ckv_K[:], PS[6][:, 128:256],
                     AF.Identity).then_inc(*aS.inc())

    k.b_done = [k.sem("b_done0"), k.sem("b_done1")]
    mode = getattr(k, "attn_mode", "full")
    if mode == "A0":
        emit_attn_A(k, block, 0, (aV, aS, aT, nA, dA, aG))
        k.fin_aV = aV; k.fin_dA = dA
        return
    if mode == "B0":
        emit_attn_A(k, block, 0, (aV, aS, aT, nA, dA, aG))
        emit_attn_B(k, block, 0, (aV, aS, aT, nA, dA, aG))
        k.fin_aV = aV; k.fin_dA = dA
        return
    for half in range(2):
        emit_attn_A(k, block, half, (aV, aS, aT, nA, dA, aG))
        emit_attn_B(k, block, half, (aV, aS, aT, nA, dA, aG))
    emit_attn_final(k, block, (aV, aS, aT, nA, dA, aG))


def emit_attn_A(k, block, half, sems):
    aV, aS, aT, nA, dA, aG = sems
    ar = k.arena
    ident = k.sb["ident"]
    cos_tm = k.sb["cos_tm"]; sin_tm = k.sb["sin_tm"]
    wqs_b = k.sb["wqs_b"]
    rstd1 = k.sb["rstd1"]; nmr1 = k.sb["nmr1"]
    tcol = k.sb["colscr"]
    scr2 = k.sb["scr2"]; scr3 = k.sb["scr3"]; scr4 = k.sb["scr4"]
    PS = k.psN
    x = k.dr["x"]

    wqc = ar[:, A_WQC:A_WQC + 8192]
    wqup = ar[:, A_WQUP:A_WQUP + 8192]
    xlnb = scr3[:, 0:D]
    xfm = scr4[:, 0:D]

    @block.sync
    def _(sy, half=half):
        if half == 0:
            sy.wait_ge(k.q_pV.h, k.q_pV.val)
        else:
            sy.wait_ge(k.b_done[0].h, k.b_done[0].val)
        for c in range(NCH):
            sy.dma_start(out=wqc[:, c * 512:(c + 1) * 512],
                         in_=k.dr["L0_attn_w_qc"].ap()[ts(c), :]).then_inc(*dA.inc(16))
        for q in range(4):
            sy.dma_start(out=wqup[:, q * 2048:(q + 1) * 2048],
                         in_=k.dr["L0_attn_w_qup"].ap()[ts(q), :]).then_inc(*dA.inc(16))

    for tl in range(8):
        t = half * 8 + tl

        @block.sync
        def _(sy, t=t):
            sy.wait_ge(aV.h, aV.val)
            sy.dma_start(out=k.sb["scr3"][:, 0:D],
                         in_=x.ap()[ts(t), :]).then_inc(*dA.inc(16))

        @block.scalar
        def _(s, t=t):
            s.wait_ge(dA.h, dA.val)
            s.activation(xlnb, xlnb, AF.Identity, bias=nmr1[:, t:t + 1],
                         scale=rstd1[:, t:t + 1]).then_inc(*aS.inc())

        @block.vector
        def _(v):
            v.wait_ge(aS.h, aS.val)
            v.engine_nop().then_inc(*aV.inc())

        emit_transpose16(k, block, xlnb, lambda c: xfm[:, ts(c)], aT, aV)

        @block.tensor
        def _(tt, t=t):
            tt.wait_ge(aV.h, aV.val)
            tt.wait_ge(dA.h, dA.val)
            for c in range(NCH):
                ins = tt.matmul(PS[0][:, 0:512], xfm[:, ts(c)],
                                wqc[:, c * 512:(c + 1) * 512],
                                start=(c == 0), stop=(c == NCH - 1))
                if c == NCH - 1:
                    ins.then_inc(*aT.inc())

        @block.vector
        def _(v):
            v.wait_ge(aT.h, aT.val)
            v.tensor_copy(scr2[:, 0:512], PS[0][:, 0:512])
            v.engine_nop().then_inc(*aV.inc())

        @block.tensor
        def _(tt):
            tt.wait_ge(aV.h, aV.val)
            for j in range(4):
                ins = tt.transpose(PS[6][:, ts(j)],
                                   scr2[:, j * 128:(j + 1) * 128], ident[:])
                if j == 3:
                    ins.then_inc(*aT.inc())

        @block.vector
        def _(v):
            v.wait_ge(aT.h, aT.val)
            v.tensor_copy(scr2[:, 512:1024], PS[6][:, 0:512])
            v.engine_nop().then_inc(*aV.inc())

        @block.tensor
        def _(tt):
            tt.wait_ge(aV.h, aV.val)
            for qc in range(4):
                for g4 in range(4):
                    ins = tt.matmul(
                        PS[2 + g4][:, 0:512],
                        scr2[:, 512 + qc * 128:512 + (qc + 1) * 128],
                        wqup[:, qc * 2048 + g4 * 512:qc * 2048 + (g4 + 1) * 512],
                        start=(qc == 0), stop=(qc == 3))
                    if qc == 3 and g4 == 3:
                        ins.then_inc(*aT.inc())

        @block.vector
        def _(v):
            v.wait_ge(aT.h, aT.val)
            for g4 in range(4):
                v.tensor_copy(scr3[:, g4 * 512:(g4 + 1) * 512],
                              PS[2 + g4][:, 0:512])
            v.engine_nop().then_inc(*aV.inc())

        @block.scalar
        def _(s):
            s.wait_ge(aV.h, aV.val)
            s.activation(scr4[:, 0:D], scr3[:, 0:D], AF.Square).then_inc(*aS.inc())

        @block.vector
        def _(v):
            v.wait_ge(aS.h, aS.val)
            sq3 = scr4[:, 0:D].rearrange("p (h d) -> p h d", d=HD)
            v.reduce_sum(tcol[:, 10:26], sq3, axis=AX.X).then_inc(*nA.inc())
            v.wait_ge(nA.h, nA.val)
            v.tensor_scalar(tcol[:, 10:26], tcol[:, 10:26], 1.0 / HD, EPS_RMS,
                            ALU.mult, ALU.add).then_inc(*nA.inc())
            v.wait_ge(nA.h, nA.val)
            v.reciprocal(tcol[:, 10:26], tcol[:, 10:26])
            v.engine_nop().then_inc(*aV.inc())

        @block.scalar
        def _(s):
            s.wait_ge(aV.h, aV.val)
            s.activation(tcol[:, 10:26], tcol[:, 10:26], AF.Sqrt).then_inc(*aS.inc())

        @block.vector
        def _(v, t=t):
            v.wait_ge(aS.h, aS.val)
            q3 = scr3[:, 0:D].rearrange("p (h d) -> p h d", d=HD)
            o3 = scr4[:, 0:D].rearrange("p (h d) -> p h d", d=HD)
            cc = cos_tm[:, t * 32:(t + 1) * 32].unsqueeze(1).to_broadcast(
                [128, H, 32])
            ss = sin_tm[:, t * 32:(t + 1) * 32].unsqueeze(1).to_broadcast(
                [128, H, 32])
            t3a = scr2[:, 0:512].rearrange("p (h d) -> p h d", d=32)
            t3b = scr2[:, 512:1024].rearrange("p (h d) -> p h d", d=32)
            t3c = k.sb["scr"][:, 0:512].rearrange("p (h d) -> p h d", d=32)
            t3d = k.sb["scr"][:, 512:1024].rearrange("p (h d) -> p h d", d=32)
            v.tensor_mul(t3a, q3[:, :, 0:32], cc)
            v.tensor_mul(t3b, q3[:, :, 32:64], ss)
            v.tensor_mul(t3c, q3[:, :, 0:32], ss)
            v.tensor_mul(t3d, q3[:, :, 32:64], cc).then_inc(*nA.inc())
            v.wait_ge(nA.h, nA.val)
            v.tensor_sub(o3[:, :, 0:32], t3a, t3b)
            v.tensor_add(o3[:, :, 32:64], t3c, t3d)
            v.tensor_copy(o3[:, :, 64:128], q3[:, :, 64:128]).then_inc(*nA.inc())
            v.wait_ge(nA.h, nA.val)
            v.tensor_mul(o3, o3, tcol[:, 10:26].unsqueeze(2).to_broadcast(
                [128, H, HD])).then_inc(*nA.inc())
            v.wait_ge(nA.h, nA.val)
            v.tensor_mul(o3, o3, wqs_b[:].unsqueeze(1).to_broadcast(
                [128, H, HD]))
            v.engine_nop().then_inc(*aV.inc())

        for grp in range(4):
            pb = PS[6 + (grp % 2)]

            @block.tensor
            def _(tt, grp=grp, pb=pb):
                tt.wait_ge(aV.h, aV.val)
                for j in range(4):
                    c = grp * 4 + j
                    ins = tt.transpose(pb[:, ts(j)], scr4[:, ts(c)], ident[:])
                    if j == 3:
                        ins.then_inc(*aT.inc())

            @block.vector
            def _(v, grp=grp, pb=pb, tl=tl):
                v.wait_ge(aT.h, aT.val)
                base = A_QST + tl * 2048
                v.tensor_copy(ar[:, base + grp * 512:base + (grp + 1) * 512],
                              pb[:, 0:512])
                v.engine_nop().then_inc(*aV.inc())


def emit_attn_B(k, block, half, sems):
    b_tiles = getattr(k, "b_tiles", 8)
    b_stage = getattr(k, "b_stage", 4)
    b_skip = getattr(k, "b_skip", set())
    aV, aS, aT, nA, dA, aG = sems
    ar = k.arena
    ident = k.sb["ident"]
    k_fm = k.sb["k_fm"]; v_tm = k.sb["v_tm"]
    kcompT = k.sb["kcompT"]; ckv_V = k.sb["ckv_V"]; ckv_K = k.sb["ckv_K"]
    qi_fm = k.sb["qi_fm"]; wih_tm = k.sb["wih_tm"]
    wmask = k.sb["wmask"]
    tcol = k.sb["colscr"]
    scr = k.sb["scr"]; scr2 = k.sb["scr2"]; scr3 = k.sb["scr3"]
    PS = k.psN

    gw = ar[:, A_GW:A_GW + 8192]
    attnb = ar[:, A_BSC:A_BSC + 2048]
    groupedb = ar[:, A_BSC + 2048:A_BSC + 4096]
    gfm = ar[:, A_BSC + 4096:A_BSC + 6144]

    if "gw" not in b_skip:
        @block.sync
        def _(sy, half=half):
            sy.wait_ge(aV.h, aV.val)
            for g in range(G):
                for ic in range(4):
                    sy.dma_start(
                        out=gw[:, (g * 4 + ic) * 512:(g * 4 + ic + 1) * 512],
                        in_=k.dr["L0_attn_group_w"].ap()[g, ic * 128:(ic + 1) * 128, :]
                    ).then_inc(*dA.inc(16))

    for tl in range(b_tiles):
        t = half * 8 + tl
        win = t // 2
        qbase = A_QST + tl * 2048

        if "mask" not in b_skip:
            @block.gpsimd
            def _(g, t=t):
                g.wait_ge(aV.h, aV.val)
                g.memset(scr[:, 640:768], 1.0).then_inc(*aG.inc())
                g.wait_ge(aG.h, aG.val)
                g.affine_select(out=scr[:, 640:768], in_=scr[:, 640:768],
                                pattern=[[-16, 128]], compare_op=ALU.is_gt,
                                fill=0.0, base=t * 128 - 15,
                                channel_multiplier=1).then_inc(*aG.inc())

        if "hs" in b_skip:
            continue
        @block.tensor
        def _(tt, t=t):
            tt.wait_ge(aV.h, aV.val)
            tt.wait_ge(dA.h, dA.val)
            for ih in range(4):
                lhs = qi_fm[:, (ih // 2) * T + t * 128:(ih // 2) * T + (t + 1) * 128]
                rhsv = k.sb["kcompT2"][:, (ih % 2) * 128:(ih % 2 + 1) * 128]
                ins = tt.matmul(PS[0][:, ih * 128:(ih + 1) * 128], lhs,
                                rhsv, start=True, stop=True)
                if ih == 3:
                    ins.then_inc(*aT.inc())

        if "isc" in b_skip:
            continue
        @block.vector
        def _(v, t=t):
            v.wait_ge(aT.h, aT.val)
            v.wait_ge(aG.h, aG.val)
            isc = scr2[:, 0:128]
            for ih in range(4):
                wb = wih_tm[:, t * 4 + ih:t * 4 + ih + 1].to_broadcast([128, 128])
                if ih == 0:
                    v.scalar_tensor_tensor(isc, PS[0][:, 0:128], 0.0, wb,
                                           ALU.max, ALU.mult)
                else:
                    v.scalar_tensor_tensor(scr2[:, 128:256],
                                           PS[0][:, ih * 128:(ih + 1) * 128],
                                           0.0, wb, ALU.max, ALU.mult)
                    v.tensor_add(isc, isc, scr2[:, 128:256])
            v.tensor_scalar(scr2[:, 256:384], scr[:, 640:768], 1e30, -1e30,
                            ALU.mult, ALU.add)
            v.tensor_add(isc, isc, scr2[:, 256:384])
            v.engine_nop().then_inc(*aV.inc())

        @block.vector
        def _(v, t=t):
            sbuf_a = scr2[:, 0:128]
            sbuf_b = scr2[:, 128:256]
            m8 = tcol[:, 16:24]
            if t >= 8:
                cur, oth = sbuf_a, sbuf_b
                for it in range(8):
                    v.max(out=m8, in_=cur).then_inc(*nA.inc())
                    v.wait_ge(nA.h, nA.val)
                    v.match_replace(out=oth, in_to_replace=m8, in_values=cur,
                                    imm_value=SENT).then_inc(*nA.inc())
                    v.wait_ge(nA.h, nA.val)
                    cur, oth = oth, cur
                v.tensor_scalar(scr2[:, 384:512], cur, SENT, None,
                                ALU.is_equal).then_inc(*nA.inc())
                v.wait_ge(nA.h, nA.val)
                v.tensor_mul(scr2[:, 384:512], scr2[:, 384:512], scr[:, 640:768])
            else:
                v.tensor_copy(scr2[:, 384:512], scr[:, 640:768])
            v.tensor_scalar(scr2[:, 256:384], scr2[:, 384:512], 1e30, -1e30,
                            ALU.mult, ALU.add)
            v.reduce_max(tcol[:, 26:27], scr2[:, 384:512], axis=AX.X)
            v.engine_nop().then_inc(*aV.inc())

        if b_stage < 2:
            continue
        for h in range(H if b_stage >= 3 else 1):
            bank = PS[2 + h // 4]
            bcol = (h % 4) * 128

            @block.tensor
            def _(tt, h=h, qbase=qbase):
                tt.wait_ge(aV.h, aV.val)
                tt.matmul(PS[0][:, 0:128],
                          ar[:, qbase + h * 128:qbase + (h + 1) * 128],
                          ckv_K[:], start=True, stop=True).then_inc(*aT.inc())

            @block.vector
            def _(v, h=h):
                v.wait_ge(aT.h, aT.val)
                smb = scr2[:, 512:640]
                v.tensor_add(smb, PS[0][:, 0:128], scr2[:, 256:384])
                v.reduce_max(tcol[:, 27:28], smb, axis=AX.X, negate=True)
                v.engine_nop().then_inc(*aV.inc())

            @block.scalar
            def _(s, h=h):
                s.wait_ge(aV.h, aV.val)
                s.activation(scr2[:, 640:768], scr2[:, 512:640], AF.Exp,
                             bias=tcol[:, 27:28],
                             accum_out=tcol[:, 28:29]).then_inc(*aS.inc())

            @block.vector
            def _(v, h=h):
                v.wait_ge(aS.h, aS.val)
                v.reciprocal(tcol[:, 28:29], tcol[:, 28:29]).then_inc(*nA.inc())
                v.wait_ge(nA.h, nA.val)
                v.tensor_mul(tcol[:, 28:29], tcol[:, 28:29],
                             tcol[:, 26:27]).then_inc(*nA.inc())
                v.wait_ge(nA.h, nA.val)
                v.tensor_scalar(scr2[:, 768:896], scr2[:, 640:768],
                                tcol[:, 28:29], None, ALU.mult)
                v.engine_nop().then_inc(*aV.inc())

            @block.tensor
            def _(tt, h=h):
                tt.wait_ge(aV.h, aV.val)
                tt.transpose(PS[1][:, 0:128], scr2[:, 768:896],
                             ident[:]).then_inc(*aT.inc())

            @block.vector
            def _(v, h=h):
                v.wait_ge(aT.h, aT.val)
                v.tensor_copy(scr2[:, 896:1024], PS[1][:, 0:128])
                v.engine_nop().then_inc(*aV.inc())

            @block.tensor
            def _(tt, h=h, win=win, bank=bank, bcol=bcol, qbase=qbase):
                tt.wait_ge(aV.h, aV.val)
                tt.wait_ge(aS.h, aS.val)
                tt.matmul(bank[:, bcol:bcol + 128], ckv_V[:],
                          scr2[:, 896:1024], start=(h % 4 == 0), stop=False)
                tt.matmul(PS[6][:, 0:512],
                          ar[:, qbase + h * 128:qbase + (h + 1) * 128],
                          k_fm[:, win * 256:win * 256 + 512],
                          start=True, stop=True).then_inc(*aT.inc())

            @block.vector
            def _(v, h=h):
                v.wait_ge(aT.h, aT.val)
                v.reduce_max(tcol[:, 29:30], PS[6][:, 0:512], axis=AX.X,
                             negate=True)
                v.engine_nop().then_inc(*aV.inc())

            @block.scalar
            def _(s, h=h):
                s.wait_ge(aV.h, aV.val)
                s.activation(scr3[:, 0:512], PS[6][:, 0:512], AF.Exp,
                             bias=tcol[:, 29:30]).then_inc(*aS.inc())

            @block.vector
            def _(v, h=h, t=t, win=win):
                v.wait_ge(aS.h, aS.val)
                mvar = (t % 2) * 2 + (1 if win == 0 else 0)
                v.tensor_mul(scr3[:, 0:512], scr3[:, 0:512],
                             wmask[:, mvar * 512:(mvar + 1) * 512]).then_inc(*nA.inc())
                v.wait_ge(nA.h, nA.val)
                v.reduce_sum(tcol[:, 30:31], scr3[:, 0:512],
                             axis=AX.X).then_inc(*nA.inc())
                v.wait_ge(nA.h, nA.val)
                v.reciprocal(tcol[:, 30:31], tcol[:, 30:31]).then_inc(*nA.inc())
                v.wait_ge(nA.h, nA.val)
                v.tensor_scalar(scr3[:, 512:1024], scr3[:, 0:512],
                                tcol[:, 30:31], None, ALU.mult)
                v.engine_nop().then_inc(*aV.inc())

            @block.tensor
            def _(tt, h=h):
                tt.wait_ge(aV.h, aV.val)
                for j in range(4):
                    ins = tt.transpose(PS[1][:, ts(j)],
                                       scr3[:, 512 + j * 128:512 + (j + 1) * 128],
                                       ident[:])
                    if j == 3:
                        ins.then_inc(*aT.inc())

            @block.vector
            def _(v, h=h):
                v.wait_ge(aT.h, aT.val)
                v.tensor_copy(scr3[:, 1024:1536], PS[1][:, 0:512])
                v.engine_nop().then_inc(*aV.inc())

            @block.tensor
            def _(tt, h=h, win=win, bank=bank, bcol=bcol):
                tt.wait_ge(aV.h, aV.val)
                for kc in range(4):
                    ins = tt.matmul(
                        bank[:, bcol:bcol + 128],
                        v_tm[:, (win * 2 + kc) * 128:(win * 2 + kc + 1) * 128],
                        scr3[:, 1024 + kc * 128:1024 + (kc + 1) * 128],
                        start=False, stop=(h % 4 == 3 and kc == 3))
                    if kc == 3:
                        ins.then_inc(*aT.inc())

        if b_stage < 4:
            continue
        @block.vector
        def _(v, t=t):
            v.wait_ge(aT.h, aT.val)
            for g4 in range(4):
                v.tensor_copy(attnb[:, g4 * 512:(g4 + 1) * 512],
                              PS[2 + g4][:, 0:512])
            v.engine_nop().then_inc(*aV.inc())

        @block.tensor
        def _(tt, t=t):
            tt.wait_ge(aV.h, aV.val)
            for g in range(G):
                for ic in range(4):
                    ins = tt.matmul(
                        PS[2 + g][:, 0:512],
                        attnb[:, (g * 4 + ic) * 128:(g * 4 + ic + 1) * 128],
                        gw[:, (g * 4 + ic) * 512:(g * 4 + ic + 1) * 512],
                        start=(ic == 0), stop=(ic == 3))
                    if g == 3 and ic == 3:
                        ins.then_inc(*aT.inc())

        @block.vector
        def _(v, t=t):
            v.wait_ge(aT.h, aT.val)
            for g4 in range(4):
                v.tensor_copy(groupedb[:, g4 * 512:(g4 + 1) * 512],
                              PS[2 + g4][:, 0:512])
            v.engine_nop().then_inc(*aV.inc())

        for grp in range(4):
            pb = PS[6 + (grp % 2)]

            @block.tensor
            def _(tt, grp=grp, pb=pb):
                tt.wait_ge(aV.h, aV.val)
                for j in range(4):
                    c = grp * 4 + j
                    ins = tt.transpose(pb[:, ts(j)], groupedb[:, ts(c)], ident[:])
                    if j == 3:
                        ins.then_inc(*aT.inc())

            @block.vector
            def _(v, grp=grp, pb=pb):
                v.wait_ge(aT.h, aT.val)
                v.tensor_copy(gfm[:, grp * 512:(grp + 1) * 512], pb[:, 0:512])
                v.engine_nop().then_inc(*aV.inc())

        @block.sync
        def _(sy, t=t):
            sy.wait_ge(aV.h, aV.val)
            for c in range(NCH):
                sy.dma_start(out=k.dr["grp_hbm"].ap()[ts(c), ts(t)],
                             in_=gfm[:, ts(c)]).then_inc(*dA.inc(16))

    @block.vector
    def _(v, half=half):
        v.wait_ge(dA.h, dA.val)
        v.engine_nop().then_inc(*k.b_done[half].inc())


def emit_attn_final(k, block, sems):
    aV, aS, aT, nA, dA, aG = sems
    ar = k.arena
    nmean = k.sb["nmean"]; rstd2 = k.sb["rstd2"]; nmr2 = k.sb["nmr2"]
    tcol = k.sb["colscr"]
    scr2 = k.sb["scr2"]; scr3 = k.sb["scr3"]; scr4 = k.sb["scr4"]
    PS = k.psN
    x = k.dr["x"]

    fw = ar[:, 0:ARENA]

    @block.sync
    def _(sy):
        sy.wait_ge(k.b_done[1].h, k.b_done[1].val)
        for kc in range(NCH):
            sy.dma_start(out=fw[:, kc * 2048:(kc + 1) * 2048],
                         in_=k.dr["L0_attn_final_w"].ap()[ts(kc), :]).then_inc(*dA.inc(16))

    for t in range(NT):
        @block.sync
        def _(sy, t=t):
            if t > 0:
                sy.wait_ge(aV.h, aV.val)
            sy.wait_ge(dA.h, dA.val)
            for c in range(NCH):
                sy.dma_start(out=scr3[:, ts(c)],
                             in_=k.dr["grp_hbm"].ap()[ts(c), ts(t)]).then_inc(*dA.inc(16))
            sy.dma_start(out=scr4[:, 0:D], in_=x.ap()[ts(t), :]).then_inc(*dA.inc(16))

        @block.tensor
        def _(tt, t=t):
            tt.wait_ge(dA.h, dA.val)
            if t > 0:
                tt.wait_ge(aV.h, aV.val)
            for kc in range(NCH):
                for g4 in range(4):
                    ins = tt.matmul(
                        PS[0 + g4][:, 0:512], scr3[:, ts(kc)],
                        fw[:, kc * 2048 + g4 * 512:kc * 2048 + (g4 + 1) * 512],
                        start=(kc == 0), stop=(kc == NCH - 1))
                    if kc == NCH - 1 and g4 == 3:
                        ins.then_inc(*aT.inc())

        @block.vector
        def _(v, t=t):
            v.wait_ge(aT.h, aT.val)
            for g4 in range(4):
                v.tensor_add(scr3[:, g4 * 512:(g4 + 1) * 512],
                             PS[0 + g4][:, 0:512],
                             scr4[:, g4 * 512:(g4 + 1) * 512])
            v.reduce_sum(tcol[:, 0:1], scr3[:, 0:D], axis=AX.X)
            v.engine_nop().then_inc(*aV.inc())

        @block.scalar
        def _(s, t=t):
            s.wait_ge(aV.h, aV.val)
            s.activation(scr4[:, 0:D], scr3[:, 0:D], AF.Square).then_inc(*aS.inc())

        @block.sync
        def _(sy, t=t):
            sy.wait_ge(aS.h, aS.val)
            sy.dma_start(out=k.dr["x2_hbm"].ap()[ts(t), :],
                         in_=scr3[:, 0:D]).then_inc(*dA.inc(16))

        @block.vector
        def _(v, t=t):
            v.wait_ge(aS.h, aS.val)
            v.reduce_sum(tcol[:, 1:2], scr4[:, 0:D], axis=AX.X).then_inc(*nA.inc())
            v.wait_ge(nA.h, nA.val)
            v.tensor_scalar(nmean[:, t:t + 1], tcol[:, 0:1], -1.0 / D, None,
                            ALU.mult).then_inc(*nA.inc())
            v.wait_ge(nA.h, nA.val)
            v.tensor_mul(tcol[:, 2:3], nmean[:, t:t + 1],
                         nmean[:, t:t + 1]).then_inc(*nA.inc())
            v.wait_ge(nA.h, nA.val)
            v.scalar_tensor_tensor(tcol[:, 3:4], tcol[:, 1:2], 1.0 / D,
                                   tcol[:, 2:3], ALU.mult,
                                   ALU.subtract).then_inc(*nA.inc())
            v.wait_ge(nA.h, nA.val)
            v.tensor_scalar(tcol[:, 3:4], tcol[:, 3:4], EPS_LN, None,
                            ALU.add).then_inc(*nA.inc())
            v.wait_ge(nA.h, nA.val)
            v.reciprocal(tcol[:, 3:4], tcol[:, 3:4])
            v.engine_nop().then_inc(*aV.inc())

        @block.scalar
        def _(s, t=t):
            s.wait_ge(aV.h, aV.val)
            s.activation(rstd2[:, t:t + 1], tcol[:, 3:4], AF.Sqrt).then_inc(*aS.inc())

        @block.vector
        def _(v, t=t):
            v.wait_ge(aS.h, aS.val)
            v.tensor_mul(nmr2[:, t:t + 1], nmean[:, t:t + 1], rstd2[:, t:t + 1])
            v.engine_nop().then_inc(*aV.inc())

    k.fin_aV = aV
    k.fin_dA = dA





SQ2048 = math.sqrt(2048.0)
SQ8192 = math.sqrt(8192.0)


def emit_ffn_tern(k, block):
    fV, fS, fT, nF, fD, fG = k.f_sems
    fW = k.fW
    ar = k.arena
    fscal = k.sb["fscal"]
    B_IN, B_TMP = 16384, 24576
    NTB = 4

    jobs = []
    for i in range(16):
        for j in range(4):
            jobs.append((k.dr["L0_ffn_w_gate"].ap()[ts(i), j * 2048:(j + 1) * 2048],
                         0, ("tern_g", i, j * 4)))
    for i in range(16):
        for j in range(4):
            jobs.append((k.dr["L0_ffn_w_up"].ap()[ts(i), j * 2048:(j + 1) * 2048],
                         1, ("tern_u", i, j * 4)))
    for i in range(64):
        jobs.append((k.dr["L0_ffn_w_down"].ap()[ts(i), :],
                     2, ("tern_d", i, 0)))

    vmark = []
    wmark = []
    smark = []
    xq_done = fV.val
    for nb, (srcap, mi, outs) in enumerate(jobs):
        b = nb % NTB
        inb = ar[:, B_IN + b * 2048:B_IN + (b + 1) * 2048]
        tmp = ar[:, B_TMP + b * 2048:B_TMP + (b + 1) * 2048]
        out_bf = ar[:, B_IN + b * 2048:B_IN + b * 2048 + 1024].bitcast(BF16)

        def _ldbody(eng, nb=nb, srcap=srcap, inb=inb):
            if nb <= 1:
                eng.wait_ge(fV.h, xq_done)
            if nb >= NTB:
                eng.wait_ge(fS.h, smark[nb - NTB])
                eng.wait_ge(fW.h, wmark[nb - NTB])
            eng.dma_start(out=inb, in_=srcap).then_inc(*fD.inc(16))

        if nb % 2 == 0:
            @block.sync
            def _(sy, body=_ldbody):
                body(sy)
        else:
            @block.scalar
            def _(sc, body=_ldbody):
                body(sc)
        dmk = fD.val

        @block.vector
        def _(v, inb=inb, tmp=tmp, mi=mi, dmk=dmk, nb=nb):
            v.wait_ge(fD.h, dmk)
            if nb >= NTB:
                v.wait_ge(fS.h, smark[nb - NTB])
            v.tensor_scalar(tmp, inb, fscal[:, mi:mi + 1], None, ALU.is_gt)
            v.tensor_scalar(inb, inb, fscal[:, 6 + mi:7 + mi], None, ALU.is_ge)
            v.tensor_tensor(tmp, tmp, inb, ALU.add)
            v.engine_nop().then_inc(*fV.inc())
        vmark.append(fV.val)

        @block.scalar
        def _(s, tmp=tmp, out_bf=out_bf, nb=nb):
            s.wait_ge(fV.h, vmark[nb])
            if nb >= NTB:
                s.wait_ge(fW.h, wmark[nb - NTB])
            s.activation(out_bf, tmp, AF.Identity,
                         bias=-1.0).then_inc(*fS.inc())
        smark.append(fS.val)

        def _stbody(eng, out_bf=out_bf, outs=outs, nb=nb):
            eng.wait_ge(fS.h, smark[nb])
            tname, i0, j4 = outs
            eng.dma_start(
                out=k.dr[tname].ap()[i0, j4:j4 + 4].rearrange("b p c -> p b c"),
                in_=out_bf.rearrange("p (b c) -> p b c", c=512)
            ).then_inc(*fW.inc(16))

        if nb % 2 == 0:
            @block.scalar
            def _(sc, body=_stbody):
                body(sc)
        else:
            @block.sync
            def _(sy, body=_stbody):
                body(sy)
        wmark.append(fW.val)


def emit_ffn_gateup(k, block):
    fV, fS, fT, nF, fD, fG = k.f_sems
    fW = k.fW
    ar = k.arena
    e0row = k.sb["e0row"]; h128 = k.sb["h128"]; ident = k.sb["ident"]
    sigma1 = k.sb["sigma1"]; fscal = k.sb["fscal"]
    scr = k.sb["scr"]; scr2 = k.sb["scr2"]; scr3 = k.sb["scr3"]; scr4 = k.sb["scr4"]
    PS = k.psN
    xq_bf = ar[:, 0:16384].bitcast(BF16)

    rowmark = []
    for g in range(4):
        @block.tensor
        def _(tt, g=g):
            if g == 0:
                tt.wait_ge(fV.h, fV.val)
            else:
                tt.wait_ge(fV.h, rowmark[g - 1])
            for j in range(4):
                ins = tt.transpose(PS[7][0:1, j * 128:(j + 1) * 128],
                                   sigma1[:, g * 4 + j:g * 4 + j + 1], ident[:])
            ins.then_inc(*fT.inc())

        @block.vector
        def _(v, g=g):
            v.wait_ge(fT.h, fT.val)
            v.tensor_copy(scr3[0:1, g * 512:(g + 1) * 512], PS[7][0:1, 0:512])
            v.engine_nop().then_inc(*fV.inc())
        rowmark.append(fV.val)

    @block.tensor
    def _(tt):
        tt.wait_ge(fV.h, fV.val)
        for q in range(4):
            ins = tt.matmul(PS[4 + q][:, 0:512], e0row[:],
                            scr3[0:128, q * 512:(q + 1) * 512],
                            start=True, stop=True)
        ins.then_inc(*fT.inc())

    @block.vector
    def _(v):
        v.wait_ge(fT.h, fT.val)
        for q in range(4):
            v.tensor_scalar(scr4[:, q * 512:(q + 1) * 512], PS[4 + q][:, 0:512],
                            fscal[:, 4:5], None, ALU.mult)
        for q in range(4):
            v.tensor_scalar(scr3[:, q * 512:(q + 1) * 512], PS[4 + q][:, 0:512],
                            fscal[:, 3:4], None, ALU.mult)
        v.engine_nop().then_inc(*fV.inc())

    NG = 64 * 4
    gumark = {}; hmark = {}; h128mark = {}; gsmark = {}; silumark = {}
    ubmark = {}; uwmark = {}; tmark = {}

    def emit_h128(gg):
        oc, qq = gg // 4, gg % 4

        @block.tensor
        def _(tt, gg=gg, qq=qq):
            tt.wait_ge(fV.h, hmark[gg])
            tt.matmul(PS[4 + qq % 2][:, 0:512], h128[:],
                      scr[:, (gg % 2) * 512:(gg % 2 + 1) * 512],
                      start=True, stop=True).then_inc(*fT.inc())
        h128mark[gg] = fT.val

    for ob in range(16):
        b = ob % 2
        stg_bf = ar[:, 16384 + b * 8192:16384 + (b + 1) * 8192].bitcast(BF16)

        @block.sync
        def _(sy, ob=ob, stg_bf=stg_bf):
            if ob == 0:
                sy.wait_ge(fW.h, fW.val)
            if ob >= 2:
                sy.wait_ge(fT.h, tmark[ob - 2])
            sy.dma_start(out=stg_bf[:, 0:8192].rearrange("p (h c) -> p h c", c=512),
                         in_=k.dr["tern_g"].ap()[:, ob].rearrange("h p c -> p h c")
                         ).then_inc(*fD.inc(16))
            sy.dma_start(out=stg_bf[:, 8192:16384].rearrange("p (h c) -> p h c", c=512),
                         in_=k.dr["tern_u"].ap()[:, ob].rearrange("h p c -> p h c")
                         ).then_inc(*fD.inc(16))
        dmark = fD.val

        for j in range(4):
            oc = ob * 4 + j
            for q in range(4):
                g = oc * 4 + q
                sl = g % 2

                @block.tensor
                def _(tt, j=j, q=q, stg_bf=stg_bf, dmark=dmark, first=(j == 0 and q == 0)):
                    if first:
                        tt.wait_ge(fD.h, dmark)
                    for hc in range(16):
                        tt.matmul(PS[q % 2][:, 0:512],
                                  stg_bf[:, hc * 512 + j * 128:hc * 512 + (j + 1) * 128],
                                  xq_bf[:, hc * 2048 + q * 512:hc * 2048 + (q + 1) * 512],
                                  start=(hc == 0), stop=(hc == 15))
                    for hc in range(16):
                        ins = tt.matmul(PS[2 + q % 2][:, 0:512],
                                        stg_bf[:, 8192 + hc * 512 + j * 128:
                                               8192 + hc * 512 + (j + 1) * 128],
                                        xq_bf[:, hc * 2048 + q * 512:hc * 2048 + (q + 1) * 512],
                                        start=(hc == 0), stop=(hc == 15))
                    ins.then_inc(*fT.inc())
                gumark[g] = fT.val

                @block.vector
                def _(v, g=g, q=q, sl=sl):
                    v.wait_ge(fT.h, gumark[g])
                    if g >= 2:
                        v.wait_ge(fW.h, uwmark[g - 2])
                    v.tensor_tensor(scr[:, sl * 512:(sl + 1) * 512],
                                    PS[q % 2][:, 0:512],
                                    scr3[:, q * 512:(q + 1) * 512], ALU.mult)
                    v.tensor_tensor(scr2[:, sl * 512:(sl + 1) * 512],
                                    PS[2 + q % 2][:, 0:512],
                                    scr4[:, q * 512:(q + 1) * 512], ALU.mult)
                    v.engine_nop().then_inc(*fV.inc())
                gsmark[g] = fV.val

                @block.scalar
                def _(s, g=g, sl=sl):
                    s.wait_ge(fV.h, gsmark[g])
                    s.activation(scr[:, sl * 512:(sl + 1) * 512],
                                 scr[:, sl * 512:(sl + 1) * 512],
                                 AF.Silu).then_inc(*fS.inc())
                silumark[g] = fS.val

                @block.vector
                def _(v, g=g, sl=sl):
                    v.wait_ge(fS.h, silumark[g])
                    v.tensor_tensor(scr[:, sl * 512:(sl + 1) * 512],
                                    scr[:, sl * 512:(sl + 1) * 512],
                                    scr2[:, sl * 512:(sl + 1) * 512], ALU.mult)
                    v.engine_nop().then_inc(*fV.inc())
                hmark[g] = fV.val

                if g >= 1:
                    emit_h128(g - 1)

                    @block.vector
                    def _(v, g1=g - 1):
                        v.wait_ge(fT.h, h128mark[g1])
                        if g1 >= 2:
                            v.wait_ge(fW.h, uwmark[g1 - 2])
                        v.tensor_copy(scr2[:, (g1 % 2) * 512:(g1 % 2 + 1) * 512],
                                      PS[4 + g1 % 2][:, 0:512])
                        v.engine_nop().then_inc(*fV.inc())
                    ubmark[g - 1] = fV.val

                    @block.sync
                    def _(sy, g1=g - 1):
                        sy.wait_ge(fV.h, ubmark[g1])
                        sy.dma_start(
                            out=k.dr["u_hbm"].ap()[ts(g1 // 4),
                                                   (g1 % 4) * 512:(g1 % 4 + 1) * 512],
                            in_=scr2[:, (g1 % 2) * 512:(g1 % 2 + 1) * 512]
                        ).then_inc(*fW.inc(16))
                    uwmark[g - 1] = fW.val
        tmark[ob] = gumark[(ob * 4 + 3) * 4 + 3]

    g = NG - 1
    emit_h128(g)

    @block.vector
    def _(v, g=g):
        v.wait_ge(fT.h, h128mark[g])
        v.tensor_copy(scr2[:, (g % 2) * 512:(g % 2 + 1) * 512],
                      PS[4 + g % 2][:, 0:512])
        v.engine_nop().then_inc(*fV.inc())
    ubmark[g] = fV.val

    @block.sync
    def _(sy, g=g):
        sy.wait_ge(fV.h, ubmark[g])
        sy.dma_start(out=k.dr["u_hbm"].ap()[ts(g // 4),
                                            (g % 4) * 512:(g % 4 + 1) * 512],
                     in_=scr2[:, (g % 2) * 512:(g % 2 + 1) * 512]
                     ).then_inc(*fW.inc(16))


def emit_ffn_phase2(k, block):
    fV, fS, fT, nF, fD, fG = k.f_sems
    fW = k.fW
    ar = k.arena
    e0row = k.sb["e0row"]; ident = k.sb["ident"]
    sigma2 = k.sb["sigma2"]; tcol = k.sb["colscr"]
    scr3 = k.sb["scr3"]
    PS = k.psN
    B = 16384
    lastz = None
    castmark = {}; xqwmark = {}

    for s in range(8):
        @block.sync
        def _(sy, s=s, lastz=lastz):
            if s == 0:
                sy.wait_ge(fW.h, fW.val)
            else:
                sy.wait_ge(fV.h, lastz)
            sy.dma_start(
                out=ar[:, 0:16384].rearrange("p (b c) -> p b c", c=256),
                in_=k.dr["u_hbm"].ap()[:, s * 256:(s + 1) * 256
                                       ].rearrange("(b p) c -> p b c", p=128)
            ).then_inc(*fD.inc(16))
        dmark = fD.val

        for st in range(6):
            stp = 1 << st

            @block.vector
            def _(v, st=st, stp=stp, dmark=dmark, scast=fS.val):
                if st == 0:
                    v.wait_ge(fD.h, dmark)
                    v.wait_ge(fS.h, scast)
                cur0 = 0 if st % 2 == 0 else B
                oth0 = B if st % 2 == 0 else 0
                for c in range(64):
                    p = c ^ stp
                    if c < p:
                        v.tensor_add(ar[:, oth0 + c * 256:oth0 + (c + 1) * 256],
                                     ar[:, cur0 + c * 256:cur0 + (c + 1) * 256],
                                     ar[:, cur0 + p * 256:cur0 + (p + 1) * 256])
                    else:
                        v.tensor_sub(ar[:, oth0 + c * 256:oth0 + (c + 1) * 256],
                                     ar[:, cur0 + p * 256:cur0 + (p + 1) * 256],
                                     ar[:, cur0 + c * 256:cur0 + (c + 1) * 256])
                v.engine_nop().then_inc(*fV.inc())

        @block.scalar
        def _(s_):
            s_.wait_ge(fV.h, fV.val)
            for c in range(64):
                ins = s_.activation(ar[:, B + c * 256:B + (c + 1) * 256],
                                    ar[:, c * 256:(c + 1) * 256], AF.Square)
            ins.then_inc(*fS.inc())

        @block.vector
        def _(v):
            v.wait_ge(fS.h, fS.val)
            n = 64
            while n > 1:
                n //= 2
                for i in range(n):
                    v.tensor_tensor(ar[:, B + i * 256:B + (i + 1) * 256],
                                    ar[:, B + (2 * i) * 256:B + (2 * i + 1) * 256],
                                    ar[:, B + (2 * i + 1) * 256:B + (2 * i + 2) * 256],
                                    ALU.max)
                v.engine_nop().then_inc(*nF.inc())
                v.wait_ge(nF.h, nF.val)
            v.engine_nop().then_inc(*fV.inc())

        @block.tensor
        def _(tt):
            tt.wait_ge(fV.h, fV.val)
            tt.transpose(PS[6][:, 0:128], ar[:, B:B + 128], ident[:])
            tt.transpose(PS[6][:, 128:256], ar[:, B + 128:B + 256],
                         ident[:]).then_inc(*fT.inc())

        @block.vector
        def _(v):
            v.wait_ge(fT.h, fT.val)
            v.reduce_max(tcol[:, 0:1], PS[6][:, 0:128], axis=AX.X).then_inc(*nF.inc())
            v.wait_ge(nF.h, nF.val)
            v.reduce_max(tcol[:, 1:2], PS[6][:, 128:256], axis=AX.X).then_inc(*nF.inc())
            v.wait_ge(nF.h, nF.val)
            v.tensor_scalar(tcol[:, 0:1], tcol[:, 0:1], EPS_FFN * EPS_FFN,
                            None, ALU.max)
            v.tensor_scalar(tcol[:, 1:2], tcol[:, 1:2], EPS_FFN * EPS_FFN,
                            None, ALU.max)
            v.engine_nop().then_inc(*fV.inc())

        @block.scalar
        def _(s_):
            s_.wait_ge(fV.h, fV.val)
            s_.activation(tcol[:, 0:2], tcol[:, 0:2],
                          AF.Sqrt).then_inc(*fS.inc())

        @block.vector
        def _(v, s=s):
            v.wait_ge(fS.h, fS.val)
            v.tensor_scalar(sigma2[:, 2 * s:2 * s + 1], tcol[:, 0:1],
                            1.0 / (127.0 * SQ8192), None, ALU.mult)
            v.tensor_scalar(sigma2[:, 2 * s + 1:2 * s + 2], tcol[:, 1:2],
                            1.0 / (127.0 * SQ8192), None, ALU.mult)
            v.reciprocal(tcol[:, 4:5], tcol[:, 0:1])
            v.reciprocal(tcol[:, 5:6], tcol[:, 1:2])
            v.engine_nop().then_inc(*nF.inc())
            v.wait_ge(nF.h, nF.val)
            v.tensor_scalar(tcol[:, 4:5], tcol[:, 4:5], 127.0, None, ALU.mult)
            v.tensor_scalar(tcol[:, 5:6], tcol[:, 5:6], 127.0, None, ALU.mult)
            v.engine_nop().then_inc(*fV.inc())

        @block.tensor
        def _(tt):
            tt.wait_ge(fV.h, fV.val)
            tt.transpose(PS[7][0:1, 0:128], tcol[:, 4:5], ident[:])
            tt.transpose(PS[7][0:1, 128:256], tcol[:, 5:6],
                         ident[:]).then_inc(*fT.inc())

        @block.vector
        def _(v):
            v.wait_ge(fT.h, fT.val)
            v.tensor_copy(scr3[0:1, 0:256], PS[7][0:1, 0:256])
            v.engine_nop().then_inc(*fV.inc())

        @block.tensor
        def _(tt):
            tt.wait_ge(fV.h, fV.val)
            tt.matmul(PS[6][:, 0:256], e0row[:], scr3[0:128, 0:256],
                      start=True, stop=True)
            tt.matmul(PS[6][:, 256:512], e0row[:], scr3[0:128, 0:256],
                      start=True, stop=True).then_inc(*fT.inc())

        @block.vector
        def _(v):
            v.wait_ge(fT.h, fT.val)
            v.tensor_copy(ar[:, B + 256:B + 768], PS[6][:, 0:512])
            v.engine_nop().then_inc(*fV.inc())

        for pp in range(32):
            @block.vector
            def _(v, pp=pp, s=s):
                if pp >= 2:
                    v.wait_ge(fS.h, castmark[(s, pp - 2)])
                z = ar[:, B + 768:B + 1280]
                w = ar[:, B + 1280 + (pp % 2) * 512:B + 1792 + (pp % 2) * 512]
                bb = ar[:, B + 2304:B + 2816]
                v.tensor_tensor(z, ar[:, pp * 512:(pp + 1) * 512],
                                ar[:, B + 256:B + 768], ALU.mult)
                peel_round(v, z, w, bb, 512)
                v.tensor_sub(w, z, w)
                v.engine_nop().then_inc(*fV.inc())
            vmarkp = fV.val

            @block.scalar
            def _(s_, pp=pp, s=s):
                s_.wait_ge(fV.h, fV.val)
                if pp >= 2:
                    s_.wait_ge(fW.h, xqwmark[(s, pp - 2)])
                cb = ar[:, B + 2816 + (pp % 2) * 256:
                        B + 3072 + (pp % 2) * 256].bitcast(BF16)
                w = ar[:, B + 1280 + (pp % 2) * 512:B + 1792 + (pp % 2) * 512]
                s_.activation(cb, w, AF.Identity, bias=-128.0).then_inc(*fS.inc())
            castmark[(s, pp)] = fS.val

            @block.sync
            def _(sy, pp=pp, s=s):
                sy.wait_ge(fS.h, fS.val)
                cb = ar[:, B + 2816 + (pp % 2) * 256:
                        B + 3072 + (pp % 2) * 256].bitcast(BF16)
                sy.dma_start(
                    out=k.dr["xq2_hbm"].ap()[s, 2 * pp * 128:(2 * pp + 2) * 128,
                                             :].rearrange("(b p) c -> p b c", p=128),
                    in_=cb.rearrange("p (b c) -> p b c", c=256)
                ).then_inc(*fW.inc(16))
            xqwmark[(s, pp)] = fW.val
        lastz = vmarkp


def emit_ffn_down(k, block):
    fV, fS, fT, nF, fD, fG = k.f_sems
    fW = k.fW
    ar = k.arena
    e0row = k.sb["e0row"]; ident = k.sb["ident"]
    sigma2 = k.sb["sigma2"]; fscal = k.sb["fscal"]
    scr3 = k.sb["scr3"]
    PS = k.psN
    STG, SDQ, FM, YB, X2B = 16384, 24576, 25088, 26112, 28160
    xq2q_bf = ar[:, 0:16384].bitcast(BF16)
    qtmark = None
    dmW = {}
    scmark = {}; trmark = {}; ycmark = {}; yam = {}; ydm = {}; xdm = {}

    for q in range(4):
        rmk = []

        @block.tensor
        def _(tt, q=q):
            tt.wait_ge(fV.h, fV.val)
            for tl in range(4):
                ins = tt.transpose(PS[7][0:1, tl * 128:(tl + 1) * 128],
                                   sigma2[:, 4 * q + tl:4 * q + tl + 1], ident[:])
            ins.then_inc(*fT.inc())

        @block.vector
        def _(v):
            v.wait_ge(fT.h, fT.val)
            v.tensor_copy(scr3[0:1, 0:512], PS[7][0:1, 0:512])
            v.engine_nop().then_inc(*fV.inc())

        @block.tensor
        def _(tt):
            tt.wait_ge(fV.h, fV.val)
            tt.matmul(PS[6][:, 0:512], e0row[:], scr3[0:128, 0:512],
                      start=True, stop=True).then_inc(*fT.inc())

        @block.vector
        def _(v):
            v.wait_ge(fT.h, fT.val)
            v.tensor_scalar(ar[:, SDQ:SDQ + 512], PS[6][:, 0:512],
                            fscal[:, 5:6], None, ALU.mult)
            v.engine_nop().then_inc(*fV.inc())

        @block.sync
        def _(sy, q=q, qtmark=qtmark):
            if q == 0:
                sy.wait_ge(fW.h, fW.val)
            else:
                sy.wait_ge(fT.h, qtmark)
            for hh in range(2):
                sy.dma_start(
                    out=xq2q_bf.rearrange("p (b c) -> p b c",
                                          c=512)[:, :, hh * 256:(hh + 1) * 256],
                    in_=k.dr["xq2_hbm"].ap()[2 * q + hh].rearrange(
                        "(b p) c -> p b c", p=128)
                ).then_inc(*fD.inc(16))
        qdmark = fD.val

        for dq in range(4):
            for w8 in range(4):
                gw = dq * 4 + w8

                @block.sync
                def _(sy, q=q, dq=dq, w8=w8, gw=gw):
                    if gw >= 2:
                        sy.wait_ge(fT.h, dmW[(q, gw - 2, "c")])
                    stg = ar[:, STG + (gw % 2) * 4096:
                             STG + (gw % 2) * 4096 + 4096].bitcast(BF16)
                    sy.dma_start(
                        out=stg.rearrange("p (b c) -> p b c", c=512),
                        in_=k.dr["tern_d"].ap()[w8 * 16:(w8 + 1) * 16, dq
                                                ].rearrange("b p c -> p b c")
                    ).then_inc(*fD.inc(16))
                dmW[(q, gw)] = fD.val

                @block.tensor
                def _(tt, q=q, dq=dq, w8=w8, gw=gw, qdmark=qdmark):
                    tt.wait_ge(fD.h, dmW[(q, gw)])
                    if w8 == 0 and dq == 0:
                        tt.wait_ge(fD.h, qdmark)
                        tt.wait_ge(fV.h, fV.val)
                    for oi in range(16):
                        oc = w8 * 16 + oi
                        stg_bf = ar[:, STG + (gw % 2) * 4096 + oi * 256:
                                    STG + (gw % 2) * 4096 + (oi + 1) * 256
                                    ].bitcast(BF16)
                        for jdc in range(4):
                            ins = tt.matmul(
                                PS[jdc][:, 0:512],
                                stg_bf[:, jdc * 128:(jdc + 1) * 128],
                                xq2q_bf[:, oc * 512:(oc + 1) * 512],
                                start=(oc == 0), stop=(oc == 63))
                    ins.then_inc(*fT.inc())
                dmW[(q, gw, "c")] = fT.val
            mmark = fT.val

            for jdc in range(4):
                @block.vector
                def _(v, q=q, dq=dq, jdc=jdc, mmark=mmark):
                    v.wait_ge(fT.h, mmark)
                    if jdc >= 2:
                        v.wait_ge(fT.h, trmark[(q, dq, jdc - 2)])
                    if jdc == 0 and (q, dq) != (0, 0):
                        pq, pdq = (q, dq - 1) if dq > 0 else (q - 1, 3)
                        v.wait_ge(fW.h, ydm[(pq, pdq)])
                    v.tensor_tensor(ar[:, FM + (jdc % 2) * 512:FM + (jdc % 2 + 1) * 512],
                                    PS[jdc][:, 0:512], ar[:, SDQ:SDQ + 512], ALU.mult)
                    v.engine_nop().then_inc(*fV.inc())
                scmark[(q, dq, jdc)] = fV.val

                @block.tensor
                def _(tt, q=q, dq=dq, jdc=jdc):
                    tt.wait_ge(fV.h, scmark[(q, dq, jdc)])
                    if jdc >= 2:
                        tt.wait_ge(fV.h, ycmark[(q, dq, jdc - 2)])
                    for tl in range(4):
                        ins = tt.transpose(
                            PS[4 + jdc % 2][:, tl * 128:(tl + 1) * 128],
                            ar[:, FM + (jdc % 2) * 512 + tl * 128:
                               FM + (jdc % 2) * 512 + (tl + 1) * 128], ident[:])
                    ins.then_inc(*fT.inc())
                trmark[(q, dq, jdc)] = fT.val

                @block.vector
                def _(v, q=q, dq=dq, jdc=jdc):
                    v.wait_ge(fT.h, trmark[(q, dq, jdc)])
                    for tl in range(4):
                        v.tensor_copy(ar[:, YB + tl * 512 + jdc * 128:
                                      YB + tl * 512 + (jdc + 1) * 128],
                                      PS[4 + jdc % 2][:, tl * 128:(tl + 1) * 128])
                    v.engine_nop().then_inc(*fV.inc())
                ycmark[(q, dq, jdc)] = fV.val

            for tl in range(4):
                @block.sync
                def _(sy, q=q, dq=dq, tl=tl):
                    if tl >= 2:
                        sy.wait_ge(fV.h, yam[(q, dq, tl - 2)])
                    elif (q, dq) != (0, 0):
                        pq, pdq = (q, dq - 1) if dq > 0 else (q - 1, 3)
                        sy.wait_ge(fV.h, yam[(pq, pdq, tl + 2)])
                    sy.dma_start(out=ar[:, X2B + (tl % 2) * 512:X2B + (tl % 2 + 1) * 512],
                                 in_=k.dr["x2_hbm"].ap()[ts(q * 4 + tl),
                                                         dq * 512:(dq + 1) * 512]
                                 ).then_inc(*fD.inc(16))
                xdm[(q, dq, tl)] = fD.val

                @block.vector
                def _(v, q=q, dq=dq, tl=tl):
                    v.wait_ge(fD.h, xdm[(q, dq, tl)])
                    v.tensor_tensor(ar[:, YB + tl * 512:YB + (tl + 1) * 512],
                                    ar[:, YB + tl * 512:YB + (tl + 1) * 512],
                                    ar[:, X2B + (tl % 2) * 512:X2B + (tl % 2 + 1) * 512],
                                    ALU.add)
                    v.engine_nop().then_inc(*fV.inc())
                yam[(q, dq, tl)] = fV.val

                @block.sync
                def _(sy, q=q, dq=dq, tl=tl):
                    sy.wait_ge(fV.h, yam[(q, dq, tl)])
                    sy.dma_start(out=k.dr["y"].ap()[ts(q * 4 + tl),
                                                    dq * 512:(dq + 1) * 512],
                                 in_=ar[:, YB + tl * 512:YB + (tl + 1) * 512]
                                 ).then_inc(*fW.inc(16))
            ydm[(q, dq)] = fW.val
        qtmark = fT.val
    k.ffn_y_done = fW.val


def emit_ffn(k, block):
    fV = k.sem("f_v"); fS = k.sem("f_s"); fT = k.sem("f_t")
    nF = k.sem("f_nv"); fD = k.sem("f_d"); fG = k.sem("f_g")
    k.f_sems = (fV, fS, fT, nF, fD, fG)
    k.fW = k.sem("f_w")
    emit_ffn_scale(k, block)
    if getattr(k, "ffn_mode", "full") == "scale":
        k.fin2_V = fV; k.fin2_D = fD
        return
    emit_ffn_xq(k, block)
    if getattr(k, "ffn_mode", "full") == "xq":
        k.fin2_V = fV; k.fin2_D = fD
        return
    emit_ffn_tern(k, block)
    emit_ffn_gateup(k, block)
    if getattr(k, "ffn_mode", "full") == "gu":
        k.fin2_V = fV; k.fin2_D = fD; k.fin2_W = k.fW
        return
    emit_ffn_phase2(k, block)
    if getattr(k, "ffn_mode", "full") == "p2":
        k.fin2_V = fV; k.fin2_D = fD; k.fin2_W = k.fW
        return
    emit_ffn_down(k, block)
    k.fin2_V = fV; k.fin2_D = fD; k.fin2_W = k.fW


def emit_ffn_scale(k, block):
    fV, fS, fT, nF, fD, fG = k.f_sems
    ar = k.arena
    e0row = k.sb["e0row"]; h128 = k.sb["h128"]
    scr2 = k.sb["scr2"]
    PS = k.psN
    fscal = k.sbuf("fscal", [128, 16], F32)
    accb = k.sbuf("accb", [128, 256], F32)

    @block.gpsimd
    def _(g):
        g.memset(accb[:], 0.0).then_inc(*fG.inc())
        g.memset(fscal[:], 0.0).then_inc(*fG.inc())
        g.memset(k.sb["sigma1"][:], 0.0).then_inc(*fG.inc())
        g.memset(k.sb["sigma2"][:], 0.0).then_inc(*fG.inc())

    @block.vector
    def _(v):
        v.wait_ge(fG.h, fG.val)
        v.engine_nop().then_inc(*fV.inc())

    loads = []
    for i in range(16):
        for j in range(4):
            loads.append(("L0_ffn_w_gate",
                          k.dr["L0_ffn_w_gate"].ap()[ts(i), j * 2048:(j + 1) * 2048]))
    for i in range(16):
        for j in range(4):
            loads.append(("L0_ffn_w_up",
                          k.dr["L0_ffn_w_up"].ap()[ts(i), j * 2048:(j + 1) * 2048]))
    for i in range(64):
        loads.append(("L0_ffn_w_down", k.dr["L0_ffn_w_down"].ap()[ts(i), :]))
    dump = ar[:, 16384:18432]
    NSB = 6
    smark = []
    attn_done = (k.fin_aV.h, k.fin_aV.val) if hasattr(k, "fin_aV") else None
    for nb, (name, srcap) in enumerate(loads):
        buf = ar[:, (nb % NSB) * 2048:(nb % NSB) * 2048 + 2048]

        def _ldbody(eng, srcap=srcap, buf=buf, nb=nb):
            if nb <= 1 and attn_done is not None:
                eng.wait_ge(*attn_done)
            if nb >= NSB:
                eng.wait_ge(fS.h, smark[nb - NSB])
            eng.dma_start(out=buf, in_=srcap).then_inc(*fD.inc(16))

        if nb % 2 == 0:
            @block.sync
            def _(sy, body=_ldbody):
                body(sy)
        else:
            @block.scalar
            def _(sc, body=_ldbody):
                body(sc)
        dmk = fD.val

        @block.scalar
        def _(s, buf=buf, nb=nb, dmk=dmk):
            s.wait_ge(fD.h, dmk)
            s.activation(dump, buf, AF.Abs,
                         accum_out=accb[:, nb:nb + 1]).then_inc(*fS.inc())
        smark.append(fS.val)

    @block.vector
    def _(v):
        v.wait_ge(fS.h, fS.val)
        v.reduce_sum(accb[:, 200:201], accb[:, 0:64], axis=AX.X).then_inc(*nF.inc())
        v.wait_ge(nF.h, nF.val)
        v.reduce_sum(accb[:, 201:202], accb[:, 64:128], axis=AX.X).then_inc(*nF.inc())
        v.wait_ge(nF.h, nF.val)
        v.reduce_sum(accb[:, 202:203], accb[:, 128:192], axis=AX.X)
        v.engine_nop().then_inc(*fV.inc())

    @block.tensor
    def _(tt):
        tt.wait_ge(fV.h, fV.val)
        tt.matmul(PS[6][0:1, 0:3], h128[:, 0:1], accb[:, 200:203],
                  start=True, stop=True).then_inc(*fT.inc())

    @block.vector
    def _(v):
        v.wait_ge(fT.h, fT.val)
        v.tensor_copy(accb[0:1, 204:207], PS[6][0:1, 0:3])
        v.engine_nop().then_inc(*fV.inc())

    @block.tensor
    def _(tt):
        tt.wait_ge(fV.h, fV.val)
        tt.matmul(PS[6][:, 4:7], e0row[:], accb[:, 204:207],
                  start=True, stop=True).then_inc(*fT.inc())

    @block.vector
    def _(v):
        v.wait_ge(fT.h, fT.val)
        v.tensor_scalar(fscal[:, 3:4], PS[6][:, 4:5], 1.0 / (IF * D), EPS_FFN,
                        ALU.mult, ALU.max).then_inc(*nF.inc())
        v.wait_ge(nF.h, nF.val)
        v.tensor_scalar(fscal[:, 4:5], PS[6][:, 5:6], 1.0 / (IF * D), EPS_FFN,
                        ALU.mult, ALU.max).then_inc(*nF.inc())
        v.wait_ge(nF.h, nF.val)
        v.tensor_scalar(fscal[:, 5:6], PS[6][:, 6:7], 1.0 / (IF * D), EPS_FFN,
                        ALU.mult, ALU.max).then_inc(*nF.inc())
        v.wait_ge(nF.h, nF.val)
        v.tensor_scalar(fscal[:, 0:1], fscal[:, 3:4], 0.5, None, ALU.mult)
        v.tensor_scalar(fscal[:, 1:2], fscal[:, 4:5], 0.5, None, ALU.mult)
        v.tensor_scalar(fscal[:, 2:3], fscal[:, 5:6], 0.5, None, ALU.mult)
        v.engine_nop().then_inc(*nF.inc())
        v.wait_ge(nF.h, nF.val)
        v.tensor_scalar(fscal[:, 6:7], fscal[:, 0:1], -1.0, None, ALU.mult)
        v.tensor_scalar(fscal[:, 7:8], fscal[:, 1:2], -1.0, None, ALU.mult)
        v.tensor_scalar(fscal[:, 8:9], fscal[:, 2:3], -1.0, None, ALU.mult)
        v.engine_nop().then_inc(*fV.inc())


def emit_ffn_xq(k, block):
    fV, fS, fT, nF, fD, fG = k.f_sems
    ar = k.arena
    ident = k.sb["ident"]; h128 = k.sb["h128"]; e0row = k.sb["e0row"]
    rstd2 = k.sb["rstd2"]; nmr2 = k.sb["nmr2"]
    sigma1 = k.sb["sigma1"]
    tcol = k.sb["colscr"]
    scr = k.sb["scr"]; scr2 = k.sb["scr2"]; scr3 = k.sb["scr3"]; scr4 = k.sb["scr4"]
    PS = k.psN
    xq_bf = ar[:, 0:16384].bitcast(BF16)
    FT = getattr(k, "ftrim", 99)

    for qq in range(4):
        fmA = ar[:, 16384:24576]
        fmB = ar[:, 24576:32768]

        for tl in range(4):
            t = qq * 4 + tl

            @block.sync
            def _(sy, t=t):
                sy.wait_ge(fV.h, fV.val)
                sy.wait_ge(fS.h, fS.val)
                sy.dma_start(out=scr3[:, 0:D],
                             in_=k.dr["x2_hbm"].ap()[ts(t), :]).then_inc(*fD.inc(16))

            @block.scalar
            def _(s, t=t):
                s.wait_ge(fD.h, fD.val)
                s.activation(scr3[:, 0:D], scr3[:, 0:D], AF.Identity,
                             bias=nmr2[:, t:t + 1],
                             scale=rstd2[:, t:t + 1]).then_inc(*fS.inc())

            @block.vector
            def _(v):
                v.wait_ge(fS.h, fS.val)
                v.engine_nop().then_inc(*fV.inc())

            emit_transpose16(k, block, scr3[:, 0:D],
                             lambda c, tl=tl, fmA=fmA: fmA[:, c * 512 + tl * 128:
                                                          c * 512 + (tl + 1) * 128],
                             fT, fV)

        if FT < 2:
            continue
        for c in range(NCH):
            @block.tensor
            def _(tt, c=c):
                tt.wait_ge(fV.h, fV.val)
                tt.matmul(PS[2 + (c % 2)][:, 0:512], h128[:],
                          fmA[:, c * 512:(c + 1) * 512],
                          start=True, stop=True).then_inc(*fT.inc())

            @block.vector
            def _(v, c=c):
                v.wait_ge(fT.h, fT.val)
                v.tensor_copy(fmB[:, c * 512:(c + 1) * 512],
                              PS[2 + (c % 2)][:, 0:512])
                v.engine_nop().then_inc(*fV.inc())

        cur, oth = fmB, fmA
        if FT < 3:
            continue
        for st_ in range(4):
            stp = 1 << st_

            @block.vector
            def _(v, cur=cur, oth=oth, stp=stp):
                v.wait_ge(fV.h, fV.val)
                for c in range(NCH):
                    p = c ^ stp
                    if c < p:
                        v.tensor_add(oth[:, c * 512:(c + 1) * 512],
                                     cur[:, c * 512:(c + 1) * 512],
                                     cur[:, p * 512:(p + 1) * 512])
                    else:
                        v.tensor_sub(oth[:, c * 512:(c + 1) * 512],
                                     cur[:, p * 512:(p + 1) * 512],
                                     cur[:, c * 512:(c + 1) * 512])
                v.engine_nop().then_inc(*fV.inc())
            cur, oth = oth, cur

        if FT < 4:
            continue
        @block.scalar
        def _(s, cur=cur, oth=oth):
            s.wait_ge(fV.h, fV.val)
            for c in range(NCH):
                ins = s.activation(oth[:, c * 512:(c + 1) * 512],
                                   cur[:, c * 512:(c + 1) * 512], AF.Square)
            ins.then_inc(*fS.inc())

        @block.vector
        def _(v, oth=oth):
            v.wait_ge(fS.h, fS.val)
            n = NCH
            while n > 1:
                n //= 2
                for i in range(n):
                    v.tensor_tensor(oth[:, i * 512:(i + 1) * 512],
                                    oth[:, (2 * i) * 512:(2 * i + 1) * 512],
                                    oth[:, (2 * i + 1) * 512:(2 * i + 2) * 512],
                                    ALU.max)
                v.engine_nop().then_inc(*nF.inc())
                v.wait_ge(nF.h, nF.val)
            v.engine_nop().then_inc(*fV.inc())

        @block.tensor
        def _(tt, oth=oth):
            tt.wait_ge(fV.h, fV.val)
            for j in range(4):
                ins = tt.transpose(PS[6][:, ts(j)],
                                   oth[:, j * 128:(j + 1) * 128], ident[:])
                if j == 3:
                    ins.then_inc(*fT.inc())

        @block.vector
        def _(v, qq=qq):
            v.wait_ge(fT.h, fT.val)
            for j in range(4):
                v.reduce_max(tcol[:, j:j + 1], PS[6][:, ts(j)],
                             axis=AX.X).then_inc(*nF.inc())
            v.wait_ge(nF.h, nF.val)
            for j in range(4):
                v.tensor_scalar(tcol[:, j:j + 1], tcol[:, j:j + 1],
                                EPS_FFN * EPS_FFN, None, ALU.max)
            v.engine_nop().then_inc(*fV.inc())

        @block.scalar
        def _(s):
            s.wait_ge(fV.h, fV.val)
            s.activation(tcol[:, 0:4], tcol[:, 0:4],
                         AF.Sqrt).then_inc(*fS.inc())

        @block.vector
        def _(v, qq=qq):
            v.wait_ge(fS.h, fS.val)
            for j in range(4):
                v.tensor_scalar(sigma1[:, qq * 4 + j:qq * 4 + j + 1],
                                tcol[:, j:j + 1], 1.0 / (127.0 * SQ2048), None,
                                ALU.mult)
                v.reciprocal(tcol[:, 4 + j:5 + j], tcol[:, j:j + 1])
            v.engine_nop().then_inc(*nF.inc())
            v.wait_ge(nF.h, nF.val)
            for j in range(4):
                v.tensor_scalar(tcol[:, 4 + j:5 + j], tcol[:, 4 + j:5 + j],
                                127.0, None, ALU.mult)
            v.engine_nop().then_inc(*fV.inc())

        if FT < 5:
            continue
        @block.tensor
        def _(tt):
            tt.wait_ge(fV.h, fV.val)
            for j in range(4):
                ins = tt.transpose(PS[7][0:1, j * 128:(j + 1) * 128],
                                   tcol[:, 4 + j:5 + j], ident[:])
                if j == 3:
                    ins.then_inc(*fT.inc())

        @block.vector
        def _(v):
            v.wait_ge(fT.h, fT.val)
            v.tensor_copy(scr[0:1, 0:512], PS[7][0:1, 0:512])
            v.engine_nop().then_inc(*fV.inc())

        @block.tensor
        def _(tt):
            tt.wait_ge(fV.h, fV.val)
            tt.matmul(PS[6][:, 0:512], e0row[:], scr[0:1, 0:512].to_broadcast(
                [128, 512]) if False else scr[0:128, 0:512],
                start=True, stop=True).then_inc(*fT.inc())

        @block.vector
        def _(v):
            v.wait_ge(fT.h, fT.val)
            v.tensor_copy(scr2[:, 0:512], PS[6][:, 0:512])
            v.engine_nop().then_inc(*fV.inc())

        if FT < 6:
            continue
        for c in range(NCH):
            @block.vector
            def _(v, c=c, cur=cur, oth=oth):
                v.wait_ge(fS.h, fS.val)
                z = oth[:, 512:1024]
                w = scr3[:, 0:512]
                b = scr3[:, 512:1024]
                v.tensor_mul(z, cur[:, c * 512:(c + 1) * 512],
                             scr2[:, 0:512]).then_inc(*nF.inc())
                v.wait_ge(nF.h, nF.val)
                peel_round(v, z, w, b, 512)
                v.tensor_sub(w, z, w)
                v.engine_nop().then_inc(*fV.inc())

            @block.scalar
            def _(s, c=c, qq=qq):
                s.wait_ge(fV.h, fV.val)
                s.activation(xq_bf[:, c * 2048 + qq * 512:c * 2048 + (qq + 1) * 512],
                             scr3[:, 0:512], AF.Identity,
                             bias=-128.0).then_inc(*fS.inc())


def emit_finish_dev(k, block):
    fin = k.sem("fin")
    last = k.last_sem
    full = hasattr(k, "ffn_y_done")

    if not full:
        @block.sync
        def _(sy):
            sy.wait_ge(last.h, last.val)
            sy.dma_start(out=k.dr["y"].ap()[0:128, :],
                         in_=k.sb["wmask"][:]).then_inc(*fin.inc(16))

    @block.gpsimd
    def _(g):
        if full:
            g.wait_ge(k.fW.h, k.ffn_y_done)
        else:
            g.wait_ge(fin.h, fin.val)
        if hasattr(k, "fin_dA"):
            g.wait_ge(k.fin_dA.h, k.fin_dA.val)
        if hasattr(k, "fin2_D"):
            g.wait_ge(k.fin2_D.h, k.fin2_D.val)
        if hasattr(k, "fin2_W"):
            g.wait_ge(k.fin2_W.h, k.fin2_W.val)
        if k.dtap.val:
            g.wait_ge(k.dtap.h, k.dtap.val)


def build_program(upto="compress", taps=(), attn_mode="full", ffn_mode="full"):
    nc = bass.Bass("TRN2", target_bir_lowering=False, detect_race_conditions=False)
    ctx = ExitStack()
    with ctx:
        k = K(nc, ctx)
        k.upto = upto
        k.tapset = set(taps)
        k.attn_mode = attn_mode
        k.ffn_mode = ffn_mode
        k.dram("x", [T, D], F32, kind="ExternalInput")
        k.dram("h128_in", [128, 128], F32, kind="ExternalInput")
        for n in WEIGHT_NAMES:
            k.dram(n, INPUT_SHAPES[n], F32, kind="ExternalInput")
        k.dram("y", [T, D], F32, kind="ExternalOutput")
        fm = getattr(k, "ffn_mode", "full")
        dbg = "ExternalOutput" if (upto == "attn" or fm != "full") else "Internal"
        k.dram("x2_hbm", [T, D], F32, kind=dbg)
        k.dram("grp_hbm", [D, T], F32, kind=dbg)
        k.dram("u_hbm", [IF, T], F32,
               kind="ExternalOutput" if fm == "gu" else "Internal")
        k.dram("xq2_hbm", [8, IF, 256], BF16,
               kind="ExternalOutput" if fm == "p2" else "Internal")
        k.dram("tern_g", [NCH, 16, 128, 512], BF16)
        k.dram("tern_u", [NCH, 16, 128, 512], BF16)
        k.dram("tern_d", [64, 4, 128, 512], BF16)

        k.arena = k.sbuf("arena", [128, ARENA], F32)
        k.sbuf("ident", [128, 128], F32)
        k.sbuf("h128", [128, 128], F32)
        k.sbuf("cos_tm", [128, NT * 32], F32)
        k.sbuf("sin_tm", [128, NT * 32], F32)
        k.sbuf("wmask", [128, 4 * 512], F32)
        k.sbuf("e0row", [128, 128], F32)
        k.sbuf("bcsrc", [128, 256], F32)
        k.sbuf("wqs_b", [128, 128], F32)
        k.sbuf("wk_b", [128, 128], F32)
        k.sbuf("nmean", [128, NT], F32)
        k.sbuf("rstd1", [128, NT], F32)
        k.sbuf("nmr1", [128, NT], F32)
        k.sbuf("rstd2", [128, NT], F32)
        k.sbuf("nmr2", [128, NT], F32)
        k.sbuf("k_fm", [128, (NT + 2) * 128], F32)
        k.sbuf("v_tm", [128, (NT + 2) * 128], F32)
        k.sbuf("kcompT", [128, 128], F32)
        k.sbuf("kcompT2", [128, 256], F32)
        k.sbuf("ckv_raw", [128, 128], F32)
        k.sbuf("ckv_V", [128, 128], F32)
        k.sbuf("ckv_K", [128, 128], F32)
        k.sbuf("qi_fm", [128, 2 * T], F32)
        k.sbuf("wih_tm", [128, NT * 4], F32)
        k.sbuf("sigma1", [128, NT], F32)
        k.sbuf("sigma2", [128, NT], F32)
        k.sbuf("scr", [128, 1024], F32)
        k.sbuf("scr2", [128, 1024], F32)
        k.sbuf("scr3", [128, 2048], F32)
        k.sbuf("scr4", [128, 2048], F32)
        k.sbuf("colscr", [128, 32], F32)
        cpool = k.sbuf("cpool", [128, 8], F32)
        k.constvals = [EPS_LN, EPS_RMS, -128.0, 1e30, 0.5, -1.0]
        for i, cv in enumerate(k.constvals):
            nc.const_aps.register(F32, float(cv), cpool[:, i:i + 1])
        k.psN = [ctx.enter_context(nc.psum_tensor(f"PS{i}", [128, 512], F32))
                 for i in range(8)]

        with nc.Block() as block:
            k.gsem = k.sem("g_prog")
            k.dtap = k.sem("d_tap")
            emit_constants(k, block)
            k.last_sem = k.c_sV
            if upto != "const":
                emit_prepass(k, block)
                k.last_sem = k.p_pV
            if upto not in ("const", "prepass"):
                emit_compress(k, block)
                k.last_sem = k.q_pV
            if upto in ("attn", "all"):
                emit_attention(k, block)
                k.last_sem = k.fin_aV
            if upto == "all":
                emit_ffn(k, block)
                k.last_sem = k.fin2_V
            ARENA_TAPS = {"kvcc": (A_KVCC, T, 128), "kvcz": (A_KVCZ, T, 128),
                          "idxc": (A_IDXC, T, 64), "idxz": (A_IDXZ, T, 64),
                          "qic": (A_QIC, T, 64), "xfm": (A_XFM, D, 128),
                          "xln": (A_XLN, D, 128)}
            for name in list(k.tapset):
                if name in k.sb:
                    k.tap(block, name, k.sb[name][:], wait=k.last_sem)
                elif name in ARENA_TAPS:
                    off, w, rows = ARENA_TAPS[name]
                    k.tap(block, name, k.arena[0:rows, off:off + w],
                          wait=k.last_sem)
            emit_finish_dev(k, block)
    return nc, k


def hadamard128():
    i = np.arange(128, dtype=np.uint32)
    a = i[:, None] & i[None, :]
    pc = np.zeros_like(a)
    for b in range(7):
        pc ^= (a >> b) & 1
    return np.where(pc == 0, 1.0, -1.0).astype(np.float32)


def kernel(**inputs):
    xfull = np.ascontiguousarray(np.asarray(inputs["L0_attn_h"], np.float32))
    weights = {n: np.ascontiguousarray(np.asarray(inputs[n], np.float32))
               for n in WEIGHT_NAMES}
    for n in ("L0_ffn_w_gate", "L0_ffn_w_up", "L0_ffn_w_down"):
        weights[n] = np.ascontiguousarray(weights[n].T)
    weights["h128_in"] = hadamard128()
    nc, _ = build_program(upto="all")
    in_maps = [dict(weights, x=np.ascontiguousarray(xfull[i])) for i in range(8)]
    res = run_bass_kernel_spmd(nc, in_maps, core_ids=list(range(8)))
    return np.stack([r["y"] for r in res.results], axis=0)
